# version 03
# baseline (speedup 1.0000x reference)
import contextlib
import numpy as np
import ml_dtypes
import concourse.bass as bass
import concourse.mybir as mybir
from concourse.bass_utils import run_bass_kernel_spmd
from concourse.bass_core_rust import IndirectOffsetOnAxis

F32 = mybir.dt.float32
BF16 = mybir.dt.bfloat16
I32 = mybir.dt.int32
AF = mybir.ActivationFunctionType
ALU = mybir.AluOpType
AX = mybir.AxisListType
bfloat16 = ml_dtypes.bfloat16

T = 4096
D = 1024
NCH = D // 128
TC = 512
NTC = T // TC
NCORE = 8
MOE_C = 1024
NEG0 = -100000.0
NEGINF = -1e30

ENGINES = ("tensor", "vector", "scalar", "sync", "gpsimd")


GLOBAL_SEMS = {"mm", "mm2", "mm3", "ve", "ve2", "ve3", "ve9", "vb", "ac", "acpa",
               "vpa", "st", "stg", "cc", "cld", "fcinit", "dbg"}


class Prog:
    def __init__(self):
        self.nc = bass.Bass("TRN2", target_bir_lowering=False)
        self.stack = contextlib.ExitStack()
        self.streams = {e: [] for e in ENGINES}
        self._sems = {}
        self._sem_counts = {}
        self._free_handles = []
        self._region_sems = []
        self._region_id = 0
        self.drams = {}
        self._regions = []
        self._name_ctr = 0

    def dram(self, name, shape, dtype, kind="Internal"):
        ap = self.nc.dram_tensor(name, list(shape), dtype, kind=kind).ap()
        self.drams[name] = ap
        return ap

    def _uname(self, name):
        self._name_ctr += 1
        return f"{name}__r{self._name_ctr}"

    def sbuf(self, name, shape, dtype):
        st = self._regions[-1] if self._regions else self.stack
        nm = self._uname(name) if self._regions else name
        return st.enter_context(self.nc.sbuf_tensor(nm, list(shape), dtype))

    def psum(self, name, shape, dtype=F32):
        st = self._regions[-1] if self._regions else self.stack
        nm = self._uname(name) if self._regions else name
        return st.enter_context(self.nc.psum_tensor(nm, list(shape), dtype))

    @contextlib.contextmanager
    def region(self):
        st = contextlib.ExitStack()
        self._regions.append(st)
        self._region_id += 1
        self._region_sems = []
        saved_fc = getattr(self, "_fconsts", {})
        self._fconsts = {}
        try:
            yield
        finally:
            self._regions.pop()
            st.close()
            self._fconsts = saved_fc
            for key in self._region_sems:
                self._free_handles.append(self._sems.pop(key))
            self._region_sems = []

    def barrier_all(self):
        self.do("sync", lambda e: e.nop(), inc="stg")
        v = self.val("stg")
        for eng in ENGINES:
            self.wait(eng, "stg", v)

    def _sem(self, name):
        key = name if (name in GLOBAL_SEMS or not self._regions) else (self._region_id, name)
        if key not in self._sems:
            if key != name and self._free_handles:
                h = self._free_handles.pop()
            else:
                h = self.stack.enter_context(self.nc.semaphore(str(key)))
                self._sem_counts[id(h)] = 0
            self._sems[key] = h
            if key != name:
                self._region_sems.append(key)
        return self._sems[key]

    def val(self, name):
        return self._sem_counts[id(self._sem(name))]

    def wait(self, engine, name, value=None):
        h = self._sem(name)
        v = self._sem_counts[id(h)] if value is None else value
        self.streams[engine].append(lambda eng, h=h, v=v: eng.wait_ge(h, v))

    def do(self, engine, fn, inc=None, n=1):
        if inc is None:
            self.streams[engine].append(fn)
        else:
            h = self._sem(inc)
            self._sem_counts[id(h)] += n
            self.streams[engine].append(
                lambda eng, fn=fn, h=h, n=n: fn(eng).then_inc(h, n))

    def fconst(self, val, dtype=F32):
        key = (val, dtype)
        if not hasattr(self, "_fconsts"):
            self._fconsts = {}
        if key not in self._fconsts:
            t = self.sbuf(f"fc_{len(self._fconsts)}", [128, 1], dtype)
            self.do("vector", lambda e, t=t, val=val: e.memset(t[:], val), inc="fcinit")
            self._fconsts[key] = t
        return self._fconsts[key]

    def finalize(self):
        with self.nc.Block() as blk:
            for e in ENGINES:
                fns = self.streams[e]
                if fns:
                    getattr(blk, e)(lambda eng, fns=fns: [f(eng) for f in fns])
        self.stack.close()
        return self.nc



def rotary_tables_np():
    DH, WINDOW = 64, 256
    f32 = np.float32
    half = np.arange(0, DH, 2, dtype=f32)
    inv_freq = (1.0 / 10000.0 ** (half / DH)).astype(f32)
    t = np.arange(T, dtype=f32)
    freqs = (t[:, None] * inv_freq[None, :]).astype(f32)
    freqs = np.concatenate([freqs, freqs], -1)
    base_scale = ((half + 0.4 * DH) / (1.4 * DH)).astype(f32)
    power = ((t - T // 2) / (WINDOW // 2)).astype(f32)
    scale = (base_scale[None, :] ** power[:, None]).astype(f32)
    scale = np.concatenate([scale, scale], -1)
    cos, sin = np.cos(freqs).astype(f32), np.sin(freqs).astype(f32)
    inv = (1.0 / scale).astype(f32)
    sgn = np.concatenate([-np.ones(32, f32), np.ones(32, f32)])
    CS = np.ascontiguousarray((cos * scale).T)
    SS = np.ascontiguousarray((sin * scale).T * sgn[:, None])
    CI = np.ascontiguousarray((cos * inv).T)
    SI = np.ascontiguousarray((sin * inv).T * sgn[:, None])
    dup = lambda A: np.ascontiguousarray(np.concatenate([A, A], axis=0))
    return dup(CS), dup(SS), dup(CI), dup(SI)


def host_constants():
    f32 = np.float32
    C = {}
    C["c_ident_bf"] = np.eye(128, dtype=bfloat16)
    C["c_ident_f32"] = np.eye(128, dtype=f32)
    C["c_ltri"] = np.triu(np.ones((128, 128), f32), 1)
    tri = np.triu(np.ones((128, 128), bool), 1)
    C["c_mask0"] = np.where(tri, f32(NEG0), f32(0)).astype(f32)
    C["c_maskinf"] = np.where(tri, f32(NEGINF), f32(0)).astype(f32)
    oi = np.arange(128)[:, None]
    s_ = np.arange(512)[None, :]
    masks = []
    for qsub in (0, 1):
        o = oi + qsub * 128
        band = (s_ < o) | (s_ > o + 256)
        masks.append(np.where(band, f32(NEGINF), f32(0)).astype(f32))
        masks.append(np.where(band | (s_ < 256), f32(NEGINF), f32(0)).astype(f32))
    C["c_lmask"] = np.ascontiguousarray(np.stack(masks))
    C["c_iota8"] = np.tile(np.arange(8, dtype=f32)[None, :], (128, 1))
    hm = np.zeros((128, 2), f32)
    hm[:64, 0] = 1.0
    hm[64:, 1] = 1.0
    C["c_halfmask"] = hm
    C["c_halfsel"] = np.ascontiguousarray(hm.T)
    CS, SS, CI, SI = rotary_tables_np()
    C["c_rot_cs"], C["c_rot_ss"], C["c_rot_ci"], C["c_rot_si"] = CS, SS, CI, SI
    return C


def vec_pm(v):
    n = v.shape[0] // 128
    return np.ascontiguousarray(v.reshape(n, 128).T)


def host_weights(I):
    W = {}
    W["w_l0a_q"] = np.ascontiguousarray(I["L0_attn_Q_w"].transpose(1, 0, 2).reshape(D, D))
    W["w_l0a_k"] = np.ascontiguousarray(I["L0_attn_K_w"].transpose(1, 0, 2).reshape(D, D))
    W["w_l0a_v"] = np.ascontiguousarray(I["L0_attn_V_w"].transpose(1, 0, 2).reshape(D, D))
    W["w_l0a_o"] = np.ascontiguousarray(I["L0_attn_O_w"].reshape(D, D))
    W["b_l0a_q"] = vec_pm(np.asarray(I["L0_attn_Q_b"]).reshape(-1))
    W["b_l0a_k"] = vec_pm(np.asarray(I["L0_attn_K_b"]).reshape(-1))
    W["b_l0a_v"] = vec_pm(np.asarray(I["L0_attn_V_b"]).reshape(-1))
    W["b_l0a_o"] = vec_pm(np.asarray(I["L0_attn_O_b"]))
    W["w_l0f_qkv"] = np.ascontiguousarray(I["L0_ffn_in_proj_w"].T)
    W["b_l0f_qkv"] = vec_pm(np.asarray(I["L0_ffn_in_proj_b"]))
    W["w_l0f_o"] = np.ascontiguousarray(I["L0_ffn_out_proj_w"].T)
    W["b_l0f_o"] = vec_pm(np.asarray(I["L0_ffn_out_proj_b"]))
    W["g_l0f_1"] = vec_pm(np.asarray(I["L0_ffn_ln1_g"]))
    W["bb_l0f_1"] = vec_pm(np.asarray(I["L0_ffn_ln1_b"]))
    W["g_l0f_2"] = vec_pm(np.asarray(I["L0_ffn_ln2_g"]))
    W["bb_l0f_2"] = vec_pm(np.asarray(I["L0_ffn_ln2_b"]))
    W["w_moe_g"] = np.ascontiguousarray(I["L0_ffn_wg"])
    W["w_moe_1"] = np.ascontiguousarray(I["L0_ffn_w1"]).reshape(8 * 1024, 4096)
    W["w_moe_2"] = np.ascontiguousarray(I["L0_ffn_w2"]).reshape(8 * 4096, 1024)
    W["b_moe_1"] = np.stack([vec_pm(np.asarray(I["L0_ffn_b1"][e])) for e in range(8)])
    W["b_moe_2"] = np.stack([vec_pm(np.asarray(I["L0_ffn_b2"][e])) for e in range(8)])
    W["w_l1a_q"] = np.ascontiguousarray(I["L1_attn_Wq"].transpose(1, 0, 2).reshape(D, D))
    W["w_l1a_k"] = np.ascontiguousarray(I["L1_attn_Wk"])
    W["w_l1a_v"] = np.ascontiguousarray(I["L1_attn_Wv"])
    W["w_l1a_p"] = np.ascontiguousarray(I["L1_attn_Wp"])
    W["b_l1a_p"] = vec_pm(np.asarray(I["L1_attn_bp"]))
    W["w_l1f_qkv"] = np.ascontiguousarray(I["L1_ffn_Wqkv"]).reshape(6 * D, 3 * D)
    W["w_l1f_o"] = np.ascontiguousarray(I["L1_ffn_Wo"]).reshape(6 * D, D)
    DI = 2730
    w1 = np.asarray(I["L1_ffn_W1"])
    w2 = np.asarray(I["L1_ffn_W2"])
    b1 = np.asarray(I["L1_ffn_b1"])
    w1a = np.zeros((6, D, FF2), np.float32); w1a[:, :, :DI] = w1[:, :, :DI]
    w1g = np.zeros((6, D, FF2), np.float32); w1g[:, :, :DI] = w1[:, :, DI:]
    w2p = np.zeros((6, FF2, D), np.float32); w2p[:, :DI, :] = w2
    b1a = np.zeros((6, FF2), np.float32); b1a[:, :DI] = b1[:, :DI]
    b1g = np.zeros((6, FF2), np.float32); b1g[:, :DI] = b1[:, DI:]
    W["w_l1f_1a"] = np.ascontiguousarray(w1a).reshape(6 * D, FF2)
    W["w_l1f_1g"] = np.ascontiguousarray(w1g).reshape(6 * D, FF2)
    W["w_l1f_2p"] = np.ascontiguousarray(w2p).reshape(6 * FF2, D)
    W["b_l1f_1a"] = np.concatenate([vec_pm(b1a[i]) for i in range(6)], 0)
    W["b_l1f_1g"] = np.concatenate([vec_pm(b1g[i]) for i in range(6)], 0)
    W["b_l1f_2"] = np.concatenate([vec_pm(np.asarray(I["L1_ffn_b2"][i])) for i in range(6)], 0)
    W["g_l1f"] = np.concatenate([vec_pm(np.asarray(I["L1_ffn_ln_g"][i])) for i in range(6)], 0)
    W["bb_l1f"] = np.concatenate([vec_pm(np.asarray(I["L1_ffn_ln_b"][i])) for i in range(6)], 0)
    qs = np.asarray(I["L1_ffn_q_scale"]); ks = np.asarray(I["L1_ffn_k_scale"])
    W["s_l1f_q"] = np.ascontiguousarray(np.concatenate([qs, qs], 1).reshape(6 * 128, 1))
    W["s_l1f_k"] = np.ascontiguousarray(np.concatenate([ks, ks], 1).reshape(6 * 128, 1))
    return W



def load_const(p, tag, dram_name, shape, dtype=F32, rows=None):
    t = p.sbuf(tag, shape, dtype)
    src_ap = p.drams[dram_name][:] if rows is None else p.drams[dram_name][rows[0]:rows[1], :]
    p.do("sync", lambda e, src_ap=src_ap: e.dma_start(out=t[:], in_=src_ap), inc="cld", n=16)
    return t


def wait_consts(p, *engines):
    for e in engines:
        p.wait(e, "cld", p.val("cld"))


def st_load_stream(p, tag, dram_bf16, kch=NCH, guard=0, ncols=T, shift_half=False):
    x = p.sbuf(f"{tag}_X", [128, kch, ncols], BF16)
    n = (ncols + TC - 1) // TC
    xbase = p.val(f"{tag}_xld")
    nld = 0
    for tc in range(n):
        w = min(TC, ncols - tc * TC)
        if not shift_half:
            p.do("sync", lambda e, tc=tc, w=w: e.dma_start(
                out=x[:, :, tc * TC:tc * TC + w],
                in_=dram_bf16[:, guard + tc * TC: guard + tc * TC + w]
                    .rearrange("(c p) t -> p c t", p=128)), inc=f"{tag}_xld", n=16)
            nld += 1
        else:
            h = kch // 2
            p.do("sync", lambda e, tc=tc, w=w, h=h: e.dma_start(
                out=x[:, 0:h, tc * TC:tc * TC + w],
                in_=dram_bf16[0:h * 128, tc * TC: tc * TC + w]
                    .rearrange("(c p) t -> p c t", p=128)), inc=f"{tag}_xld", n=16)
            if tc == 0:
                p.do("sync", lambda e, w=w, h=h: e.dma_start(
                    out=x[:, h:kch, 1:w],
                    in_=dram_bf16[h * 128:kch * 128, 0:w - 1]
                        .rearrange("(c p) t -> p c t", p=128)), inc=f"{tag}_xld", n=16)
                p.wait("vector", f"{tag}_xld", xbase + 16 * (nld + 2))
                p.do("vector", lambda e, h=h: e.memset(x[:, h:kch, 0:1], 0.0), inc=f"{tag}_shz")
            else:
                p.do("sync", lambda e, tc=tc, w=w, h=h: e.dma_start(
                    out=x[:, h:kch, tc * TC:tc * TC + w],
                    in_=dram_bf16[h * 128:kch * 128, tc * TC - 1: tc * TC + w - 1]
                        .rearrange("(c p) t -> p c t", p=128)), inc=f"{tag}_xld", n=16)
            nld += 2
    if shift_half:
        p.wait("tensor", f"{tag}_shz", p.val(f"{tag}_shz"))
    return x, p.val(f"{tag}_xld")



def emit_proj(p, tag, x_sb, kch, w_dram, nout, epilogue, n_banks=2,
              tc_list=None, x_col0=0, w_row0=0, mo_list=None,
              banks=None, bank_state=None, wbufs=None):
    n_mo = (nout + 127) // 128
    if mo_list is None:
        mo_list = list(range(n_mo))
    if tc_list is None:
        tc_list = list(range(NTC))
    if banks is None:
        banks = [p.psum(f"{tag}_pb{i}", [128, TC], F32) for i in range(n_banks)]
    n_banks = len(banks)
    if bank_state is None:
        bank_state = [None] * n_banks
    if wbufs is None:
        wbufs = [p.sbuf(f"{tag}_w{i}", [128, kch, 128], BF16) for i in range(2)]
    wbuf_mm_target = [0, 0]
    ld, evac = f"{tag}_wld", f"{tag}_ev"
    ldbase = p.val(ld)
    bi = 0
    nld = 0
    for wi, mo in enumerate(mo_list):
        mwid = min(128, nout - mo * 128)
        buf = wbufs[wi % 2]
        if wi >= 2:
            p.wait("gpsimd", "mm", wbuf_mm_target[wi % 2])
        p.do("gpsimd", lambda e, buf=buf, mo=mo, mwid=mwid: e.dma_start(
            out=buf[:, :, 0:mwid],
            in_=w_dram[w_row0:w_row0 + kch * 128, mo * 128:mo * 128 + mwid]
                .rearrange("(c k) m -> k c m", k=128)), inc=ld, n=16)
        nld += 1
        p.wait("tensor", ld, ldbase + 16 * nld)
        for tc in tc_list:
            if bank_state[bi % n_banks] is not None:
                gsem, gval = bank_state[bi % n_banks]
                p.wait("tensor", gsem, gval)
            ps = banks[bi % n_banks]
            for k in range(kch):
                p.do("tensor", lambda e, ps=ps, buf=buf, k=k, mwid=mwid, tc=tc: e.matmul(
                    ps[0:mwid, :], buf[:, k, 0:mwid],
                    x_sb[:, k, x_col0 + tc * TC: x_col0 + (tc + 1) * TC],
                    start=(k == 0), stop=(k == kch - 1)),
                    inc=("mm" if k == kch - 1 else None))
            tgt = p.val(evac) + 1
            epilogue(mo, tc, ps, mwid)
            assert p.val(evac) == tgt, f"{tag}: epilogue must inc {evac} exactly once"
            bank_state[bi % n_banks] = (evac, tgt)
            bi += 1
        wbuf_mm_target[wi % 2] = p.val("mm")
    return evac



def st_ln(p, tag, x_dram, out_f32, out_bf16, g_sb=None, b_sb=None,
          eps=1e-5, x_guard=0, out_guard=0, g_name=None, b_name=None,
          out_tm=None, post_act=None, shift_half=False, g_rows=None):
    if g_name is not None:
        g_sb = load_const(p, f"{tag}_g", g_name, [128, NCH], rows=g_rows)
        b_sb = load_const(p, f"{tag}_bb", b_name, [128, NCH], rows=g_rows)
        wait_consts(p, "vector")
    xt = [p.sbuf(f"{tag}_x{i}", [128, NCH, TC], F32) for i in range(2)]
    ot = p.sbuf(f"{tag}_o", [128, NCH, TC], F32)
    ob = p.sbuf(f"{tag}_ob", [128, NCH, TC], BF16)
    sq = p.sbuf(f"{tag}_sq", [128, TC], F32)
    strow = p.sbuf(f"{tag}_st", [1, 4 * TC], F32)
    ones = p.sbuf(f"{tag}_1s", [128, 1], F32)
    onesr = p.sbuf(f"{tag}_1r", [1, 128], F32)
    p.do("vector", lambda e: e.memset(ones[:], 1.0))
    p.do("vector", lambda e: e.memset(onesr[:], 1.0), inc="ve")
    epst = p.fconst(float(eps))
    if out_tm is not None:
        ident_tm = load_const(p, f"{tag}_idtm", "c_ident_bf", [128, 128], BF16)
        wait_consts(p, "tensor")
        tm_ps = p.psum(f"{tag}_tmps", [128, 1024], BF16)
        tm_sb = p.sbuf(f"{tag}_tmsb", [128, 128], BF16)
    ps_s = p.psum(f"{tag}_pss", [1, TC], F32)
    ps_s2 = p.psum(f"{tag}_pss2", [1, TC], F32)
    ps_b = p.psum(f"{tag}_psb", [128, 2 * TC], F32)
    ldsem, stsem = f"{tag}_ld", f"{tag}_stores"
    p.wait("tensor", "ve", p.val("ve"))
    lnbase = p.val(ldsem)
    appbase = p.val(f"{tag}_app")
    nld_ln = 0
    for tc in range(NTC):
        x = xt[tc % 2]
        if tc >= 2:
            p.wait("sync", f"{tag}_app", appbase + tc - 1)
        if not shift_half:
            p.do("sync", lambda e, x=x, tc=tc: e.dma_start(
                out=x[:],
                in_=x_dram[:, x_guard + tc * TC: x_guard + (tc + 1) * TC]
                    .rearrange("(c p) t -> p c t", p=128)), inc=ldsem, n=16)
            nld_ln += 1
        else:
            hh_ = NCH // 2
            p.do("sync", lambda e, x=x, tc=tc, hh_=hh_: e.dma_start(
                out=x[:, 0:hh_, :],
                in_=x_dram[0:hh_ * 128, tc * TC:(tc + 1) * TC]
                    .rearrange("(c p) t -> p c t", p=128)), inc=ldsem, n=16)
            if tc == 0:
                p.do("sync", lambda e, x=x, hh_=hh_: e.dma_start(
                    out=x[:, hh_:NCH, 1:TC],
                    in_=x_dram[hh_ * 128:D, 0:TC - 1]
                        .rearrange("(c p) t -> p c t", p=128)), inc=ldsem, n=16)
                p.wait("vector", ldsem, lnbase + 16 * (nld_ln + 2))
                p.do("vector", lambda e, x=x, hh_=hh_: e.memset(x[:, hh_:NCH, 0:1], 0.0),
                     inc="ve")
            else:
                p.do("sync", lambda e, x=x, tc=tc, hh_=hh_: e.dma_start(
                    out=x[:, hh_:NCH, :],
                    in_=x_dram[hh_ * 128:D, tc * TC - 1:(tc + 1) * TC - 1]
                        .rearrange("(c p) t -> p c t", p=128)), inc=ldsem, n=16)
            nld_ln += 2
        p.wait("tensor", ldsem, lnbase + 16 * nld_ln)
        p.wait("tensor", "ve", p.val("ve"))
        for k in range(NCH):
            p.do("tensor", lambda e, x=x, k=k: e.matmul(
                ps_s[:], ones[:], x[:, k, :], start=(k == 0), stop=(k == NCH - 1)))
        p.wait("scalar", ldsem, lnbase + 16 * nld_ln)
        for k in range(NCH):
            p.wait("scalar", "mm", p.val("mm"))
            p.do("scalar", lambda e, x=x, k=k: e.activation(out=sq[:], in_=x[:, k, :], func=AF.Square), inc="ac")
            p.wait("tensor", "ac", p.val("ac"))
            p.do("tensor", lambda e, k=k: e.matmul(
                ps_s2[:], ones[:], sq[:], start=(k == 0), stop=(k == NCH - 1)),
                inc="mm")
        p.wait("vector", "mm", p.val("mm"))
        mu, var = strow[:, 0:TC], strow[:, TC:2 * TC]
        sd, r = strow[:, 2 * TC:3 * TC], strow[:, 3 * TC:4 * TC]
        p.do("vector", lambda e: e.tensor_scalar_mul(mu, ps_s[:], 1.0 / D))
        p.do("vector", lambda e: e.tensor_scalar_mul(var, ps_s2[:], 1.0 / D))
        p.do("vector", lambda e: e.tensor_tensor(sq[0:1, :], mu, mu, ALU.mult))
        p.do("vector", lambda e: e.tensor_sub(var, var, sq[0:1, :]), inc="ve")
        p.wait("scalar", "ve", p.val("ve"))
        p.wait("scalar", "fcinit", p.val("fcinit"))
        p.do("scalar", lambda e: e.activation(out=sd, in_=var, func=AF.Sqrt, bias=epst[0:1, :]), inc="ac")
        p.wait("vector", "ac", p.val("ac"))
        p.do("vector", lambda e: e.reciprocal(r, sd), inc="ve")
        p.wait("tensor", "ve", p.val("ve"))
        p.do("tensor", lambda e: e.matmul(ps_b[:, 0:TC], onesr[:], mu, start=True, stop=True))
        p.do("tensor", lambda e: e.matmul(ps_b[:, TC:2 * TC], onesr[:], r, start=True, stop=True), inc="mm")
        p.wait("vector", "mm", p.val("mm"))
        if tc >= 1:
            p.wait("vector", stsem, p.val(stsem))
        for k in range(NCH):
            p.do("vector", lambda e, x=x, k=k: e.tensor_sub(ot[:, k, :], x[:, k, :], ps_b[:, 0:TC]))
            p.do("vector", lambda e, k=k: e.tensor_mul(ot[:, k, :], ot[:, k, :], ps_b[:, TC:2 * TC]))
            if g_sb is not None:
                p.do("vector", lambda e, k=k: e.tensor_scalar(
                    ot[:, k, :], ot[:, k, :], g_sb[:, k:k + 1], b_sb[:, k:k + 1], ALU.mult, ALU.add))
            if post_act is not None:
                p.do("vector", lambda e: e.engine_nop(), inc="vpa")
                p.wait("scalar", "vpa", p.val("vpa"))
                p.do("scalar", lambda e, k=k: e.activation(
                    out=ot[:, k, :], in_=ot[:, k, :], func=post_act), inc="acpa")
                p.wait("vector", "acpa", p.val("acpa"))
            p.do("vector", lambda e, k=k: e.tensor_copy(ob[:, k, :], ot[:, k, :]),
                 inc=("ve" if k == NCH - 1 else None))
        p.do("vector", lambda e: e.engine_nop(), inc=f"{tag}_app")
        p.wait("sync", "ve", p.val("ve"))
        if out_f32 is not None:
            p.do("sync", lambda e, tc=tc: e.dma_start(
                out=out_f32[:, out_guard + tc * TC: out_guard + (tc + 1) * TC]
                    .rearrange("(c p) t -> p c t", p=128), in_=ot[:]), inc=stsem, n=16)
        if out_bf16 is not None:
            p.do("sync", lambda e, tc=tc: e.dma_start(
                out=out_bf16[:, out_guard + tc * TC: out_guard + (tc + 1) * TC]
                    .rearrange("(c p) t -> p c t", p=128), in_=ob[:]), inc=stsem, n=16)
        if out_tm is not None:
            p.wait("tensor", "ve", p.val("ve"))
            for k in range(NCH):
                for j in range(4):
                    p.wait("tensor", f"{tag}_tmev", p.val(f"{tag}_tmev"))
                    p.do("tensor", lambda e, k=k, j=j: e.transpose(
                        tm_ps[:, 0:128], ob[:, k, j * 128:(j + 1) * 128], ident_tm[:]),
                        inc="mm2")
                    p.wait("vector", "mm2", p.val("mm2"))
                    p.wait("vector", stsem, p.val(stsem))
                    p.do("vector", lambda e: e.tensor_copy(tm_sb[:], tm_ps[:, 0:128]),
                         inc=f"{tag}_tmev")
                    p.wait("sync", f"{tag}_tmev", p.val(f"{tag}_tmev"))
                    p.do("sync", lambda e, k=k, j=j, tc=tc: e.dma_start(
                        out=out_tm[tc * TC + j * 128: tc * TC + (j + 1) * 128,
                                   k * 128:(k + 1) * 128], in_=tm_sb[:]), inc=stsem, n=16)
    p.wait("sync", stsem, p.val(stsem))



def st_l0_qkv(p):
    x_sb, ldcnt = st_load_stream(p, "a2", p.drams["xn_bf"])
    p.wait("tensor", "a2_xld", ldcnt)
    bq = load_const(p, "a2bq", "b_l0a_q", [128, 8])
    bk = load_const(p, "a2bk", "b_l0a_k", [128, 8])
    bv = load_const(p, "a2bv", "b_l0a_v", [128, 8])
    ident = load_const(p, "a2id", "c_ident_bf", [128, 128], BF16)
    wait_consts(p, "scalar", "tensor")
    ev = p.sbuf("a2_ev", [128, TC], BF16)
    evt = p.sbuf("a2_evt", [128, TC], BF16)
    psb = p.psum("a2_trp", [128, 1024], BF16)
    q_fm, k_fm, v_tm = p.drams["l0_q"], p.drams["l0_k"], p.drams["l0_v"]

    def mk_epi(bias, dst, is_v, tag):
        def epi(mo, tc, ps, mwid):
            evac = f"{tag}_ev"
            p.wait("scalar", "a2st", p.val("a2st"))
            p.wait("scalar", "mm", p.val("mm"))
            p.do("scalar", lambda e: e.activation(
                out=ev[:], in_=ps[:], func=AF.Identity, bias=bias[:, mo:mo + 1]), inc="ac")
            if not is_v:
                p.wait("sync", "ac", p.val("ac"))
                p.do("sync", lambda e: e.dma_start(
                    out=dst[mo * 128:(mo + 1) * 128, tc * TC:(tc + 1) * TC], in_=ev[:]),
                    inc="a2st", n=16)
                p.do("sync", lambda e: e.nop(), inc=evac)
            else:
                p.wait("tensor", "ac", p.val("ac"))
                p.wait("tensor", "ve2", p.val("ve2"))
                for j in range(4):
                    p.do("tensor", lambda e, j=j: e.transpose(
                        psb[:, j * 128:(j + 1) * 128], ev[:, j * 128:(j + 1) * 128], ident[:]),
                        inc=("mm2" if j == 3 else None))
                p.wait("vector", "mm2", p.val("mm2"))
                p.wait("vector", "a2st", p.val("a2st"))
                for j in range(4):
                    p.do("vector", lambda e, j=j: e.tensor_copy(
                        evt[:, j * 128:(j + 1) * 128], psb[:, j * 128:(j + 1) * 128]),
                        inc=("ve2" if j == 3 else None))
                p.wait("sync", "ve2", p.val("ve2"))
                for j in range(4):
                    for hh in range(2):
                        p.do("sync", lambda e, mo=mo, tc=tc, j=j, hh=hh: e.dma_start(
                            out=dst[2 * mo + hh, tc * TC + j * 128: tc * TC + (j + 1) * 128, :],
                            in_=evt[:, j * 128 + hh * 64: j * 128 + hh * 64 + 64]),
                            inc="a2st", n=16)
                p.do("sync", lambda e: e.nop(), inc=evac)
        return epi

    emit_proj(p, "a2q", x_sb, NCH, p.drams["w_l0a_q"], D, mk_epi(bq, q_fm, False, "a2q"))
    emit_proj(p, "a2k", x_sb, NCH, p.drams["w_l0a_k"], D, mk_epi(bk, k_fm, False, "a2k"))
    emit_proj(p, "a2v", x_sb, NCH, p.drams["w_l0a_v"], D, mk_epi(bv, v_tm, True, "a2v"))
    p.wait("sync", "a2st", p.val("a2st"))


def st_causal_attn(p, tag, q_fm, k_fm, v_tm, o_fm, n_heads, d_head, inv_scale,
                   mask_name, k_dram_rows=None):
    ident = load_const(p, f"{tag}id", "c_ident_bf", [128, 128], BF16)
    mask = load_const(p, f"{tag}mk", mask_name, [128, 128])
    wait_consts(p, "tensor", "vector")
    shared_kv = (k_dram_rows == d_head)
    kq_parts = d_head
    k_sb = p.sbuf(f"{tag}_k", [kq_parts, T], BF16)
    v_sb = p.sbuf(f"{tag}_v", [128, 32, d_head + 1], BF16)
    q_sb = [p.sbuf(f"{tag}_q{i}", [kq_parts, 128], BF16) for i in range(2)]
    S_sb = p.sbuf(f"{tag}_S", [128, T], F32)
    P_sb = p.sbuf(f"{tag}_P", [128, T], BF16)
    S_ps = [p.psum(f"{tag}_sp{i}", [128, TC], F32) for i in range(2)]
    PT_ps = [p.psum(f"{tag}_ptp{i}", [128, 1024], BF16) for i in range(2)]
    PT_sb = p.sbuf(f"{tag}_pt", [128, 2, 128], BF16)
    AV_ps = p.psum(f"{tag}_av", [128, TC], F32)
    AVc = p.sbuf(f"{tag}_avc", [128, d_head + 1], F32)
    O32 = p.sbuf(f"{tag}_o32", [128, d_head], F32)
    Obf = p.sbuf(f"{tag}_obf", [128, d_head], BF16)
    OT_ps = p.psum(f"{tag}_otp", [128, 1024], BF16)
    OT_sb = p.sbuf(f"{tag}_ot", [d_head, 128], BF16)
    nm = p.sbuf(f"{tag}_nm", [128, 1], F32)
    nmr = p.sbuf(f"{tag}_nmr", [128, 1], F32)
    kl, vl, ql = f"{tag}_kld", f"{tag}_vld", f"{tag}_qld"
    hd = f"{tag}_hd"
    hdbase = p.val(hd)
    for h in range(n_heads):
        if shared_kv:
            kv_load = (h == 0)
            kv_idx = 0
        else:
            kv_load = True
            kv_idx = h
        if kv_load:
            if h > 0:
                p.wait("sync", hd, hdbase + h)
            krow = 0 if shared_kv else h * d_head
            p.do("sync", lambda e, krow=krow: e.dma_start(
                out=k_sb[:], in_=k_fm[krow:krow + d_head, :]), inc=kl, n=16)
            p.do("sync", lambda e, kv_idx=kv_idx: e.dma_start(
                out=v_sb[:, :, 0:d_head],
                in_=v_tm[kv_idx].rearrange("(tt p) d -> p tt d", p=128)), inc=vl, n=16)
            p.wait("vector", vl, p.val(vl))
            p.do("vector", lambda e: e.memset(v_sb[:, :, d_head:d_head + 1], 1.0), inc="ve")
        p.wait("tensor", kl, p.val(kl))
        p.wait("tensor", vl, p.val(vl))
        p.wait("tensor", "ve", p.val("ve"))
        for qs in range(32):
            qb = qs % 2
            p.do("sync", lambda e, h=h, qs=qs, qb=qb: e.dma_start(
                out=q_sb[qb][:], in_=q_fm[h * d_head:(h + 1) * d_head, qs * 128:(qs + 1) * 128]),
                inc=ql, n=16)
            p.wait("tensor", ql, p.val(ql))
            w_tot = (qs + 1) * 128
            n_kt = (w_tot + TC - 1) // TC
            for kt in range(n_kt):
                w = min(TC, w_tot - kt * TC)
                S = S_ps[kt % 2]
                p.wait("tensor", "ac", p.val("ac"))
                p.do("tensor", lambda e, S=S, qb=qb, kt=kt, w=w: e.matmul(
                    S[:, 0:w], q_sb[qb][:], k_sb[:, kt * TC:kt * TC + w],
                    start=True, stop=True), inc="mm")
                p.wait("scalar", "mm", p.val("mm"))
                p.do("scalar", lambda e, S=S, kt=kt, w=w: e.activation(
                    out=S_sb[:, kt * TC:kt * TC + w], in_=S[:, 0:w], func=AF.Identity,
                    scale=float(inv_scale)), inc="ac")
            p.wait("vector", "ac", p.val("ac"))
            p.do("vector", lambda e, w_tot=w_tot: e.scalar_tensor_tensor(
                S_sb[:, w_tot - 128:w_tot], mask[:], float(inv_scale),
                S_sb[:, w_tot - 128:w_tot], ALU.mult, ALU.add))
            p.do("vector", lambda e, w_tot=w_tot: e.reduce_max(
                nm[:], S_sb[:, 0:w_tot], axis=AX.X, negate=True), inc="ve")
            p.wait("scalar", "ve", p.val("ve"))
            p.wait("scalar", "mm3", p.val("mm3"))
            p.do("scalar", lambda e, w_tot=w_tot: e.activation(
                out=P_sb[:, 0:w_tot], in_=S_sb[:, 0:w_tot], func=AF.Exp, bias=nm[:]),
                inc="ac")
            p.wait("tensor", "ac", p.val("ac"))
            for j in range(w_tot // 128):
                tb = j % 2
                if j >= 2:
                    p.wait("tensor", "ve2", p.val("ve2"))
                p.do("tensor", lambda e, j=j, tb=tb: e.transpose(
                    PT_ps[tb][:, 0:128], P_sb[:, j * 128:(j + 1) * 128], ident[:]), inc="mm2")
                p.wait("vector", "mm2", p.val("mm2"))
                p.do("vector", lambda e, tb=tb: e.tensor_copy(
                    PT_sb[:, tb, :], PT_ps[tb][:, 0:128]), inc="ve2")
                p.wait("tensor", "ve2", p.val("ve2"))
                p.do("tensor", lambda e, j=j, tb=tb, w_tot=w_tot: e.matmul(
                    AV_ps[:, 0:d_head + 1], PT_sb[:, tb, :], v_sb[:, j, :],
                    start=(j == 0), stop=(j == w_tot // 128 - 1)),
                    inc=("mm3" if j == w_tot // 128 - 1 else None))
            if "dbg_S" in p.drams and h == 0 and qs == 0:
                p.wait("sync", "ac", p.val("ac"))
                p.do("sync", lambda e: e.dma_start(out=p.drams["dbg_S"][:], in_=S_sb[:, 0:128]), inc="dbg", n=16)
                dbgP = p.sbuf(f"{tag}_dbgP", [128, 128], F32)
                p.do("vector", lambda e: e.tensor_copy(dbgP[:], P_sb[:, 0:128]), inc="ve9")
                p.wait("sync", "ve9", p.val("ve9"))
                p.do("sync", lambda e: e.dma_start(out=p.drams["dbg_P"][:], in_=dbgP[:]), inc="dbg", n=16)
            p.wait("vector", "mm3", p.val("mm3"))
            if "dbg_S" in p.drams and h == 0 and qs == 0:
                dbgA = p.sbuf(f"{tag}_dbgA", [128, d_head + 1], F32)
                p.do("vector", lambda e: e.tensor_copy(dbgA[:], AV_ps[:, 0:d_head + 1]), inc="ve9")
                p.wait("sync", "ve9", p.val("ve9"))
                p.do("sync", lambda e: e.dma_start(out=p.drams["dbg_AV"][:], in_=dbgA[:]), inc="dbg", n=16)
            p.do("vector", lambda e: e.tensor_copy(AVc[:], AV_ps[:, 0:d_head + 1]), inc="vb")
            p.wait("vector", "vb", p.val("vb"))
            p.do("vector", lambda e: e.reciprocal(nmr[:], AVc[:, d_head:d_head + 1]), inc="vb")
            p.wait("vector", "vb", p.val("vb"))
            p.do("vector", lambda e: e.tensor_scalar_mul(
                O32[:], AVc[:, 0:d_head], nmr[:]), inc="vb")
            p.wait("vector", "vb", p.val("vb"))
            p.do("vector", lambda e: e.tensor_copy(Obf[:], O32[:]), inc="ve")
            if "dbg_S" in p.drams and h == 0 and qs == 0:
                p.wait("sync", "ve", p.val("ve"))
                p.do("sync", lambda e: e.dma_start(out=p.drams["dbg_N"][:, 0:d_head + 1], in_=AVc[:]), inc="dbg", n=16)
                p.do("sync", lambda e: e.dma_start(out=p.drams["dbg_N"][:, d_head + 1:d_head + 2], in_=nmr[:], allow_slow_non_contiguous=True), inc="dbg", n=16)
                p.do("sync", lambda e: e.dma_start(out=p.drams["dbg_N"][:, d_head + 2:2 * d_head + 2], in_=O32[:]), inc="dbg", n=16)
            p.wait("tensor", "ve", p.val("ve"))
            p.do("tensor", lambda e: e.transpose(OT_ps[0:d_head, 0:128], Obf[:], ident[:]), inc="mm2")
            p.wait("vector", "mm2", p.val("mm2"))
            p.wait("vector", f"{tag}_st", p.val(f"{tag}_st"))
            p.do("vector", lambda e: e.tensor_copy(OT_sb[:], OT_ps[0:d_head, 0:128]), inc="ve3")
            p.wait("sync", "ve3", p.val("ve3"))
            p.do("sync", lambda e, h=h, qs=qs: e.dma_start(
                out=o_fm[h * d_head:(h + 1) * d_head, qs * 128:(qs + 1) * 128], in_=OT_sb[:]),
                inc=f"{tag}_st", n=16)
        p.do("sync", lambda e: e.nop(), inc=hd)
    p.wait("sync", f"{tag}_st", p.val(f"{tag}_st"))


def st_proj_simple(p, tag, x_dram_name, w_name, nout, out_name, bias_name=None,
                   out_dtype="bf16", kch=NCH, act=None, out_guard=0, add_from=None,
                   w_row0=0, also_bf16=None):
    x_sb, ldcnt = st_load_stream(p, tag, p.drams[x_dram_name], kch=kch)
    p.wait("tensor", f"{tag}_xld", ldcnt)
    bias = load_const(p, f"{tag}_b", bias_name, [128, (nout + 127) // 128]) if bias_name else None
    wait_consts(p, "scalar")
    ev32 = p.sbuf(f"{tag}_e32", [128, TC], F32)
    evb = p.sbuf(f"{tag}_eb", [128, TC], BF16)
    dst = p.drams[out_name]
    dst_bf = p.drams[also_bf16] if also_bf16 else None
    addt = p.sbuf(f"{tag}_add", [128, TC], F32) if add_from else None

    def epi(mo, tc, ps, mwid):
        p.wait("scalar", "mm", p.val("mm"))
        p.wait("scalar", f"{tag}st", p.val(f"{tag}st"))
        kw = {}
        if bias is not None:
            kw["bias"] = bias[:, mo:mo + 1]
        fn = act if act is not None else AF.Identity
        if add_from is not None:
            p.wait("sync", "mm", p.val("mm"))
            p.do("sync", lambda e, mo=mo, tc=tc, mwid=mwid: e.dma_start(
                out=addt[0:mwid, :],
                in_=p.drams[add_from][mo * 128:mo * 128 + mwid, tc * TC:(tc + 1) * TC]),
                inc=f"{tag}_ald", n=16)
            p.wait("scalar", f"{tag}_ald", p.val(f"{tag}_ald"))
        p.do("scalar", lambda e, mo=mo, mwid=mwid: e.activation(
            out=(ev32 if out_dtype == "f32" else evb)[0:mwid, :], in_=ps[0:mwid, :], func=fn, **kw), inc="ac")
        src = ev32 if out_dtype == "f32" else evb
        if add_from is not None:
            p.wait("vector", "ac", p.val("ac"))
            p.do("vector", lambda e, mwid=mwid: e.tensor_add(src[0:mwid, :], src[0:mwid, :], addt[0:mwid, :]), inc="ve")
            if also_bf16 is not None:
                p.do("vector", lambda e, mwid=mwid: e.tensor_copy(evb[0:mwid, :], src[0:mwid, :]), inc="ve")
            p.wait("sync", "ve", p.val("ve"))
        else:
            if also_bf16 is not None:
                p.wait("vector", "ac", p.val("ac"))
                p.do("vector", lambda e, mwid=mwid: e.tensor_copy(evb[0:mwid, :], src[0:mwid, :]), inc="ve")
                p.wait("sync", "ve", p.val("ve"))
            else:
                p.wait("sync", "ac", p.val("ac"))
        p.do("sync", lambda e, mo=mo, tc=tc, mwid=mwid: e.dma_start(
            out=dst[mo * 128:mo * 128 + mwid, out_guard + tc * TC:out_guard + (tc + 1) * TC],
            in_=src[0:mwid, :]), inc=f"{tag}st", n=16)
        if also_bf16 is not None:
            p.do("sync", lambda e, mo=mo, tc=tc, mwid=mwid: e.dma_start(
                out=dst_bf[mo * 128:mo * 128 + mwid, out_guard + tc * TC:out_guard + (tc + 1) * TC],
                in_=evb[0:mwid, :]), inc=f"{tag}st", n=16)
        p.do("sync", lambda e: e.nop(), inc=f"{tag}_ev")

    emit_proj(p, tag, x_sb, kch, p.drams[w_name], nout, epi, w_row0=w_row0)
    p.wait("sync", f"{tag}st", p.val(f"{tag}st"))



def tm_epilogue(p, tag, ev, evt, psb, ident, dst_tm, bias):
    def epi(mo, tc, ps, mwid):
        p.wait("scalar", f"{tag}st", p.val(f"{tag}st"))
        p.wait("scalar", "mm", p.val("mm"))
        p.do("scalar", lambda e: e.activation(
            out=ev[:], in_=ps[:], func=AF.Identity, bias=bias[:, mo:mo + 1]), inc="ac")
        p.wait("tensor", "ac", p.val("ac"))
        p.wait("tensor", "ve2", p.val("ve2"))
        for j in range(4):
            p.do("tensor", lambda e, j=j: e.transpose(
                psb[:, j * 128:(j + 1) * 128], ev[:, j * 128:(j + 1) * 128], ident[:]),
                inc=("mm2" if j == 3 else None))
        p.wait("vector", "mm2", p.val("mm2"))
        p.wait("vector", f"{tag}st", p.val(f"{tag}st"))
        for j in range(4):
            p.do("vector", lambda e, j=j: e.tensor_copy(
                evt[:, j * 128:(j + 1) * 128], psb[:, j * 128:(j + 1) * 128]),
                inc=("ve2" if j == 3 else None))
        p.wait("sync", "ve2", p.val("ve2"))
        mo8 = mo % 8
        for j in range(4):
            p.do("sync", lambda e, mo8=mo8, tc=tc, j=j: e.dma_start(
                out=dst_tm(mo)[tc * TC + j * 128: tc * TC + (j + 1) * 128,
                               mo8 * 128:(mo8 + 1) * 128],
                in_=evt[:, j * 128:(j + 1) * 128]), inc=f"{tag}st", n=16)
        p.do("sync", lambda e: e.nop(), inc=f"{tag}_ev")
    return epi


def st_l0f_qkv(p):
    x_sb, ldcnt = st_load_stream(p, "b1", p.drams["xn_bf"])
    p.wait("tensor", "b1_xld", ldcnt)
    bias = load_const(p, "b1_b", "b_l0f_qkv", [128, 24])
    ident = load_const(p, "b1_id", "c_ident_bf", [128, 128], BF16)
    wait_consts(p, "scalar", "tensor")
    ev = p.sbuf("b1_ev", [128, TC], BF16)
    evt = p.sbuf("b1_evt", [128, TC], BF16)
    psb = p.psum("b1_trp", [128, 1024], BF16)
    dsts = [p.drams["q0_tm"], p.drams["k0_tm"], p.drams["v0_tm"]]
    epi = tm_epilogue(p, "b1", ev, evt, psb, ident, lambda mo: dsts[mo // 8], bias)
    emit_proj(p, "b1", x_sb, NCH, p.drams["w_l0f_qkv"], 3 * D, epi)
    p.wait("sync", "b1st", p.val("b1st"))


def st_l0f_mha(p):
    q0, k0a, v0a, o0 = (p.drams["q0_tm"], p.drams["k0_all"], p.drams["v0_all"],
                        p.drams["o0_fm"])
    ident = load_const(p, "b2_id", "c_ident_bf", [128, 128], BF16)
    wait_consts(p, "tensor")
    qt = [p.sbuf(f"b2_q{i}", [128, D], BF16) for i in range(2)]
    kv = [p.sbuf(f"b2_kv{i}", [128, D], BF16) for i in range(2)]
    vt32 = p.sbuf("b2_v32", [128, D], F32)
    prod = p.sbuf("b2_pr", [128, D], F32)
    sc = p.sbuf("b2_sc", [128, 32], F32)
    ex = p.sbuf("b2_ex", [128, 32], F32)
    nmx4 = p.sbuf("b2_nm", [128, 4], F32)
    bh4 = p.sbuf("b2_bh", [128, 4], F32)
    s4 = p.sbuf("b2_s4", [128, 4], F32)
    rs4 = p.sbuf("b2_rs", [128, 4], F32)
    oacc = p.sbuf("b2_o", [128, D], F32)
    tmp = p.sbuf("b2_t", [128, 256], F32)
    obf = p.sbuf("b2_ob", [128, D], BF16)
    trp = p.psum("b2_trp", [128, 1024], BF16)
    trs = p.sbuf("b2_trs", [128, 128], BF16)
    ql, kl = "b2_ql", "b2_kl"
    for tt in range(32):
        qb = tt % 2
        p.wait("sync", "b2_pr1", p.val("b2_pr1"))
        p.do("sync", lambda e, tt=tt, qb=qb: e.dma_start(
            out=qt[qb][:], in_=q0[tt * 128:(tt + 1) * 128, :]), inc=ql, n=16)
        for m in range(8):
            mb = m % 2
            if m >= 2:
                p.wait("sync", "b2_pr1", p.val("b2_pr1"))
            p.do("sync", lambda e, m=m, mb=mb, tt=tt: e.dma_start(
                out=kv[mb][:], in_=k0a[m, tt * 128:(tt + 1) * 128, :]), inc=kl, n=16)
            p.wait("vector", kl, p.val(kl))
            p.wait("vector", ql, p.val(ql))
            p.do("vector", lambda e, mb=mb, qb=qb: e.tensor_tensor(
                prod[:], qt[qb][:], kv[mb][:], ALU.mult), inc="vb")
            p.wait("vector", "vb", p.val("vb"))
            p.do("vector", lambda e, m=m: e.tensor_reduce(
                sc.rearrange("p (h m) -> p h m", m=8)[:, :, m], prod[:].rearrange(
                    "p (h d) -> p h d", d=256), axis=AX.X, op=ALU.add),
                inc="b2_pr1")
            if "dbg_pr" in p.drams and tt == 0 and m == 0:
                p.wait("sync", "b2_pr1", p.val("b2_pr1"))
                p.do("sync", lambda e: e.dma_start(out=p.drams["dbg_pr"][0], in_=prod[:]), inc="dbg", n=16)
                qf = p.sbuf("b2_dbgq", [128, D], F32)
                kf = p.sbuf("b2_dbgk", [128, D], F32)
                p.do("vector", lambda e, qb=qb: e.tensor_copy(qf[:], qt[qb][:]))
                p.do("vector", lambda e, mb=mb: e.tensor_copy(kf[:], kv[mb][:]), inc="ve9")
                p.wait("sync", "ve9", p.val("ve9"))
                p.do("sync", lambda e: e.dma_start(out=p.drams["dbg_pr"][1], in_=qf[:]), inc="dbg", n=16)
                p.do("sync", lambda e: e.dma_start(out=p.drams["dbg_pr"][2], in_=kf[:]), inc="dbg", n=16)
                p.wait("vector", "dbg", p.val("dbg"))
        p.wait("vector", "b2_pr1", p.val("b2_pr1"))
        for h in range(4):
            p.do("vector", lambda e, h=h: e.reduce_max(
                nmx4[:, h:h + 1], sc[:, h * 8:(h + 1) * 8], axis=AX.X, negate=True),
                inc=("vb" if h == 3 else None))
        p.wait("vector", "vb", p.val("vb"))
        p.do("vector", lambda e: e.tensor_scalar_mul(bh4[:], nmx4[:], 1.0 / 16.0), inc="vb")
        p.wait("scalar", "vb", p.val("vb"))
        for h in range(4):
            p.do("scalar", lambda e, h=h: e.activation(
                out=ex[:, h * 8:(h + 1) * 8], in_=sc[:, h * 8:(h + 1) * 8], func=AF.Exp,
                scale=1.0 / 16.0, bias=bh4[:, h:h + 1]), inc="ac")
        p.wait("vector", "ac", p.val("ac"))
        for h in range(4):
            p.do("vector", lambda e, h=h: e.reduce_sum(
                s4[:, h:h + 1], ex[:, h * 8:(h + 1) * 8], axis=AX.X),
                inc=("vb" if h == 3 else None))
        p.wait("vector", "vb", p.val("vb"))
        p.do("vector", lambda e: e.reciprocal(rs4[:], s4[:]), inc="vb")
        if "dbg_sc" in p.drams and tt == 0:
            p.wait("sync", "vb", p.val("vb"))
            p.wait("sync", "ac", p.val("ac"))
            p.do("sync", lambda e: e.dma_start(out=p.drams["dbg_sc"][:, 0:32], in_=sc[:]), inc="dbg", n=16)
            p.do("sync", lambda e: e.dma_start(out=p.drams["dbg_sc"][:, 32:64], in_=ex[:]), inc="dbg", n=16)
            p.do("sync", lambda e: e.dma_start(out=p.drams["dbg_sc"][:, 64:68], in_=s4[:]), inc="dbg", n=16)
            p.wait("vector", "dbg", p.val("dbg"))
            p.wait("scalar", "dbg", p.val("dbg"))
        p.do("vector", lambda e: e.memset(oacc[:], 0.0))
        p.wait("sync", "b2_pr1", p.val("b2_pr1"))
        for m in range(8):
            mb = m % 2
            p.wait("sync", "b2_pr2", p.val("b2_pr2"))
            p.do("sync", lambda e, m=m, mb=mb, tt=tt: e.dma_start(
                out=kv[mb][:], in_=v0a[m, tt * 128:(tt + 1) * 128, :]), inc=kl, n=16)
            p.wait("vector", kl, p.val(kl))
            p.do("vector", lambda e, mb=mb: e.tensor_copy(vt32[:], kv[mb][:]), inc="b2_pr2")
            for h in range(4):
                p.do("vector", lambda e, h=h, m=m: e.tensor_scalar_mul(
                    tmp[:], vt32[:, h * 256:(h + 1) * 256], ex[:, h * 8 + m:h * 8 + m + 1]))
                p.do("vector", lambda e, h=h: e.tensor_add(
                    oacc[:, h * 256:(h + 1) * 256], oacc[:, h * 256:(h + 1) * 256], tmp[:]))
        p.wait("vector", "vb", p.val("vb"))
        for h in range(4):
            p.do("vector", lambda e, h=h: e.tensor_scalar_mul(
                oacc[:, h * 256:(h + 1) * 256], oacc[:, h * 256:(h + 1) * 256],
                rs4[:, h:h + 1]))
        p.do("vector", lambda e: e.tensor_copy(obf[:], oacc[:]), inc="ve")
        p.wait("tensor", "ve", p.val("ve"))
        for k in range(8):
            p.wait("tensor", "b2_tev", p.val("b2_tev"))
            p.do("tensor", lambda e, k=k: e.transpose(
                trp[:, 0:128], obf[:, k * 128:(k + 1) * 128], ident[:]), inc="mm2")
            p.wait("vector", "mm2", p.val("mm2"))
            p.wait("vector", "b2_st", p.val("b2_st"))
            p.do("vector", lambda e: e.tensor_copy(trs[:], trp[:, 0:128]), inc="b2_tev")
            p.wait("sync", "b2_tev", p.val("b2_tev"))
            p.do("sync", lambda e, k=k, tt=tt: e.dma_start(
                out=o0[k * 128:(k + 1) * 128, tt * 128:(tt + 1) * 128], in_=trs[:]),
                inc="b2_st", n=16)
    p.wait("sync", "b2_st", p.val("b2_st"))


def st_moe_route_a(p):
    y1, wg_d = p.drams["y1_bf"], p.drams["w_moe_g"]
    ltri = load_const(p, "b3_lt", "c_ltri", [128, 128])
    iota8 = load_const(p, "b3_io", "c_iota8", [128, 8])
    wg = p.sbuf("b3_wg", [128, NCH, 8], BF16)
    p.do("gpsimd", lambda e: e.dma_start(
        out=wg[:], in_=wg_d.rearrange("(c k) m -> k c m", k=128)), inc="b3_wld", n=16)
    wait_consts(p, "tensor", "vector")
    p.wait("tensor", "b3_wld", p.val("b3_wld"))
    y1t = [p.sbuf(f"b3_y{i}", [128, NCH, 128], BF16) for i in range(2)]
    ones1r = p.sbuf("b3_1r", [1, 128], F32)
    ones128 = p.sbuf("b3_1c", [128, 1], F32)
    run = p.sbuf("b3_run", [1, 8], F32)
    g32 = p.sbuf("b3_g", [128, 8], F32)
    nmx = p.sbuf("b3_nm", [128, 1], F32)
    mx = p.sbuf("b3_mx", [128, 1], F32)
    eq = p.sbuf("b3_eq", [128, 8], F32)
    ex8 = p.sbuf("b3_ex", [128, 8], F32)
    s8 = p.sbuf("b3_s8", [128, 1], F32)
    H = p.sbuf("b3_H", [128, 8], F32)
    Ef = p.sbuf("b3_E", [128, 8], F32)
    tsel = p.sbuf("b3_ts", [128, 8], F32)
    loc_acc = p.sbuf("b3_loc", [128, 32], F32)
    idx_acc = p.sbuf("b3_idx", [128, 32], F32)
    gate_acc = p.sbuf("b3_gv", [128, 32], F32)
    ps_g = p.psum("b3_psg", [128, TC], F32)
    ps_E = p.psum("b3_psE", [128, TC], F32)
    ps_t = p.psum("b3_pst", [1, TC], F32)
    p.do("vector", lambda e: e.memset(ones1r[:], 1.0))
    p.do("vector", lambda e: e.memset(ones128[:], 1.0))
    p.do("vector", lambda e: e.memset(run[:], 0.0), inc="ve")
    p.wait("tensor", "ve", p.val("ve"))
    mm_after_tile = []
    for tt in range(32):
        yb = tt % 2
        if tt >= 2:
            p.wait("sync", "mm", mm_after_tile[tt - 2])
        p.do("sync", lambda e, tt=tt, yb=yb: e.dma_start(
            out=y1t[yb][:], in_=y1[:, tt * 128:(tt + 1) * 128]
                .rearrange("(c p) t -> p c t", p=128)), inc="b3_yld", n=16)
        p.wait("tensor", "b3_yld", p.val("b3_yld"))
        p.wait("tensor", "ve", p.val("ve"))
        for k in range(NCH):
            p.do("tensor", lambda e, k=k, yb=yb: e.matmul(
                ps_g[:, 0:8], y1t[yb][:, k, :], wg[:, k, :],
                start=(k == 0), stop=(k == NCH - 1)), inc=("mm" if k == NCH - 1 else None))
        p.wait("vector", "mm", p.val("mm"))
        def gdo(fn):
            p.do("vector", fn, inc="vb")
            p.wait("vector", "vb", p.val("vb"))
        gdo(lambda e: e.tensor_copy(g32[:], ps_g[:, 0:8]))
        if "dbg_g" in p.drams and tt == 0:
            p.wait("sync", "vb", p.val("vb"))
            p.do("sync", lambda e: e.dma_start(out=p.drams["dbg_g"][:], in_=g32[:]), inc="dbg", n=16)
            p.wait("vector", "dbg", p.val("dbg"))
            p.wait("tensor", "dbg", p.val("dbg"))
        gdo(lambda e: e.reduce_max(nmx[:], g32[:], axis=AX.X, negate=True))
        gdo(lambda e: e.tensor_scalar_mul(mx[:], nmx[:], -1.0))
        gdo(lambda e: e.tensor_single_scalar(eq[:], g32[:], mx[:], ALU.is_equal))
        gdo(lambda e: e.tensor_scalar(eq[:], eq[:], -1e9, 1e9, ALU.mult, ALU.add))
        gdo(lambda e: e.tensor_add(eq[:], eq[:], iota8[:]))
        gdo(lambda e, tt=tt: e.tensor_reduce(
            idx_acc[:, tt:tt + 1], eq[:], axis=AX.X, op=ALU.min))
        p.wait("scalar", "vb", p.val("vb"))
        p.do("scalar", lambda e: e.activation(out=ex8[:], in_=g32[:], func=AF.Exp,
                                              bias=nmx[:]), inc="ac")
        p.wait("vector", "ac", p.val("ac"))
        gdo(lambda e: e.reduce_sum(s8[:], ex8[:], axis=AX.X))
        gdo(lambda e, tt=tt: e.reciprocal(gate_acc[:, tt:tt + 1], s8[:]))
        gdo(lambda e, tt=tt: e.tensor_single_scalar(
            H[:], iota8[:], idx_acc[:, tt:tt + 1], ALU.is_equal))
        if "dbg_g" in p.drams and tt == 0:
            p.wait("sync", "vb", p.val("vb"))
            p.do("sync", lambda e: e.dma_start(out=p.drams["dbg_r"][:, 0:1], in_=nmx[:], allow_slow_non_contiguous=True), inc="dbg", n=16)
            p.do("sync", lambda e: e.dma_start(out=p.drams["dbg_r"][:, 1:2], in_=mx[:], allow_slow_non_contiguous=True), inc="dbg", n=16)
            p.do("sync", lambda e: e.dma_start(out=p.drams["dbg_r"][:, 2:10], in_=eq[:], allow_slow_non_contiguous=True), inc="dbg", n=16)
            p.do("sync", lambda e: e.dma_start(out=p.drams["dbg_r"][:, 10:18], in_=H[:], allow_slow_non_contiguous=True), inc="dbg", n=16)
            p.do("sync", lambda e: e.dma_start(out=p.drams["dbg_r"][:, 18:19], in_=idx_acc[:, 0:1], allow_slow_non_contiguous=True), inc="dbg", n=16)
            p.wait("vector", "dbg", p.val("dbg"))
            p.wait("tensor", "dbg", p.val("dbg"))
        p.wait("tensor", "vb", p.val("vb"))
        p.do("tensor", lambda e: e.matmul(ps_E[:, 0:8], ltri[:], H[:], start=True, stop=False))
        p.do("tensor", lambda e: e.matmul(ps_E[:, 0:8], ones1r[:], run[:], start=False, stop=True))
        p.do("tensor", lambda e: e.matmul(ps_t[:, 0:8], ones128[:], H[:], start=True, stop=True), inc="mm")
        p.wait("vector", "mm", p.val("mm"))
        gdo(lambda e: e.tensor_copy(Ef[:], ps_E[:, 0:8]))
        gdo(lambda e: e.tensor_mul(tsel[:], Ef[:], H[:]))
        gdo(lambda e, tt=tt: e.reduce_sum(loc_acc[:, tt:tt + 1], tsel[:], axis=AX.X))
        p.do("vector", lambda e: e.tensor_add(run[:], run[:], ps_t[0:1, 0:8]), inc="ve")
        mm_after_tile.append(p.val("mm"))
    p.wait("sync", "ve", p.val("ve"))
    p.do("sync", lambda e: e.dma_start(out=p.drams["r_loc"][:], in_=loc_acc[:]), inc="b3_st", n=16)
    p.do("sync", lambda e: e.dma_start(out=p.drams["r_idx"][:], in_=idx_acc[:]), inc="b3_st", n=16)
    p.do("sync", lambda e: e.dma_start(out=p.drams["r_gate"][:], in_=gate_acc[:]), inc="b3_st", n=16)
    p.do("sync", lambda e: e.dma_start(out=p.drams["r_hist"][:], in_=run[:]), inc="b3_st", n=16)
    p.wait("sync", "b3_st", p.val("b3_st"))


def st_moe_route_b(p):
    iota8 = load_const(p, "b4_io", "c_iota8", [128, 8])
    hist = load_const(p, "b4_h", "hist_all", [8, 8])
    csel = load_const(p, "b4_cs", "c_coresel", [8, 1])
    locs = load_const(p, "b4_lc", "r_loc", [128, 32])
    idxs = load_const(p, "b4_ix", "r_idx", [128, 32])
    wait_consts(p, "tensor", "vector")
    ones1r = p.sbuf("b4_1r", [1, 128], F32)
    co_row = p.sbuf("b4_cor", [1, 8], F32)
    co_bc = p.sbuf("b4_cob", [128, 8], F32)
    H = p.sbuf("b4_H", [128, 8], F32)
    t8 = p.sbuf("b4_t8", [128, 8], F32)
    col = p.sbuf("b4_col", [128, 6], F32)
    gk = p.sbuf("b4_gk", [128, 32], F32)
    gates_sb = load_const(p, "b4_gt", "r_gate", [128, 32])
    desti = p.sbuf("b4_di", [128, 32], I32)
    y1t = [p.sbuf(f"b4_y{i}", [128, D], BF16) for i in range(2)]
    ps1 = p.psum("b4_ps1", [1, TC], F32)
    ps2 = p.psum("b4_ps2", [128, TC], F32)
    disp = p.drams["moe_disp"]
    scbase = p.val("b4_sc")
    p.do("vector", lambda e: e.memset(ones1r[:], 1.0), inc="ve")
    p.wait("tensor", "ve", p.val("ve"))
    p.do("tensor", lambda e: e.matmul(ps1[:, 0:8], csel[:], hist[:], start=True, stop=True), inc="mm")
    p.wait("vector", "mm", p.val("mm"))
    p.do("vector", lambda e: e.tensor_copy(co_row[:], ps1[0:1, 0:8]), inc="ve")
    p.wait("tensor", "ve", p.val("ve"))
    p.do("tensor", lambda e: e.matmul(ps2[:, 0:8], ones1r[:], co_row[:], start=True, stop=True), inc="mm")
    p.wait("vector", "mm", p.val("mm"))
    p.do("vector", lambda e: e.tensor_copy(co_bc[:], ps2[:, 0:8]), inc="ve")
    def gdo(fn):
        p.do("vector", fn, inc="vb")
        p.wait("vector", "vb", p.val("vb"))
    for tt in range(32):
        yb = tt % 2
        gdo(lambda e, tt=tt: e.tensor_single_scalar(
            H[:], iota8[:], idxs[:, tt:tt + 1], ALU.is_equal))
        gdo(lambda e: e.tensor_mul(t8[:], H[:], co_bc[:]))
        gdo(lambda e: e.reduce_sum(col[:, 0:1], t8[:], axis=AX.X))
        gdo(lambda e, tt=tt: e.tensor_add(col[:, 1:2], locs[:, tt:tt + 1], col[:, 0:1]))
        gdo(lambda e: e.tensor_single_scalar(col[:, 2:3], col[:, 1:2], float(T * NCORE // 8), ALU.is_lt))
        gdo(lambda e, tt=tt: e.tensor_single_scalar(col[:, 3:4], locs[:, tt:tt + 1], float(MOE_C), ALU.is_lt))
        gdo(lambda e: e.tensor_mul(col[:, 2:3], col[:, 2:3], col[:, 3:4]))
        gdo(lambda e: e.tensor_scalar(col[:, 5:6], col[:, 2:3], -float(8 * MOE_C), float(8 * MOE_C), ALU.mult, ALU.add))
        gdo(lambda e, tt=tt: e.tensor_single_scalar(col[:, 4:5], idxs[:, tt:tt + 1], float(MOE_C), ALU.mult))
        gdo(lambda e, tt=tt: e.tensor_add(col[:, 4:5], col[:, 4:5], locs[:, tt:tt + 1]))
        gdo(lambda e: e.tensor_mul(col[:, 4:5], col[:, 4:5], col[:, 2:3]))
        gdo(lambda e: e.tensor_add(col[:, 4:5], col[:, 4:5], col[:, 5:6]))
        gdo(lambda e, tt=tt: e.tensor_mul(gk[:, tt:tt + 1], gates_sb[:, tt:tt + 1], col[:, 2:3]))
        p.do("vector", lambda e, tt=tt: e.tensor_copy(desti[:, tt:tt + 1], col[:, 4:5]), inc="ve")
        p.do("sync", lambda e, tt=tt, yb=yb: e.dma_start(
            out=y1t[yb][:], in_=p.drams["y1_tm"][tt * 128:(tt + 1) * 128, :]), inc="b4_yld", n=16)
        p.wait("gpsimd", "b4_yld", p.val("b4_yld"))
        p.wait("gpsimd", "ve", p.val("ve"))
        p.do("gpsimd", lambda e, tt=tt, yb=yb: e.indirect_dma_start(
            out=disp[:], out_offset=IndirectOffsetOnAxis(desti[:, tt:tt + 1], 0),
            in_=y1t[yb][:], in_offset=None), inc="b4_sc", n=16)
        if tt >= 1:
            p.wait("sync", "b4_sc", scbase + 16 * tt)
    p.wait("sync", "b4_sc", p.val("b4_sc"))
    p.do("sync", lambda e: e.dma_start(out=p.drams["r_dest"][:], in_=desti[:]), inc="b4_st", n=16)
    p.do("sync", lambda e: e.dma_start(out=p.drams["r_gatek"][:], in_=gk[:]), inc="b4_st", n=16)
    p.wait("sync", "b4_st", p.val("b4_st"))


B5_NEXP = 8
B5_DO_Y = True
B5_DO_H = True
B5_HMODE = 0


def st_moe_ffn(p):
    ident = load_const(p, "b5_id", "c_ident_bf", [128, 128], BF16)
    identf = load_const(p, "b5_idf", "c_ident_f32", [128, 128])
    b1v = p.sbuf("b5_b1", [128, 8, 32], F32)
    p.do("sync", lambda e: e.dma_start(
        out=b1v[:], in_=p.drams["b_moe_1"].rearrange("e p c -> p e c")), inc="cld", n=16)
    b2v = p.sbuf("b5_b2", [128, 8, 8], F32)
    p.do("sync", lambda e: e.dma_start(
        out=b2v[:], in_=p.drams["b_moe_2"].rearrange("e p c -> p e c")), inc="cld", n=16)
    wait_consts(p, "scalar", "tensor", "vector")
    disp, ydr = p.drams["moe_disp"], p.drams["moe_y"]
    dtile = p.sbuf("b5_dt", [128, D], BF16)
    dfm = p.sbuf("b5_dfm", [128, NCH, MOE_C], BF16)
    hfm = p.sbuf("b5_hfm", [128, 32, MOE_C], BF16)
    ev = p.sbuf("b5_ev", [128, TC], BF16)
    ev32 = p.sbuf("b5_e32", [128, TC], F32)
    evt = p.sbuf("b5_evt", [128, TC], F32)
    trp = p.psum("b5_trp", [128, 1024], BF16)
    trpf = p.psum("b5_trpf", [128, TC], F32)
    shared_banks = [p.psum(f"b5_pb{i}", [128, TC], F32) for i in range(2)]
    dcpbase = p.val("b5_dcp")
    shared_state = [None, None]
    wb_h = [p.sbuf(f"b5_wh{i}", [128, NCH, 128], BF16) for i in range(2)]
    wb_y = [p.sbuf(f"b5_wy{i}", [128, 32, 128], BF16) for i in range(2)]
    for e_i in range(B5_NEXP):
        for st_ in range(MOE_C // 128):
            p.wait("sync", "b5_dcp", dcpbase + NCH * (e_i * (MOE_C // 128) + st_))
            p.do("sync", lambda e, e_i=e_i, st_=st_: e.dma_start(
                out=dtile[:], in_=disp[e_i * MOE_C + st_ * 128: e_i * MOE_C + (st_ + 1) * 128, :]),
                inc="b5_dld", n=16)
            p.wait("tensor", "b5_dld", p.val("b5_dld"))
            for k in range(NCH):
                p.wait("tensor", "b5_dcp", p.val("b5_dcp"))
                p.do("tensor", lambda e, k=k: e.transpose(
                    trp[:, 0:128], dtile[:, k * 128:(k + 1) * 128], ident[:]), inc="mm2")
                p.wait("vector", "mm2", p.val("mm2"))
                p.do("vector", lambda e, k=k, st_=st_: e.tensor_copy(
                    dfm[:, k, st_ * 128:(st_ + 1) * 128], trp[:, 0:128]), inc="b5_dcp")
        p.wait("tensor", "b5_dcp", p.val("b5_dcp"))
        def epi_h(mo, tc, ps, mwid, e_i=e_i):
            p.wait("scalar", "mm", p.val("mm"))
            p.do("scalar", lambda e, mo=mo, tc=tc: e.activation(
                out=hfm[:, mo, tc * TC:(tc + 1) * TC], in_=ps[:], func=AF.Identity,
                bias=b1v[:, e_i, mo:mo + 1]), inc=f"b5h{e_i}_ev")
        if B5_DO_H:
            emit_proj(p, f"b5h{e_i}", dfm, NCH, p.drams["w_moe_1"], 4096, epi_h,
                      tc_list=list(range(MOE_C // TC)), banks=shared_banks,
                      bank_state=shared_state, wbufs=wb_h, w_row0=e_i * 1024)
        p.wait("tensor", f"b5h{e_i}_ev", p.val(f"b5h{e_i}_ev"))
        def epi_y(mo, tc, ps, mwid, e_i=e_i):
            p.wait("scalar", "mm", p.val("mm"))
            p.wait("scalar", "b5_yst", p.val("b5_yst"))
            p.do("scalar", lambda e, mo=mo: e.activation(
                out=ev32[:], in_=ps[:], func=AF.Identity,
                bias=b2v[:, e_i, mo:mo + 1]), inc="ac")
            p.wait("tensor", "ac", p.val("ac"))
            p.wait("tensor", "b5_ytev", p.val("b5_ytev"))
            for j in range(4):
                p.do("tensor", lambda e, j=j: e.transpose(
                    trpf[:, 0:128], ev32[:, j * 128:(j + 1) * 128], identf[:]), inc="mm2")
                p.wait("vector", "mm2", p.val("mm2"))
                p.wait("vector", "b5_yst", p.val("b5_yst"))
                p.do("vector", lambda e, j=j: e.tensor_copy(
                    evt[:, j * 128:(j + 1) * 128], trpf[:, 0:128]), inc="b5_ytev")
                p.wait("tensor", "b5_ytev", p.val("b5_ytev"))
            p.wait("sync", "b5_ytev", p.val("b5_ytev"))
            for j in range(4):
                p.do("sync", lambda e, e_i=e_i, mo=mo, tc=tc, j=j: e.dma_start(
                    out=ydr[e_i * MOE_C + tc * TC + j * 128: e_i * MOE_C + tc * TC + (j + 1) * 128,
                            mo * 128:(mo + 1) * 128],
                    in_=evt[:, j * 128:(j + 1) * 128]), inc="b5_yst", n=16)
            p.do("sync", lambda e: e.nop(), inc=f"b5y{e_i}_ev")
        if B5_DO_Y:
            emit_proj(p, f"b5y{e_i}", hfm, 32, p.drams["w_moe_2"], D, epi_y,
                      tc_list=list(range(MOE_C // TC)), banks=shared_banks,
                      bank_state=shared_state, wbufs=wb_y, w_row0=e_i * 4096)
    p.wait("sync", "b5_yst", p.val("b5_yst"))


def st_moe_gather(p):
    identf = load_const(p, "b6_idf", "c_ident_f32", [128, 128])
    dsti = load_const(p, "b6_di", "r_dest", [128, 32], I32)
    gatec = load_const(p, "b6_gv", "r_gatek", [128, 32])
    wait_consts(p, "tensor", "vector", "gpsimd")
    zrow = p.sbuf("b6_z", [1, D], F32)
    p.do("vector", lambda e: e.memset(zrow[:], 0.0), inc="ve")
    p.wait("sync", "ve", p.val("ve"))
    p.do("sync", lambda e: e.dma_start(
        out=p.drams["moe_y"][8 * MOE_C:8 * MOE_C + 1, :], in_=zrow[:]), inc="b6_zs", n=16)
    p.wait("gpsimd", "b6_zs", p.val("b6_zs"))
    mo_t = p.sbuf("b6_mt", [128, D], F32)
    y1t = p.sbuf("b6_y1", [128, NCH, 128], F32)
    osum = p.sbuf("b6_os", [128, NCH, 128], F32)
    trpf = p.psum("b6_trp", [128, TC], F32)
    ydr, y2 = p.drams["moe_y"], p.drams["y2pre"]
    for tt in range(32):
        p.wait("gpsimd", "b6_mtc", p.val("b6_mtc"))
        p.do("gpsimd", lambda e, tt=tt: e.indirect_dma_start(
            out=mo_t[:], out_offset=None, in_=ydr[:],
            in_offset=IndirectOffsetOnAxis(dsti[:, tt:tt + 1], 0)), inc="b6_g", n=16)
        p.wait("sync", "ve", p.val("ve"))
        p.do("sync", lambda e, tt=tt: e.dma_start(
            out=y1t[:], in_=p.drams["y1_f32"][:, tt * 128:(tt + 1) * 128]
                .rearrange("(c p) t -> p c t", p=128)), inc="b6_yld", n=16)
        p.wait("vector", "b6_g", p.val("b6_g"))
        p.do("vector", lambda e, tt=tt: e.tensor_scalar_mul(
            mo_t[:], mo_t[:], gatec[:, tt:tt + 1]), inc="vb")
        p.wait("tensor", "vb", p.val("vb"))
        for k in range(NCH):
            p.wait("tensor", "b6_tev", p.val("b6_tev"))
            p.do("tensor", lambda e, k=k: e.transpose(
                trpf[:, 0:128], mo_t[:, k * 128:(k + 1) * 128], identf[:]), inc="mm2")
            p.wait("vector", "mm2", p.val("mm2"))
            if k == 0:
                p.wait("vector", "b6_yld", p.val("b6_yld"))
                p.wait("vector", "b6_st", p.val("b6_st"))
            p.do("vector", lambda e, k=k: e.tensor_add(
                osum[:, k, :], y1t[:, k, :], trpf[:, 0:128]),
                inc="b6_tev" if k < NCH - 1 else "ve")
            if k == NCH - 1:
                p.do("vector", lambda e: e.engine_nop(), inc="b6_tev")
            p.wait("tensor", "b6_tev", p.val("b6_tev"))
        p.do("vector", lambda e: e.engine_nop(), inc="b6_mtc")
        p.wait("sync", "ve", p.val("ve"))
        p.do("sync", lambda e, tt=tt: e.dma_start(
            out=y2[:, tt * 128:(tt + 1) * 128].rearrange("(c p) t -> p c t", p=128),
            in_=osum[:]), inc="b6_st", n=16)
    p.wait("sync", "b6_st", p.val("b6_st"))


def st_combine(p, tag, x_name, s_name, g_name, out_name):
    xt = p.sbuf(f"{tag}_x", [128, NCH, TC], F32)
    st_ = p.sbuf(f"{tag}_s", [128, NCH, TC], F32)
    gt = p.sbuf(f"{tag}_g", [128, NCH, TC], F32)
    ot = p.sbuf(f"{tag}_o", [128, NCH, TC], F32)
    stbase = p.val(f"{tag}_st")
    for tc in range(NTC):
        if tc >= 1:
            p.wait("sync", f"{tag}_st", stbase + 16 * tc)
        for nm_, tile in ((x_name, xt), (s_name, st_), (g_name, gt)):
            p.do("sync", lambda e, nm_=nm_, tile=tile, tc=tc: e.dma_start(
                out=tile[:], in_=p.drams[nm_][:, tc * TC:(tc + 1) * TC]
                    .rearrange("(c p) t -> p c t", p=128)), inc=f"{tag}_ld", n=16)
        p.wait("vector", f"{tag}_ld", p.val(f"{tag}_ld"))
        if tc >= 1:
            p.wait("vector", f"{tag}_st", stbase + 16 * tc)
        for k in range(NCH):
            p.do("vector", lambda e, k=k: e.tensor_mul(ot[:, k, :], st_[:, k, :], gt[:, k, :]))
            p.do("vector", lambda e, k=k: e.tensor_add(ot[:, k, :], ot[:, k, :], xt[:, k, :]),
                 inc=("ve" if k == NCH - 1 else None))
        p.wait("sync", "ve", p.val("ve"))
        p.do("sync", lambda e, tc=tc: e.dma_start(
            out=p.drams[out_name][:, tc * TC:(tc + 1) * TC]
                .rearrange("(c p) t -> p c t", p=128), in_=ot[:]), inc=f"{tag}_st", n=16)
    p.wait("sync", f"{tag}_st", p.val(f"{tag}_st"))



FF2 = 2816


def st_l1_kv(p):
    x_sb, ldc = st_load_stream(p, "d2kv", p.drams["xn1_bf"])
    p.wait("tensor", "d2kv_xld", ldc)
    ident = load_const(p, "d2kv_id", "c_ident_bf", [128, 128], BF16)
    wait_consts(p, "tensor", "scalar")
    ev = p.sbuf("d2kv_ev", [128, TC], BF16)
    evt = p.sbuf("d2kv_evt", [128, TC], BF16)
    psb = p.psum("d2kv_trp", [128, 1024], BF16)

    def epi_k(mo, tc, ps, mwid):
        p.wait("scalar", "mm", p.val("mm"))
        p.wait("scalar", "d2kvst", p.val("d2kvst"))
        p.do("scalar", lambda e: e.activation(out=ev[:], in_=ps[:], func=AF.Identity),
             inc="ac")
        p.wait("sync", "ac", p.val("ac"))
        p.do("sync", lambda e, tc=tc: e.dma_start(
            out=p.drams["l1_k"][:, tc * TC:(tc + 1) * TC], in_=ev[:]), inc="d2kvst", n=16)
        p.do("sync", lambda e: e.nop(), inc="d2kvk_ev")

    def epi_v(mo, tc, ps, mwid):
        p.wait("scalar", "mm", p.val("mm"))
        p.wait("scalar", "d2kvst", p.val("d2kvst"))
        p.do("scalar", lambda e: e.activation(out=ev[:], in_=ps[:], func=AF.Identity),
             inc="ac")
        p.wait("tensor", "ac", p.val("ac"))
        p.wait("tensor", "ve2", p.val("ve2"))
        for j in range(4):
            p.do("tensor", lambda e, j=j: e.transpose(
                psb[:, j * 128:(j + 1) * 128], ev[:, j * 128:(j + 1) * 128], ident[:]),
                inc=("mm2" if j == 3 else None))
        p.wait("vector", "mm2", p.val("mm2"))
        p.wait("vector", "d2kvst", p.val("d2kvst"))
        for j in range(4):
            p.do("vector", lambda e, j=j: e.tensor_copy(
                evt[:, j * 128:(j + 1) * 128], psb[:, j * 128:(j + 1) * 128]),
                inc=("ve2" if j == 3 else None))
        p.wait("sync", "ve2", p.val("ve2"))
        for j in range(4):
            p.do("sync", lambda e, tc=tc, j=j: e.dma_start(
                out=p.drams["l1_v"][0, tc * TC + j * 128:tc * TC + (j + 1) * 128, :],
                in_=evt[:, j * 128:(j + 1) * 128]), inc="d2kvst", n=16)
        p.do("sync", lambda e: e.nop(), inc="d2kvv_ev")

    emit_proj(p, "d2kvk", x_sb, NCH, p.drams["w_l1a_k"], 128, epi_k)
    emit_proj(p, "d2kvv", x_sb, NCH, p.drams["w_l1a_v"], 128, epi_v)
    p.wait("sync", "d2kvst", p.val("d2kvst"))


def st_l1f_qkv(p, i, x_bf_name):
    x_sb, ldc = st_load_stream(p, "ea", p.drams[x_bf_name], shift_half=True)
    p.wait("tensor", "ea_xld", ldc)
    ident = load_const(p, "ea_id", "c_ident_bf", [128, 128], BF16)
    hmask = load_const(p, "ea_hm", "c_halfmask", [128, 2])
    hsel = load_const(p, "ea_hs", "c_halfsel", [2, 128])
    qsl = p.sbuf("ea_qs", [128, 1], F32)
    ksl = p.sbuf("ea_ks", [128, 1], F32)
    p.do("sync", lambda e: e.dma_start(
        out=qsl[:], in_=p.drams["s_l1f_q"][i * 128:(i + 1) * 128, :]), inc="cld", n=16)
    p.do("sync", lambda e: e.dma_start(
        out=ksl[:], in_=p.drams["s_l1f_k"][i * 128:(i + 1) * 128, :]), inc="cld", n=16)
    tabs = p.sbuf("ea_tab", [128, 4, T], F32)
    for ti, nm_ in enumerate(["c_rot_cs", "c_rot_ss", "c_rot_ci", "c_rot_si"]):
        p.do("sync", lambda e, ti=ti, nm_=nm_: e.dma_start(
            out=tabs[:, ti, :], in_=p.drams[nm_][:]), inc="cld", n=16)
    wait_consts(p, "vector", "tensor", "scalar")
    qf = p.sbuf("ea_qf", [128, TC], F32)
    q2 = p.sbuf("ea_q2", [128, TC], F32)
    r2 = p.sbuf("ea_r2", [2, TC], F32)
    t1 = p.sbuf("ea_t1", [128, TC], F32)
    rot = p.sbuf("ea_rot", [128, TC], F32)
    m1 = p.sbuf("ea_m1", [128, TC], F32)
    ev = p.sbuf("ea_ev", [128, TC], BF16)
    evt = p.sbuf("ea_evt", [128, TC], BF16)
    ps2 = p.psum("ea_ps2", [2, TC], F32)
    psb128 = p.psum("ea_psb", [128, TC], F32)
    psbt = p.psum("ea_trp", [128, 1024], BF16)

    def epi(mo, tc, ps, mwid):
        grp, mo8 = mo // 8, mo % 8
        p.wait("scalar", "mm", p.val("mm"))
        if grp == 2:
            p.wait("scalar", "east", p.val("east"))
            p.do("scalar", lambda e: e.activation(out=ev[:], in_=ps[:], func=AF.Identity),
                 inc="ea_ev")
            p.wait("tensor", "ea_ev", p.val("ea_ev"))
            p.wait("tensor", "ve2", p.val("ve2"))
            for j in range(4):
                p.do("tensor", lambda e, j=j: e.transpose(
                    psbt[:, j * 128:(j + 1) * 128], ev[:, j * 128:(j + 1) * 128], ident[:]),
                    inc=("mm2" if j == 3 else None))
            p.wait("vector", "mm2", p.val("mm2"))
            p.wait("vector", "east", p.val("east"))
            for j in range(4):
                p.do("vector", lambda e, j=j: e.tensor_copy(
                    evt[:, j * 128:(j + 1) * 128], psbt[:, j * 128:(j + 1) * 128]),
                    inc=("ve2" if j == 3 else None))
            p.wait("sync", "ve2", p.val("ve2"))
            for j in range(4):
                for hh in range(2):
                    p.do("sync", lambda e, mo8=mo8, tc=tc, j=j, hh=hh: e.dma_start(
                        out=p.drams["l1f_v"][2 * mo8 + hh,
                                             tc * TC + j * 128:tc * TC + (j + 1) * 128, :],
                        in_=evt[:, j * 128 + hh * 64:j * 128 + hh * 64 + 64]),
                        inc="east", n=16)
            return
        p.wait("scalar", "eaqf", p.val("eaqf"))
        p.do("scalar", lambda e: e.activation(out=qf[:], in_=ps[:], func=AF.Identity),
             inc="ea_ev")
        p.do("scalar", lambda e: e.activation(out=q2[:], in_=qf[:], func=AF.Square), inc="ac")
        p.wait("tensor", "ac", p.val("ac"))
        p.wait("tensor", "vb", p.val("vb"))
        p.do("tensor", lambda e: e.matmul(ps2[:], hmask[:], q2[:], start=True, stop=True),
             inc="mm2")
        p.wait("scalar", "mm2", p.val("mm2"))
        p.do("scalar", lambda e: e.activation(out=r2[:], in_=ps2[:], func=AF.Sqrt), inc="ac")
        p.wait("vector", "ac", p.val("ac"))
        p.do("vector", lambda e: e.tensor_scalar_max(r2[:], r2[:], 1e-12), inc="vb")
        p.wait("vector", "vb", p.val("vb"))
        p.do("vector", lambda e: e.reciprocal(r2[:], r2[:]), inc="vb")
        p.wait("tensor", "vb", p.val("vb"))
        p.wait("tensor", "eaqf", p.val("eaqf"))
        p.do("tensor", lambda e: e.matmul(psb128[:], hsel[:], r2[:], start=True, stop=True),
             inc="mm2")
        p.wait("vector", "mm2", p.val("mm2"))
        p.do("vector", lambda e: e.tensor_mul(t1[:], qf[:], psb128[:]), inc="vb")
        p.wait("vector", "vb", p.val("vb"))
        sv = qsl if grp == 0 else ksl
        p.do("vector", lambda e, sv=sv: e.tensor_scalar_mul(t1[:], t1[:], sv[:]), inc="eaqf")
        p.wait("vector", "eaqf", p.val("eaqf"))
        for (a, b_) in ((0, 32), (32, 0), (64, 96), (96, 64)):
            p.do("vector", lambda e, a=a, b_=b_: e.tensor_copy(
                rot[a:a + 32, :], t1[b_:b_ + 32, :]))
        tq = 0 if grp == 0 else 2
        p.do("vector", lambda e, tq=tq, tc=tc: e.tensor_mul(
            m1[:], t1[:], tabs[:, tq, tc * TC:(tc + 1) * TC]), inc="vb")
        p.wait("vector", "vb", p.val("vb"))
        p.do("vector", lambda e, tq=tq, tc=tc: e.tensor_mul(
            rot[:], rot[:], tabs[:, tq + 1, tc * TC:(tc + 1) * TC]), inc="vb")
        p.wait("vector", "vb", p.val("vb"))
        p.wait("vector", "east", p.val("east"))
        p.do("vector", lambda e: e.tensor_add(m1[:], m1[:], rot[:]), inc="vb")
        p.wait("vector", "vb", p.val("vb"))
        p.do("vector", lambda e: e.tensor_copy(ev[:], m1[:]), inc="ve2x")
        p.wait("sync", "ve2x", p.val("ve2x"))
        dst = p.drams["l1f_q"] if grp == 0 else p.drams["l1f_k"]
        p.do("sync", lambda e, dst=dst, mo8=mo8, tc=tc: e.dma_start(
            out=dst[mo8 * 128:(mo8 + 1) * 128, tc * TC:(tc + 1) * TC], in_=ev[:]),
            inc="east", n=16)

    emit_proj(p, "ea", x_sb, NCH, p.drams["w_l1f_qkv"], 3 * D, epi, w_row0=i * D)
    p.wait("sync", "east", p.val("east"))
    p.wait("sync", "ea_ev", p.val("ea_ev"))


def st_l1f_lattn(p):
    ident = load_const(p, "eb_id", "c_ident_bf", [128, 128], BF16)
    lmask = p.sbuf("eb_lm", [128, 4, 512], F32)
    p.do("sync", lambda e: e.dma_start(
        out=lmask[:], in_=p.drams["c_lmask"].rearrange("v p s -> p v s")), inc="cld", n=16)
    wait_consts(p, "tensor", "vector")
    q_sb = p.sbuf("eb_q", [64, T], BF16)
    k_sb = p.sbuf("eb_k", [64, T], BF16)
    v_sb = p.sbuf("eb_v", [128, 32, 65], BF16)
    S_sb = p.sbuf("eb_S", [128, 512], F32)
    P_sb = p.sbuf("eb_P", [128, 512], BF16)
    PT_sb = p.sbuf("eb_pt", [128, 2, 128], BF16)
    AVc = p.sbuf("eb_avc", [128, 65], F32)
    O32 = p.sbuf("eb_o32", [128, 64], F32)
    Obf = p.sbuf("eb_obf", [128, 64], BF16)
    OT_sb = p.sbuf("eb_ot", [64, 128], BF16)
    nm = p.sbuf("eb_nm", [128, 1], F32)
    nmr = p.sbuf("eb_nmr", [128, 1], F32)
    S_ps = p.psum("eb_sp", [128, TC], F32)
    PT_ps = [p.psum(f"eb_ptp{x}", [128, 1024], BF16) for x in range(2)]
    AV_ps = p.psum("eb_av", [128, TC], F32)
    OT_ps = p.psum("eb_otp", [128, 1024], BF16)
    o_fm = p.drams["l1f_o"]
    hdbase = p.val("eb_hd")
    for h in range(16):
        if h > 0:
            p.wait("sync", "eb_hd", hdbase + h)
        p.do("sync", lambda e, h=h: e.dma_start(
            out=q_sb[:], in_=p.drams["l1f_q"][h * 64:(h + 1) * 64, :]), inc="eb_ld", n=16)
        p.do("sync", lambda e, h=h: e.dma_start(
            out=k_sb[:], in_=p.drams["l1f_k"][h * 64:(h + 1) * 64, :]), inc="eb_ld", n=16)
        p.do("sync", lambda e, h=h: e.dma_start(
            out=v_sb[:, :, 0:64],
            in_=p.drams["l1f_v"][h].rearrange("(tt p) d -> p tt d", p=128)), inc="eb_ld", n=16)
        p.wait("vector", "eb_ld", p.val("eb_ld"))
        p.do("vector", lambda e: e.memset(v_sb[:, :, 64:65], 1.0), inc="ve")
        p.wait("tensor", "eb_ld", p.val("eb_ld"))
        p.wait("tensor", "ve", p.val("ve"))
        for w in range(16):
            for qsub in range(2):
                q0c = w * 256 + qsub * 128
                win0 = (w == 0)
                s_lo = 256 if win0 else 0
                kt0 = (w * 256 - 256) + s_lo
                nsl = 512 - s_lo
                p.wait("tensor", "ac", p.val("ac"))
                p.do("tensor", lambda e, q0c=q0c, kt0=kt0, nsl=nsl: e.matmul(
                    S_ps[:, 0:nsl], q_sb[:, q0c:q0c + 128], k_sb[:, kt0:kt0 + nsl],
                    start=True, stop=True), inc="mm")
                p.wait("scalar", "mm", p.val("mm"))
                p.wait("scalar", "vb", p.val("vb"))
                p.do("scalar", lambda e, s_lo=s_lo, nsl=nsl: e.activation(
                    out=S_sb[:, s_lo:512], in_=S_ps[:, 0:nsl], func=AF.Identity,
                    scale=8.0), inc="ac")
                p.wait("vector", "ac", p.val("ac"))
                mv = qsub * 2 + (1 if win0 else 0)
                p.do("vector", lambda e, mv=mv, s_lo=s_lo: e.tensor_add(
                    S_sb[:, s_lo:512], S_sb[:, s_lo:512], lmask[:, mv, s_lo:512]))
                p.do("vector", lambda e, s_lo=s_lo: e.reduce_max(
                    nm[:], S_sb[:, s_lo:512], axis=AX.X, negate=True), inc="vb")
                p.wait("scalar", "vb", p.val("vb"))
                p.wait("scalar", "mm3", p.val("mm3"))
                p.do("scalar", lambda e, s_lo=s_lo: e.activation(
                    out=P_sb[:, s_lo:512], in_=S_sb[:, s_lo:512], func=AF.Exp,
                    bias=nm[:]), inc="ac")
                p.wait("tensor", "ac", p.val("ac"))
                jlist = list(range(s_lo // 128, 4))
                for ji, j in enumerate(jlist):
                    tb = j % 2
                    if ji >= 2:
                        p.wait("tensor", "ve2", p.val("ve2"))
                    p.do("tensor", lambda e, j=j, tb=tb: e.transpose(
                        PT_ps[tb][:, 0:128], P_sb[:, j * 128:(j + 1) * 128], ident[:]),
                        inc="mm2")
                    p.wait("vector", "mm2", p.val("mm2"))
                    p.do("vector", lambda e, tb=tb: e.tensor_copy(
                        PT_sb[:, tb, :], PT_ps[tb][:, 0:128]), inc="ve2")
                    p.wait("tensor", "ve2", p.val("ve2"))
                    vtile = 2 * w - 2 + j
                    p.do("tensor", lambda e, tb=tb, vtile=vtile, ji=ji, last=(ji == len(jlist) - 1): e.matmul(
                        AV_ps[:, 0:65], PT_sb[:, tb, :], v_sb[:, vtile, :],
                        start=(ji == 0), stop=last),
                        inc=("mm3" if ji == len(jlist) - 1 else None))
                p.wait("vector", "mm3", p.val("mm3"))
                p.do("vector", lambda e: e.tensor_copy(AVc[:], AV_ps[:, 0:65]), inc="vb")
                p.wait("vector", "vb", p.val("vb"))
                p.do("vector", lambda e: e.reciprocal(nmr[:], AVc[:, 64:65]), inc="vb")
                p.wait("vector", "vb", p.val("vb"))
                p.do("vector", lambda e: e.tensor_scalar_mul(O32[:], AVc[:, 0:64], nmr[:]), inc="vb")
                p.wait("vector", "vb", p.val("vb"))
                p.do("vector", lambda e: e.tensor_copy(Obf[:], O32[:]), inc="ve")
                p.wait("tensor", "ve", p.val("ve"))
                p.do("tensor", lambda e: e.transpose(OT_ps[0:64, 0:128], Obf[:], ident[:]),
                     inc="mm2")
                p.wait("vector", "mm2", p.val("mm2"))
                p.wait("vector", "eb_st", p.val("eb_st"))
                p.do("vector", lambda e: e.tensor_copy(OT_sb[:], OT_ps[0:64, 0:128]), inc="ve3")
                p.wait("sync", "ve3", p.val("ve3"))
                p.do("sync", lambda e, h=h, q0c=q0c: e.dma_start(
                    out=o_fm[h * 64:(h + 1) * 64, q0c:q0c + 128], in_=OT_sb[:]),
                    inc="eb_st", n=16)
        p.do("sync", lambda e: e.nop(), inc="eb_hd")
    p.wait("sync", "eb_st", p.val("eb_st"))


def st_l1f_ffn(p, i, xn_name, res_name, out_f32_name, out_bf_name):
    w1a = p.sbuf("ee_w1a", [128, NCH, FF2], BF16)
    w1g = p.sbuf("ee_w1g", [128, NCH, FF2], BF16)
    w2 = p.sbuf("ee_w2", [128, FF2 // 128, 1024], BF16)
    p.do("gpsimd", lambda e: e.dma_start(
        out=w1a[:], in_=p.drams["w_l1f_1a"][i * D:(i + 1) * D, :]
            .rearrange("(c k) m -> k c m", k=128)), inc="ee_wld", n=16)
    p.do("gpsimd", lambda e: e.dma_start(
        out=w1g[:], in_=p.drams["w_l1f_1g"][i * D:(i + 1) * D, :]
            .rearrange("(c k) m -> k c m", k=128)), inc="ee_wld", n=16)
    p.do("gpsimd", lambda e: e.dma_start(
        out=w2[:], in_=p.drams["w_l1f_2p"][i * FF2:(i + 1) * FF2, :]
            .rearrange("(c k) m -> k c m", k=128)), inc="ee_wld", n=16)
    b1a = load_const(p, "ee_b1a", "b_l1f_1a", [128, 22], rows=(i * 128, (i + 1) * 128))
    b1g = load_const(p, "ee_b1g", "b_l1f_1g", [128, 22], rows=(i * 128, (i + 1) * 128))
    b2 = load_const(p, "ee_b2", "b_l1f_2", [128, 8], rows=(i * 128, (i + 1) * 128))
    wait_consts(p, "scalar")
    p.wait("tensor", "ee_wld", p.val("ee_wld"))
    xt = [p.sbuf(f"ee_x{x}", [128, NCH, TC], BF16) for x in range(2)]
    hh = p.sbuf("ee_hh", [128, FF2 // 128, TC], BF16)
    qa = p.sbuf("ee_qa", [128, TC], F32)
    gg = p.sbuf("ee_gg", [128, TC], F32)
    yv = p.sbuf("ee_yv", [128, TC], F32)
    ybf = p.sbuf("ee_ybf", [128, TC], BF16)
    rest = p.sbuf("ee_res", [128, TC], F32)
    psA = p.psum("ee_psA", [128, TC], F32)
    psB = p.psum("ee_psB", [128, TC], F32)
    psY = p.psum("ee_psY", [128, TC], F32)
    mm_done_tile = []
    for tc in range(NTC):
        xb = tc % 2
        if tc >= 2:
            p.wait("sync", "mm", mm_done_tile[tc - 2])
        p.do("sync", lambda e, tc=tc, xb=xb: e.dma_start(
            out=xt[xb][:], in_=p.drams[xn_name][:, tc * TC:(tc + 1) * TC]
                .rearrange("(c p) t -> p c t", p=128)), inc="ee_xld", n=16)
        p.wait("tensor", "ee_xld", p.val("ee_xld"))
        for mo in range(FF2 // 128):
            p.wait("tensor", "eeq", p.val("eeq"))
            for k in range(NCH):
                p.do("tensor", lambda e, k=k, mo=mo, xb=xb: e.matmul(
                    psA[:], w1a[:, k, mo * 128:(mo + 1) * 128], xt[xb][:, k, :],
                    start=(k == 0), stop=(k == NCH - 1)))
            for k in range(NCH):
                p.do("tensor", lambda e, k=k, mo=mo, xb=xb: e.matmul(
                    psB[:], w1g[:, k, mo * 128:(mo + 1) * 128], xt[xb][:, k, :],
                    start=(k == 0), stop=(k == NCH - 1)),
                    inc=("mm" if k == NCH - 1 else None))
            p.wait("scalar", "mm", p.val("mm"))
            p.wait("scalar", "eeh", p.val("eeh"))
            p.do("scalar", lambda e, mo=mo: e.activation(
                out=qa[:], in_=psA[:], func=AF.Identity, bias=b1a[:, mo:mo + 1]), inc="ac")
            p.do("scalar", lambda e, mo=mo: e.activation(
                out=gg[:], in_=psB[:], func=AF.Gelu, bias=b1g[:, mo:mo + 1]), inc="eeq")
            p.wait("vector", "eeq", p.val("eeq"))
            p.wait("vector", "mm", p.val("mm"))
            p.do("vector", lambda e, mo=mo: e.tensor_tensor(
                hh[:, mo, :], qa[:], gg[:], ALU.mult), inc="eeh")
        p.wait("tensor", "eeh", p.val("eeh"))
        for mo2 in range(NCH):
            p.wait("tensor", "eey", p.val("eey"))
            for k in range(FF2 // 128):
                p.do("tensor", lambda e, k=k, mo2=mo2: e.matmul(
                    psY[:], w2[:, k, mo2 * 128:(mo2 + 1) * 128], hh[:, k, :],
                    start=(k == 0), stop=(k == FF2 // 128 - 1)),
                    inc=("mm" if k == FF2 // 128 - 1 else None))
            p.do("sync", lambda e, mo2=mo2, tc=tc: e.dma_start(
                out=rest[:], in_=p.drams[res_name][mo2 * 128:(mo2 + 1) * 128,
                                                   tc * TC:(tc + 1) * TC]), inc="ee_rld", n=16)
            p.wait("scalar", "mm", p.val("mm"))
            p.wait("scalar", "eest", p.val("eest"))
            p.do("scalar", lambda e, mo2=mo2: e.activation(
                out=yv[:], in_=psY[:], func=AF.Identity, bias=b2[:, mo2:mo2 + 1]), inc="eey")
            p.wait("vector", "eey", p.val("eey"))
            p.wait("vector", "ee_rld", p.val("ee_rld"))
            p.do("vector", lambda e: e.tensor_add(yv[:], yv[:], rest[:]), inc="vb")
            p.wait("vector", "vb", p.val("vb"))
            p.do("vector", lambda e: e.tensor_copy(ybf[:], yv[:]), inc="ve")
            p.wait("sync", "ve", p.val("ve"))
            p.do("sync", lambda e, mo2=mo2, tc=tc: e.dma_start(
                out=p.drams[out_f32_name][mo2 * 128:(mo2 + 1) * 128, tc * TC:(tc + 1) * TC],
                in_=yv[:]), inc="eest", n=16)
            p.do("sync", lambda e, mo2=mo2, tc=tc: e.dma_start(
                out=p.drams[out_bf_name][mo2 * 128:(mo2 + 1) * 128, tc * TC:(tc + 1) * TC],
                in_=ybf[:]), inc="eest", n=16)
        mm_done_tile.append(p.val("mm"))
    p.wait("sync", "eest", p.val("eest"))


def st_silu_combine(p, tag, x_name, s_name, g_name, out_name):
    xt = p.sbuf(f"{tag}_x", [128, NCH, TC], F32)
    st_ = p.sbuf(f"{tag}_s", [128, NCH, TC], F32)
    gt = p.sbuf(f"{tag}_g", [128, NCH, TC], F32)
    ot = p.sbuf(f"{tag}_o", [128, NCH, TC], F32)
    stbase = p.val(f"{tag}_st")
    for tc in range(NTC):
        if tc >= 1:
            p.wait("sync", f"{tag}_st", stbase + 16 * tc)
        for nm_, tile in ((x_name, xt), (s_name, st_), (g_name, gt)):
            p.do("sync", lambda e, nm_=nm_, tile=tile, tc=tc: e.dma_start(
                out=tile[:], in_=p.drams[nm_][:, tc * TC:(tc + 1) * TC]
                    .rearrange("(c p) t -> p c t", p=128)), inc=f"{tag}_ld", n=16)
        p.wait("scalar", f"{tag}_ld", p.val(f"{tag}_ld"))
        if tc >= 1:
            p.wait("scalar", f"{tag}_st", stbase + 16 * tc)
        for k in range(NCH):
            p.do("scalar", lambda e, k=k: e.activation(
                out=gt[:, k, :], in_=gt[:, k, :], func=AF.Silu),
                inc=("ac" if k == NCH - 1 else None))
        p.wait("vector", "ac", p.val("ac"))
        p.wait("vector", f"{tag}_ld", p.val(f"{tag}_ld"))
        if tc >= 1:
            p.wait("vector", f"{tag}_st", stbase + 16 * tc)
        for k in range(NCH):
            p.do("vector", lambda e, k=k: e.tensor_mul(ot[:, k, :], st_[:, k, :], gt[:, k, :]))
            p.do("vector", lambda e, k=k: e.tensor_add(ot[:, k, :], ot[:, k, :], xt[:, k, :]),
                 inc=("ve" if k == NCH - 1 else None))
        p.wait("sync", "ve", p.val("ve"))
        p.do("sync", lambda e, tc=tc: e.dma_start(
            out=p.drams[out_name][:, tc * TC:(tc + 1) * TC]
                .rearrange("(c p) t -> p c t", p=128), in_=ot[:]), inc=f"{tag}_st", n=16)
    p.wait("sync", f"{tag}_st", p.val(f"{tag}_st"))



MID_SPECS = {
    "x_fm": ([D, T], F32),
    "xn_f32": ([D, T], F32),
    "xn_bf": ([D, T], BF16),
    "l0_q": ([D, T], BF16),
    "l0_k": ([D, T], BF16),
    "l0_v": ([16, T, 64], BF16),
    "l0_o": ([D, T], BF16),
    "s_fm": ([D, T], F32),
    "q0_tm": ([T, D], BF16),
    "k0_tm": ([T, D], BF16),
    "v0_tm": ([T, D], BF16),
    "k0_all": ([NCORE, T, D], BF16),
    "v0_all": ([NCORE, T, D], BF16),
    "o0_fm": ([D, T], BF16),
    "y1pre": ([D, T], F32),
    "y1_f32": ([D, T], F32),
    "y1_bf": ([D, T], BF16),
    "y1_tm": ([T, D], BF16),
    "r_hist": ([1, 8], F32),
    "hist_all": ([NCORE, 8], F32),
    "r_loc": ([128, 32], F32),
    "r_idx": ([128, 32], F32),
    "r_gate": ([128, 32], F32),
    "r_dest": ([128, 32], I32),
    "moe_disp": ([8 * MOE_C + 1, D], BF16),
    "moe_y": ([8 * MOE_C + 1, D], F32),
    "r_gatek": ([128, 32], F32),
    "y2pre": ([D, T], F32),
    "g_fm": ([D, T], F32),
    "x1_fm": ([D, T], F32),
    "xn1_f32": ([D, T], F32),
    "xn1_bf": ([D, T], BF16),
    "l1_k": ([128, T], BF16),
    "l1_v": ([1, T, 128], BF16),
    "l1_q": ([D, T], BF16),
    "l1_o": ([D, T], BF16),
    "s1_fm": ([D, T], F32),
    "l1f_q": ([D, T], BF16),
    "l1f_k": ([D, T], BF16),
    "l1f_v": ([16, T, 64], BF16),
    "l1f_o": ([D, T], BF16),
    "l1mid_f32": ([D, T], F32),
    "l1mid_bf": ([D, T], BF16),
    "l1ln_bf": ([D, T], BF16),
    "l1x_a_f32": ([D, T], F32),
    "l1x_a_bf": ([D, T], BF16),
    "l1x_b_f32": ([D, T], F32),
    "l1x_b_bf": ([D, T], BF16),
    "out_fm": ([D, T], F32),
}

REP_MAP = {"k0_all": "k0_tm", "v0_all": "v0_tm", "hist_all": "r_hist"}

_NP2MYBIR = {
    np.dtype(np.float32): F32,
    np.dtype(bfloat16): BF16,
    np.dtype(np.int32): I32,
}


class Launch:
    def __init__(self, name, builder, ins, outs):
        self.name, self.builder, self.ins, self.outs = name, builder, ins, outs


def stage_launches():
    L = []
    L.append(Launch("a1", lambda p: st_ln(p, "a1", p.drams["x_fm"],
                                          p.drams["xn_f32"], p.drams["xn_bf"]),
                    ["x_fm"], ["xn_f32", "xn_bf"]))
    L.append(Launch("a2", st_l0_qkv,
                    ["xn_bf", "w_l0a_q", "w_l0a_k", "w_l0a_v",
                     "b_l0a_q", "b_l0a_k", "b_l0a_v", "c_ident_bf"],
                    ["l0_q", "l0_k", "l0_v"]))
    L.append(Launch("a3", lambda p: st_causal_attn(
                        p, "a3", p.drams["l0_q"], p.drams["l0_k"], p.drams["l0_v"],
                        p.drams["l0_o"], 16, 64, 0.125, "c_mask0", k_dram_rows=1024),
                    ["l0_q", "l0_k", "l0_v", "c_ident_bf", "c_mask0"], ["l0_o"]))
    L.append(Launch("a4", lambda p: st_proj_simple(
                        p, "a4", "l0_o", "w_l0a_o", D, "s_fm", "b_l0a_o", out_dtype="f32"),
                    ["l0_o", "w_l0a_o", "b_l0a_o"], ["s_fm"]))
    L.append(Launch("b1", st_l0f_qkv,
                    ["xn_bf", "w_l0f_qkv", "b_l0f_qkv", "c_ident_bf"],
                    ["q0_tm", "k0_tm", "v0_tm"]))
    L.append(Launch("b2", st_l0f_mha,
                    ["q0_tm", "k0_all", "v0_all", "c_ident_bf"], ["o0_fm"]))
    L.append(Launch("b2b", lambda p: st_proj_simple(
                        p, "b2b", "o0_fm", "w_l0f_o", D, "y1pre", "b_l0f_o",
                        out_dtype="f32", add_from="xn_f32"),
                    ["o0_fm", "w_l0f_o", "b_l0f_o", "xn_f32"], ["y1pre"]))
    L.append(Launch("b2c", lambda p: st_ln(p, "b2c", p.drams["y1pre"],
                                           p.drams["y1_f32"], p.drams["y1_bf"],
                                           g_name="g_l0f_1", b_name="bb_l0f_1",
                                           out_tm=p.drams["y1_tm"]),
                    ["y1pre", "g_l0f_1", "bb_l0f_1", "c_ident_bf"],
                    ["y1_f32", "y1_bf", "y1_tm"]))
    L.append(Launch("b3", st_moe_route_a,
                    ["y1_bf", "w_moe_g", "c_ltri", "c_iota8"],
                    ["r_loc", "r_idx", "r_gate", "r_hist"]))
    L.append(Launch("b4", st_moe_route_b,
                    ["y1_tm", "r_loc", "r_idx", "r_gate", "hist_all", "c_coresel", "c_iota8"],
                    ["moe_disp", "r_dest", "r_gatek"]))
    L.append(Launch("b5", st_moe_ffn,
                    ["moe_disp", "w_moe_1", "w_moe_2", "b_moe_1", "b_moe_2",
                     "c_ident_bf", "c_ident_f32"],
                    ["moe_y"]))
    L.append(Launch("b6", st_moe_gather,
                    ["moe_y", "y1_f32", "r_dest", "r_gatek", "c_ident_f32"],
                    ["y2pre"]))
    L.append(Launch("b7", lambda p: st_ln(p, "b7", p.drams["y2pre"],
                                          p.drams["g_fm"], None,
                                          g_name="g_l0f_2", b_name="bb_l0f_2",
                                          post_act=AF.Silu),
                    ["y2pre", "g_l0f_2", "bb_l0f_2"], ["g_fm"]))
    L.append(Launch("c1", lambda p: st_combine(p, "c1", "x_fm", "s_fm", "g_fm", "x1_fm"),
                    ["x_fm", "s_fm", "g_fm"], ["x1_fm"]))
    L.append(Launch("d1", lambda p: st_ln(p, "d1", p.drams["x1_fm"],
                                          p.drams["xn1_f32"], p.drams["xn1_bf"]),
                    ["x1_fm"], ["xn1_f32", "xn1_bf"]))
    L.append(Launch("d2kv", st_l1_kv,
                    ["xn1_bf", "w_l1a_k", "w_l1a_v", "c_ident_bf"], ["l1_k", "l1_v"]))
    L.append(Launch("d2q", lambda p: st_proj_simple(
                        p, "d2q", "xn1_bf", "w_l1a_q", D, "l1_q"),
                    ["xn1_bf", "w_l1a_q"], ["l1_q"]))
    L.append(Launch("d2c", lambda p: st_causal_attn(
                        p, "d2c", p.drams["l1_q"], p.drams["l1_k"], p.drams["l1_v"],
                        p.drams["l1_o"], 8, 128, 1.0 / 32.0, "c_maskinf", k_dram_rows=128),
                    ["l1_q", "l1_k", "l1_v", "c_ident_bf", "c_maskinf"], ["l1_o"]))
    L.append(Launch("d2d", lambda p: st_proj_simple(
                        p, "d2d", "l1_o", "w_l1a_p", D, "s1_fm", "b_l1a_p", out_dtype="f32"),
                    ["l1_o", "w_l1a_p", "b_l1a_p"], ["s1_fm"]))
    lat_ins = ["l1f_q", "l1f_k", "l1f_v", "c_ident_bf", "c_lmask"]
    qkv_ins = ["w_l1f_qkv", "s_l1f_q", "s_l1f_k", "c_ident_bf", "c_halfmask", "c_halfsel",
               "c_rot_cs", "c_rot_ss", "c_rot_ci", "c_rot_si"]
    for i in range(6):
        xin_f32 = "xn1_f32" if i == 0 else (f"l1x_{'a' if i % 2 == 1 else 'b'}_f32")
        xin_bf = "xn1_bf" if i == 0 else (f"l1x_{'a' if i % 2 == 1 else 'b'}_bf")
        xout_f32 = f"l1x_{'a' if i % 2 == 0 else 'b'}_f32"
        xout_bf = f"l1x_{'a' if i % 2 == 0 else 'b'}_bf"
        L.append(Launch(f"e{i}a", lambda p, i=i, xb=xin_bf: st_l1f_qkv(p, i, xb),
                        [xin_bf] + qkv_ins, ["l1f_q", "l1f_k", "l1f_v"]))
        L.append(Launch(f"e{i}b", st_l1f_lattn, lat_ins, ["l1f_o"]))
        L.append(Launch(f"e{i}c", lambda p, i=i, xf=xin_f32: st_proj_simple(
                            p, "ec", "l1f_o", "w_l1f_o", D, "l1mid_f32",
                            out_dtype="f32", add_from=xf, w_row0=i * D,
                            also_bf16="l1mid_bf"),
                        ["l1f_o", "w_l1f_o", xin_f32], ["l1mid_f32", "l1mid_bf"]))
        L.append(Launch(f"e{i}d", lambda p, i=i: st_ln(
                            p, "ed", p.drams["l1mid_f32"], None, p.drams["l1ln_bf"],
                            g_name="g_l1f", b_name="bb_l1f",
                            g_rows=(i * 128, (i + 1) * 128), shift_half=True),
                        ["l1mid_f32", "g_l1f", "bb_l1f"], ["l1ln_bf"]))
        L.append(Launch(f"e{i}e", lambda p, i=i, of=xout_f32, ob=xout_bf: st_l1f_ffn(
                            p, i, "l1ln_bf", "l1mid_f32", of, ob),
                        ["l1ln_bf", "l1mid_f32", "w_l1f_1a", "w_l1f_1g", "w_l1f_2p",
                         "b_l1f_1a", "b_l1f_1g", "b_l1f_2"],
                        [xout_f32, xout_bf]))
    L.append(Launch("c2", lambda p: st_silu_combine(
                        p, "c2", "x1_fm", "s1_fm", "l1x_b_f32", "out_fm"),
                    ["x1_fm", "s1_fm", "l1x_b_f32"], ["out_fm"]))
    return L


def run_pipeline(inputs, upto=None, mids_init=None):
    C = host_constants()
    W = host_weights(inputs)
    x = np.asarray(inputs["L0_attn_x"], dtype=np.float32)
    mids = {"x_fm": [np.ascontiguousarray(x[c].T) for c in range(NCORE)]}
    if mids_init:
        mids.update(mids_init)
    static = {}
    static.update(C)
    static.update(W)
    percore = {"c_coresel": []}
    for c in range(NCORE):
        sel = np.zeros((8, 1), np.float32)
        sel[:c, 0] = 1.0
        percore["c_coresel"].append(sel)
    for launch in stage_launches():
        if upto is not None and launch.name not in upto:
            continue
        p = Prog()
        for name in launch.ins:
            if name in static:
                a = static[name]
                p.dram(name, a.shape, _NP2MYBIR[a.dtype], kind="ExternalInput")
            elif name in percore:
                a = percore[name][0]
                p.dram(name, a.shape, _NP2MYBIR[a.dtype], kind="ExternalInput")
            else:
                shape, dt = MID_SPECS[name]
                p.dram(name, shape, dt, kind="ExternalInput")
        for name in launch.outs:
            shape, dt = MID_SPECS[name]
            p.dram(name, shape, dt, kind="ExternalOutput")
        launch.builder(p)
        nc = p.finalize()
        in_maps = []
        for c in range(NCORE):
            m = {}
            for name in launch.ins:
                if name in static:
                    m[name] = static[name]
                elif name in percore:
                    m[name] = percore[name][c]
                elif name in REP_MAP:
                    shape, dt = MID_SPECS[name]
                    npdt = {F32: np.float32, BF16: bfloat16, I32: np.int32}[dt]
                    m[name] = np.concatenate(
                        [np.asarray(a).reshape(1, -1) for a in mids[REP_MAP[name]]], 0
                    ).reshape(shape).astype(npdt, copy=False)
                else:
                    m[name] = mids[name][c]
            in_maps.append(m)
        res = run_bass_kernel_spmd(nc, in_maps, core_ids=list(range(NCORE)))
        for name in launch.outs:
            mids[name] = [res.results[c][name] for c in range(NCORE)]
    return mids


def emit_allgather(p, pairs):
    for src_name, dst_name in pairs:
        sap = p.drams[src_name][:]
        dap = p.drams[dst_name]
        if len(dap.shape) == 3:
            dap = dap.rearrange("m t d -> (m t) d")
        else:
            dap = dap[:]
        p.do("gpsimd", lambda e, sap=sap, dap=dap: e.collective_compute(
            "AllGather", ALU.bypass, replica_groups=[list(range(NCORE))],
            ins=[sap.opt()], outs=[dap.opt()]), inc="cc")
    p.wait("gpsimd", "cc", p.val("cc"))
    p.do("gpsimd", lambda e: e.nop(), inc="stg")
    v = p.val("stg")
    for eng in ENGINES:
        p.wait(eng, "stg", v)


GATHER_AFTER = {"b1": [("k0_tm", "k0_all"), ("v0_tm", "v0_all")],
                "b3": [("r_hist", "hist_all")]}

FUSED = True


def build_fused(static, debug_outs=()):
    p = Prog()
    for name, a in static.items():
        p.dram(name, a.shape, _NP2MYBIR[a.dtype], kind="ExternalInput")
    p.dram("c_coresel", [8, 1], F32, kind="ExternalInput")
    for name, (shape, dt) in MID_SPECS.items():
        if name == "x_fm":
            p.dram(name, shape, dt, kind="ExternalInput")
        elif name == "out_fm" or name in debug_outs:
            p.dram(name, shape, dt, kind="ExternalOutput")
        else:
            p.dram(name, shape, dt, kind="Internal")
    for launch in stage_launches():
        with p.region():
            launch.builder(p)
        p.barrier_all()
        if launch.name in GATHER_AFTER:
            emit_allgather(p, GATHER_AFTER[launch.name])
    return p.finalize()


def kernel(**inputs):
    if not FUSED:
        mids = run_pipeline(inputs)
        return np.stack([np.ascontiguousarray(np.asarray(mids["out_fm"][c], np.float32).T)
                         for c in range(NCORE)])
    C = host_constants()
    W = host_weights(inputs)
    static = {}
    static.update(C)
    static.update(W)
    nc = build_fused(static)
    x = np.asarray(inputs["L0_attn_x"], dtype=np.float32)
    in_maps = []
    for c in range(NCORE):
        sel = np.zeros((8, 1), np.float32)
        sel[:c, 0] = 1.0
        m = dict(static)
        m["c_coresel"] = sel
        m["x_fm"] = np.ascontiguousarray(x[c].T)
        in_maps.append(m)
    res = run_bass_kernel_spmd(nc, in_maps, core_ids=list(range(NCORE)))
    return np.stack([np.ascontiguousarray(np.asarray(res.results[c]["out_fm"], np.float32).T)
                     for c in range(NCORE)])


# version 04
# speedup vs baseline: 1.2235x; 1.2235x over previous
import contextlib
import numpy as np
import ml_dtypes
import concourse.bass as bass
import concourse.mybir as mybir
from concourse.bass_utils import run_bass_kernel_spmd
from concourse.bass_core_rust import IndirectOffsetOnAxis

F32 = mybir.dt.float32
BF16 = mybir.dt.bfloat16
I32 = mybir.dt.int32
AF = mybir.ActivationFunctionType
ALU = mybir.AluOpType
AX = mybir.AxisListType
bfloat16 = ml_dtypes.bfloat16

T = 4096
D = 1024
NCH = D // 128
TC = 512
NTC = T // TC
NCORE = 8
MOE_C = 1024
NEG0 = -100000.0
NEGINF = -1e30

ENGINES = ("tensor", "vector", "scalar", "sync", "gpsimd")


GLOBAL_SEMS = {"mm", "mm2", "mm3", "ve", "ve2", "ve3", "ve9", "vb", "ac", "acpa",
               "vpa", "st", "stg", "cc", "cld", "fcinit", "dbg"}


class Prog:
    def __init__(self):
        self.nc = bass.Bass("TRN2", target_bir_lowering=False)
        self.stack = contextlib.ExitStack()
        self.streams = {e: [] for e in ENGINES}
        self._sems = {}
        self._sem_counts = {}
        self._free_handles = []
        self._region_sems = []
        self._region_id = 0
        self.drams = {}
        self._regions = []
        self._name_ctr = 0

    def dram(self, name, shape, dtype, kind="Internal"):
        ap = self.nc.dram_tensor(name, list(shape), dtype, kind=kind).ap()
        self.drams[name] = ap
        return ap

    def _uname(self, name):
        self._name_ctr += 1
        return f"{name}__r{self._name_ctr}"

    def sbuf(self, name, shape, dtype):
        st = self._regions[-1] if self._regions else self.stack
        nm = self._uname(name) if self._regions else name
        return st.enter_context(self.nc.sbuf_tensor(nm, list(shape), dtype))

    def psum(self, name, shape, dtype=F32):
        st = self._regions[-1] if self._regions else self.stack
        nm = self._uname(name) if self._regions else name
        return st.enter_context(self.nc.psum_tensor(nm, list(shape), dtype))

    @contextlib.contextmanager
    def region(self):
        st = contextlib.ExitStack()
        self._regions.append(st)
        self._region_id += 1
        self._region_sems = []
        saved_fc = getattr(self, "_fconsts", {})
        self._fconsts = {}
        try:
            yield
        finally:
            self._regions.pop()
            st.close()
            self._fconsts = saved_fc
            for key in self._region_sems:
                self._free_handles.append(self._sems.pop(key))
            self._region_sems = []

    def barrier_all(self):
        self.do("sync", lambda e: e.nop(), inc="stg")
        v = self.val("stg")
        for eng in ENGINES:
            self.wait(eng, "stg", v)

    def _sem(self, name):
        key = name if (name in GLOBAL_SEMS or not self._regions) else (self._region_id, name)
        if key not in self._sems:
            if key != name and self._free_handles:
                h = self._free_handles.pop()
            else:
                h = self.stack.enter_context(self.nc.semaphore(str(key)))
                self._sem_counts[id(h)] = 0
            self._sems[key] = h
            if key != name:
                self._region_sems.append(key)
        return self._sems[key]

    def val(self, name):
        return self._sem_counts[id(self._sem(name))]

    def wait(self, engine, name, value=None):
        h = self._sem(name)
        v = self._sem_counts[id(h)] if value is None else value
        self.streams[engine].append(lambda eng, h=h, v=v: eng.wait_ge(h, v))

    def do(self, engine, fn, inc=None, n=1):
        if inc is None:
            self.streams[engine].append(fn)
        else:
            h = self._sem(inc)
            self._sem_counts[id(h)] += n
            self.streams[engine].append(
                lambda eng, fn=fn, h=h, n=n: fn(eng).then_inc(h, n))

    def fconst(self, val, dtype=F32):
        key = (val, dtype)
        if not hasattr(self, "_fconsts"):
            self._fconsts = {}
        if key not in self._fconsts:
            t = self.sbuf(f"fc_{len(self._fconsts)}", [128, 1], dtype)
            self.do("vector", lambda e, t=t, val=val: e.memset(t[:], val), inc="fcinit")
            self._fconsts[key] = t
        return self._fconsts[key]

    def finalize(self):
        with self.nc.Block() as blk:
            for e in ENGINES:
                fns = self.streams[e]
                if fns:
                    getattr(blk, e)(lambda eng, fns=fns: [f(eng) for f in fns])
        self.stack.close()
        return self.nc



def rotary_tables_np():
    DH, WINDOW = 64, 256
    f32 = np.float32
    half = np.arange(0, DH, 2, dtype=f32)
    inv_freq = (1.0 / 10000.0 ** (half / DH)).astype(f32)
    t = np.arange(T, dtype=f32)
    freqs = (t[:, None] * inv_freq[None, :]).astype(f32)
    freqs = np.concatenate([freqs, freqs], -1)
    base_scale = ((half + 0.4 * DH) / (1.4 * DH)).astype(f32)
    power = ((t - T // 2) / (WINDOW // 2)).astype(f32)
    scale = (base_scale[None, :] ** power[:, None]).astype(f32)
    scale = np.concatenate([scale, scale], -1)
    cos, sin = np.cos(freqs).astype(f32), np.sin(freqs).astype(f32)
    inv = (1.0 / scale).astype(f32)
    sgn = np.concatenate([-np.ones(32, f32), np.ones(32, f32)])
    CS = np.ascontiguousarray((cos * scale).T)
    SS = np.ascontiguousarray((sin * scale).T * sgn[:, None])
    CI = np.ascontiguousarray((cos * inv).T)
    SI = np.ascontiguousarray((sin * inv).T * sgn[:, None])
    dup = lambda A: np.ascontiguousarray(np.concatenate([A, A], axis=0))
    return dup(CS), dup(SS), dup(CI), dup(SI)


def host_constants():
    f32 = np.float32
    C = {}
    C["c_ident_bf"] = np.eye(128, dtype=bfloat16)
    C["c_ident_f32"] = np.eye(128, dtype=f32)
    C["c_ltri"] = np.triu(np.ones((128, 128), f32), 1)
    tri = np.triu(np.ones((128, 128), bool), 1)
    C["c_mask0"] = np.where(tri, f32(NEG0), f32(0)).astype(f32)
    C["c_maskinf"] = np.where(tri, f32(NEGINF), f32(0)).astype(f32)
    oi = np.arange(128)[:, None]
    s_ = np.arange(512)[None, :]
    masks = []
    for qsub in (0, 1):
        o = oi + qsub * 128
        band = (s_ < o) | (s_ > o + 256)
        masks.append(np.where(band, f32(NEGINF), f32(0)).astype(f32))
        masks.append(np.where(band | (s_ < 256), f32(NEGINF), f32(0)).astype(f32))
    C["c_lmask"] = np.ascontiguousarray(np.stack(masks))
    C["c_iota8"] = np.tile(np.arange(8, dtype=f32)[None, :], (128, 1))
    hm = np.zeros((128, 2), f32)
    hm[:64, 0] = 1.0
    hm[64:, 1] = 1.0
    C["c_halfmask"] = hm
    C["c_halfsel"] = np.ascontiguousarray(hm.T)
    CS, SS, CI, SI = rotary_tables_np()
    C["c_rot_cs"], C["c_rot_ss"], C["c_rot_ci"], C["c_rot_si"] = CS, SS, CI, SI
    return C


def vec_pm(v):
    n = v.shape[0] // 128
    return np.ascontiguousarray(v.reshape(n, 128).T)


def host_weights(I):
    W = {}
    W["w_l0a_q"] = np.ascontiguousarray(I["L0_attn_Q_w"].transpose(1, 0, 2).reshape(D, D))
    W["w_l0a_k"] = np.ascontiguousarray(I["L0_attn_K_w"].transpose(1, 0, 2).reshape(D, D))
    W["w_l0a_v"] = np.ascontiguousarray(I["L0_attn_V_w"].transpose(1, 0, 2).reshape(D, D))
    W["w_l0a_o"] = np.ascontiguousarray(I["L0_attn_O_w"].reshape(D, D))
    W["b_l0a_q"] = vec_pm(np.asarray(I["L0_attn_Q_b"]).reshape(-1))
    W["b_l0a_k"] = vec_pm(np.asarray(I["L0_attn_K_b"]).reshape(-1))
    W["b_l0a_v"] = vec_pm(np.asarray(I["L0_attn_V_b"]).reshape(-1))
    W["b_l0a_o"] = vec_pm(np.asarray(I["L0_attn_O_b"]))
    W["w_l0f_qkv"] = np.ascontiguousarray(I["L0_ffn_in_proj_w"].T)
    W["b_l0f_qkv"] = vec_pm(np.asarray(I["L0_ffn_in_proj_b"]))
    W["w_l0f_o"] = np.ascontiguousarray(I["L0_ffn_out_proj_w"].T)
    W["b_l0f_o"] = vec_pm(np.asarray(I["L0_ffn_out_proj_b"]))
    W["g_l0f_1"] = vec_pm(np.asarray(I["L0_ffn_ln1_g"]))
    W["bb_l0f_1"] = vec_pm(np.asarray(I["L0_ffn_ln1_b"]))
    W["g_l0f_2"] = vec_pm(np.asarray(I["L0_ffn_ln2_g"]))
    W["bb_l0f_2"] = vec_pm(np.asarray(I["L0_ffn_ln2_b"]))
    W["w_moe_g"] = np.ascontiguousarray(I["L0_ffn_wg"])
    W["w_moe_1"] = np.ascontiguousarray(I["L0_ffn_w1"]).reshape(8 * 1024, 4096)
    W["w_moe_2"] = np.ascontiguousarray(I["L0_ffn_w2"]).reshape(8 * 4096, 1024)
    W["b_moe_1"] = np.stack([vec_pm(np.asarray(I["L0_ffn_b1"][e])) for e in range(8)])
    W["b_moe_2"] = np.stack([vec_pm(np.asarray(I["L0_ffn_b2"][e])) for e in range(8)])
    W["w_l1a_q"] = np.ascontiguousarray(I["L1_attn_Wq"].transpose(1, 0, 2).reshape(D, D))
    W["w_l1a_k"] = np.ascontiguousarray(I["L1_attn_Wk"])
    W["w_l1a_v"] = np.ascontiguousarray(I["L1_attn_Wv"])
    W["w_l1a_p"] = np.ascontiguousarray(I["L1_attn_Wp"])
    W["b_l1a_p"] = vec_pm(np.asarray(I["L1_attn_bp"]))
    W["w_l1f_qkv"] = np.ascontiguousarray(I["L1_ffn_Wqkv"]).reshape(6 * D, 3 * D)
    W["w_l1f_o"] = np.ascontiguousarray(I["L1_ffn_Wo"]).reshape(6 * D, D)
    DI = 2730
    w1 = np.asarray(I["L1_ffn_W1"])
    w2 = np.asarray(I["L1_ffn_W2"])
    b1 = np.asarray(I["L1_ffn_b1"])
    w1a = np.zeros((6, D, FF2), np.float32); w1a[:, :, :DI] = w1[:, :, :DI]
    w1g = np.zeros((6, D, FF2), np.float32); w1g[:, :, :DI] = w1[:, :, DI:]
    w2p = np.zeros((6, FF2, D), np.float32); w2p[:, :DI, :] = w2
    b1a = np.zeros((6, FF2), np.float32); b1a[:, :DI] = b1[:, :DI]
    b1g = np.zeros((6, FF2), np.float32); b1g[:, :DI] = b1[:, DI:]
    W["w_l1f_1a"] = np.ascontiguousarray(w1a).reshape(6 * D, FF2)
    W["w_l1f_1g"] = np.ascontiguousarray(w1g).reshape(6 * D, FF2)
    W["w_l1f_2p"] = np.ascontiguousarray(w2p).reshape(6 * FF2, D)
    W["b_l1f_1a"] = np.concatenate([vec_pm(b1a[i]) for i in range(6)], 0)
    W["b_l1f_1g"] = np.concatenate([vec_pm(b1g[i]) for i in range(6)], 0)
    W["b_l1f_2"] = np.concatenate([vec_pm(np.asarray(I["L1_ffn_b2"][i])) for i in range(6)], 0)
    W["g_l1f"] = np.concatenate([vec_pm(np.asarray(I["L1_ffn_ln_g"][i])) for i in range(6)], 0)
    W["bb_l1f"] = np.concatenate([vec_pm(np.asarray(I["L1_ffn_ln_b"][i])) for i in range(6)], 0)
    qs = np.asarray(I["L1_ffn_q_scale"]); ks = np.asarray(I["L1_ffn_k_scale"])
    W["s_l1f_q"] = np.ascontiguousarray(np.concatenate([qs, qs], 1).reshape(6 * 128, 1))
    W["s_l1f_k"] = np.ascontiguousarray(np.concatenate([ks, ks], 1).reshape(6 * 128, 1))
    return W



def load_const(p, tag, dram_name, shape, dtype=F32, rows=None):
    t = p.sbuf(tag, shape, dtype)
    src_ap = p.drams[dram_name][:] if rows is None else p.drams[dram_name][rows[0]:rows[1], :]
    p.do("sync", lambda e, src_ap=src_ap: e.dma_start(out=t[:], in_=src_ap), inc="cld", n=16)
    return t


def wait_consts(p, *engines):
    for e in engines:
        p.wait(e, "cld", p.val("cld"))


def st_load_stream(p, tag, dram_bf16, kch=NCH, guard=0, ncols=T, shift_half=False):
    x = p.sbuf(f"{tag}_X", [128, kch, ncols], BF16)
    n = (ncols + TC - 1) // TC
    xbase = p.val(f"{tag}_xld")
    nld = 0
    for tc in range(n):
        w = min(TC, ncols - tc * TC)
        if not shift_half:
            p.do("sync", lambda e, tc=tc, w=w: e.dma_start(
                out=x[:, :, tc * TC:tc * TC + w],
                in_=dram_bf16[:, guard + tc * TC: guard + tc * TC + w]
                    .rearrange("(c p) t -> p c t", p=128)), inc=f"{tag}_xld", n=16)
            nld += 1
        else:
            h = kch // 2
            p.do("sync", lambda e, tc=tc, w=w, h=h: e.dma_start(
                out=x[:, 0:h, tc * TC:tc * TC + w],
                in_=dram_bf16[0:h * 128, tc * TC: tc * TC + w]
                    .rearrange("(c p) t -> p c t", p=128)), inc=f"{tag}_xld", n=16)
            if tc == 0:
                p.do("sync", lambda e, w=w, h=h: e.dma_start(
                    out=x[:, h:kch, 1:w],
                    in_=dram_bf16[h * 128:kch * 128, 0:w - 1]
                        .rearrange("(c p) t -> p c t", p=128)), inc=f"{tag}_xld", n=16)
                p.wait("vector", f"{tag}_xld", xbase + 16 * (nld + 2))
                p.do("vector", lambda e, h=h: e.memset(x[:, h:kch, 0:1], 0.0), inc=f"{tag}_shz")
            else:
                p.do("sync", lambda e, tc=tc, w=w, h=h: e.dma_start(
                    out=x[:, h:kch, tc * TC:tc * TC + w],
                    in_=dram_bf16[h * 128:kch * 128, tc * TC - 1: tc * TC + w - 1]
                        .rearrange("(c p) t -> p c t", p=128)), inc=f"{tag}_xld", n=16)
            nld += 2
    if shift_half:
        p.wait("tensor", f"{tag}_shz", p.val(f"{tag}_shz"))
    return x, p.val(f"{tag}_xld")



def emit_proj(p, tag, x_sb, kch, w_dram, nout, epilogue, n_banks=2,
              tc_list=None, x_col0=0, w_row0=0, mo_list=None,
              banks=None, bank_state=None, wbufs=None):
    n_mo = (nout + 127) // 128
    if mo_list is None:
        mo_list = list(range(n_mo))
    if tc_list is None:
        tc_list = list(range(NTC))
    if banks is None:
        banks = [p.psum(f"{tag}_pb{i}", [128, TC], F32) for i in range(n_banks)]
    n_banks = len(banks)
    if bank_state is None:
        bank_state = [None] * n_banks
    if wbufs is None:
        wbufs = [p.sbuf(f"{tag}_w{i}", [128, kch, 128], BF16) for i in range(2)]
    wbuf_mm_target = [0, 0]
    ld, evac = f"{tag}_wld", f"{tag}_ev"
    ldbase = p.val(ld)
    bi = 0
    nld = 0
    for wi, mo in enumerate(mo_list):
        mwid = min(128, nout - mo * 128)
        buf = wbufs[wi % 2]
        if wi >= 2:
            p.wait("gpsimd", "mm", wbuf_mm_target[wi % 2])
        p.do("gpsimd", lambda e, buf=buf, mo=mo, mwid=mwid: e.dma_start(
            out=buf[:, :, 0:mwid],
            in_=w_dram[w_row0:w_row0 + kch * 128, mo * 128:mo * 128 + mwid]
                .rearrange("(c k) m -> k c m", k=128)), inc=ld, n=16)
        nld += 1
        p.wait("tensor", ld, ldbase + 16 * nld)
        for tc in tc_list:
            if bank_state[bi % n_banks] is not None:
                gsem, gval = bank_state[bi % n_banks]
                p.wait("tensor", gsem, gval)
            ps = banks[bi % n_banks]
            for k in range(kch):
                p.do("tensor", lambda e, ps=ps, buf=buf, k=k, mwid=mwid, tc=tc: e.matmul(
                    ps[0:mwid, :], buf[:, k, 0:mwid],
                    x_sb[:, k, x_col0 + tc * TC: x_col0 + (tc + 1) * TC],
                    start=(k == 0), stop=(k == kch - 1)),
                    inc=("mm" if k == kch - 1 else None))
            tgt = p.val(evac) + 1
            epilogue(mo, tc, ps, mwid, bi % 2)
            assert p.val(evac) == tgt, f"{tag}: epilogue must inc {evac} exactly once"
            bank_state[bi % n_banks] = (evac, tgt)
            bi += 1
        wbuf_mm_target[wi % 2] = p.val("mm")
    return evac



def st_ln(p, tag, x_dram, out_f32, out_bf16, g_sb=None, b_sb=None,
          eps=1e-5, x_guard=0, out_guard=0, g_name=None, b_name=None,
          out_tm=None, post_act=None, shift_half=False, g_rows=None):
    if g_name is not None:
        g_sb = load_const(p, f"{tag}_g", g_name, [128, NCH], rows=g_rows)
        b_sb = load_const(p, f"{tag}_bb", b_name, [128, NCH], rows=g_rows)
        wait_consts(p, "vector")
    xt = [p.sbuf(f"{tag}_x{i}", [128, NCH, TC], F32) for i in range(2)]
    ot = p.sbuf(f"{tag}_o", [128, NCH, TC], F32)
    ob = p.sbuf(f"{tag}_ob", [128, NCH, TC], BF16)
    sq = p.sbuf(f"{tag}_sq", [128, NCH, TC], F32)
    sqs = p.sbuf(f"{tag}_sqs", [1, TC], F32)
    strow = p.sbuf(f"{tag}_st", [1, 4 * TC], F32)
    ones = p.sbuf(f"{tag}_1s", [128, 1], F32)
    onesr = p.sbuf(f"{tag}_1r", [1, 128], F32)
    p.do("vector", lambda e: e.memset(ones[:], 1.0))
    p.do("vector", lambda e: e.memset(onesr[:], 1.0), inc="ve")
    epst = p.fconst(float(eps))
    if out_tm is not None:
        ident_tm = load_const(p, f"{tag}_idtm", "c_ident_bf", [128, 128], BF16)
        wait_consts(p, "tensor")
        tm_ps = p.psum(f"{tag}_tmps", [128, 1024], BF16)
        tm_sb = p.sbuf(f"{tag}_tmsb", [128, 128], BF16)
    ps_s = p.psum(f"{tag}_pss", [1, TC], F32)
    ps_s2 = p.psum(f"{tag}_pss2", [1, TC], F32)
    ps_b = p.psum(f"{tag}_psb", [128, 2 * TC], F32)
    ldsem, stsem = f"{tag}_ld", f"{tag}_stores"
    p.wait("tensor", "ve", p.val("ve"))
    lnbase = p.val(ldsem)
    appbase = p.val(f"{tag}_app")
    nld_ln = 0
    for tc in range(NTC):
        x = xt[tc % 2]
        if tc >= 2:
            p.wait("sync", f"{tag}_app", appbase + tc - 1)
        if not shift_half:
            p.do("sync", lambda e, x=x, tc=tc: e.dma_start(
                out=x[:],
                in_=x_dram[:, x_guard + tc * TC: x_guard + (tc + 1) * TC]
                    .rearrange("(c p) t -> p c t", p=128)), inc=ldsem, n=16)
            nld_ln += 1
        else:
            hh_ = NCH // 2
            p.do("sync", lambda e, x=x, tc=tc, hh_=hh_: e.dma_start(
                out=x[:, 0:hh_, :],
                in_=x_dram[0:hh_ * 128, tc * TC:(tc + 1) * TC]
                    .rearrange("(c p) t -> p c t", p=128)), inc=ldsem, n=16)
            if tc == 0:
                p.do("sync", lambda e, x=x, hh_=hh_: e.dma_start(
                    out=x[:, hh_:NCH, 1:TC],
                    in_=x_dram[hh_ * 128:D, 0:TC - 1]
                        .rearrange("(c p) t -> p c t", p=128)), inc=ldsem, n=16)
                p.wait("vector", ldsem, lnbase + 16 * (nld_ln + 2))
                p.do("vector", lambda e, x=x, hh_=hh_: e.memset(x[:, hh_:NCH, 0:1], 0.0),
                     inc="ve")
            else:
                p.do("sync", lambda e, x=x, tc=tc, hh_=hh_: e.dma_start(
                    out=x[:, hh_:NCH, :],
                    in_=x_dram[hh_ * 128:D, tc * TC - 1:(tc + 1) * TC - 1]
                        .rearrange("(c p) t -> p c t", p=128)), inc=ldsem, n=16)
            nld_ln += 2
        p.wait("tensor", ldsem, lnbase + 16 * nld_ln)
        p.wait("tensor", "ve", p.val("ve"))
        for k in range(NCH):
            p.do("tensor", lambda e, x=x, k=k: e.matmul(
                ps_s[:], ones[:], x[:, k, :], start=(k == 0), stop=(k == NCH - 1)))
        p.wait("scalar", ldsem, lnbase + 16 * nld_ln)
        p.wait("scalar", "mm", p.val("mm"))
        for k in range(NCH):
            p.do("scalar", lambda e, x=x, k=k: e.activation(
                out=sq[:, k, :], in_=x[:, k, :], func=AF.Square),
                inc=("ac" if k == NCH - 1 else None))
        p.wait("tensor", "ac", p.val("ac"))
        for k in range(NCH):
            p.do("tensor", lambda e, k=k: e.matmul(
                ps_s2[:], ones[:], sq[:, k, :], start=(k == 0), stop=(k == NCH - 1)),
                inc=("mm" if k == NCH - 1 else None))
        p.wait("vector", "mm", p.val("mm"))
        mu, var = strow[:, 0:TC], strow[:, TC:2 * TC]
        sd, r = strow[:, 2 * TC:3 * TC], strow[:, 3 * TC:4 * TC]
        p.do("vector", lambda e: e.tensor_scalar_mul(mu, ps_s[:], 1.0 / D))
        p.do("vector", lambda e: e.tensor_scalar_mul(var, ps_s2[:], 1.0 / D))
        p.do("vector", lambda e: e.tensor_tensor(sqs[:], mu, mu, ALU.mult))
        p.do("vector", lambda e: e.tensor_sub(var, var, sqs[:]), inc="ve")
        p.wait("scalar", "ve", p.val("ve"))
        p.wait("scalar", "fcinit", p.val("fcinit"))
        p.do("scalar", lambda e: e.activation(out=sd, in_=var, func=AF.Sqrt, bias=epst[0:1, :]), inc="ac")
        p.wait("vector", "ac", p.val("ac"))
        p.do("vector", lambda e: e.reciprocal(r, sd), inc="ve")
        p.wait("tensor", "ve", p.val("ve"))
        p.do("tensor", lambda e: e.matmul(ps_b[:, 0:TC], onesr[:], mu, start=True, stop=True))
        p.do("tensor", lambda e: e.matmul(ps_b[:, TC:2 * TC], onesr[:], r, start=True, stop=True), inc="mm")
        p.wait("vector", "mm", p.val("mm"))
        if tc >= 1:
            p.wait("vector", stsem, p.val(stsem))
        for k in range(NCH):
            p.do("vector", lambda e, x=x, k=k: e.tensor_sub(ot[:, k, :], x[:, k, :], ps_b[:, 0:TC]))
            p.do("vector", lambda e, k=k: e.tensor_mul(ot[:, k, :], ot[:, k, :], ps_b[:, TC:2 * TC]))
            if g_sb is not None:
                p.do("vector", lambda e, k=k: e.tensor_scalar(
                    ot[:, k, :], ot[:, k, :], g_sb[:, k:k + 1], b_sb[:, k:k + 1], ALU.mult, ALU.add))
            if post_act is not None:
                p.do("vector", lambda e: e.engine_nop(), inc="vpa")
                p.wait("scalar", "vpa", p.val("vpa"))
                p.do("scalar", lambda e, k=k: e.activation(
                    out=ot[:, k, :], in_=ot[:, k, :], func=post_act), inc="acpa")
                p.wait("vector", "acpa", p.val("acpa"))
            p.do("vector", lambda e, k=k: e.tensor_copy(ob[:, k, :], ot[:, k, :]),
                 inc=("ve" if k == NCH - 1 else None))
        p.do("vector", lambda e: e.engine_nop(), inc=f"{tag}_app")
        p.wait("sync", "ve", p.val("ve"))
        if out_f32 is not None:
            p.do("sync", lambda e, tc=tc: e.dma_start(
                out=out_f32[:, out_guard + tc * TC: out_guard + (tc + 1) * TC]
                    .rearrange("(c p) t -> p c t", p=128), in_=ot[:]), inc=stsem, n=16)
        if out_bf16 is not None:
            p.do("sync", lambda e, tc=tc: e.dma_start(
                out=out_bf16[:, out_guard + tc * TC: out_guard + (tc + 1) * TC]
                    .rearrange("(c p) t -> p c t", p=128), in_=ob[:]), inc=stsem, n=16)
        if out_tm is not None:
            p.wait("tensor", "ve", p.val("ve"))
            for k in range(NCH):
                for j in range(4):
                    p.wait("tensor", f"{tag}_tmev", p.val(f"{tag}_tmev"))
                    p.do("tensor", lambda e, k=k, j=j: e.transpose(
                        tm_ps[:, 0:128], ob[:, k, j * 128:(j + 1) * 128], ident_tm[:]),
                        inc="mm2")
                    p.wait("vector", "mm2", p.val("mm2"))
                    p.wait("vector", stsem, p.val(stsem))
                    p.do("vector", lambda e: e.tensor_copy(tm_sb[:], tm_ps[:, 0:128]),
                         inc=f"{tag}_tmev")
                    p.wait("sync", f"{tag}_tmev", p.val(f"{tag}_tmev"))
                    p.do("sync", lambda e, k=k, j=j, tc=tc: e.dma_start(
                        out=out_tm[tc * TC + j * 128: tc * TC + (j + 1) * 128,
                                   k * 128:(k + 1) * 128], in_=tm_sb[:]), inc=stsem, n=16)
    p.wait("sync", stsem, p.val(stsem))



def st_l0_qkv(p):
    x_sb, ldcnt = st_load_stream(p, "a2", p.drams["xn_bf"])
    p.wait("tensor", "a2_xld", ldcnt)
    bq = load_const(p, "a2bq", "b_l0a_q", [128, 8])
    bk = load_const(p, "a2bk", "b_l0a_k", [128, 8])
    bv = load_const(p, "a2bv", "b_l0a_v", [128, 8])
    ident = load_const(p, "a2id", "c_ident_bf", [128, 128], BF16)
    wait_consts(p, "scalar", "tensor")
    ev = [p.sbuf(f"a2_ev{i}", [128, TC], BF16) for i in range(2)]
    evt = [p.sbuf(f"a2_evt{i}", [128, TC], BF16) for i in range(2)]
    psb = p.psum("a2_trp", [128, 1024], BF16)
    q_fm, k_fm, v_tm = p.drams["l0_q"], p.drams["l0_k"], p.drams["l0_v"]

    def mk_epi(bias, dst, is_v, tag):
        def epi(mo, tc, ps, mwid, par=0):
            evac = f"{tag}_ev"
            ev_, evt_ = ev[par], evt[par]
            stp = f"a2st{par}"
            p.wait("scalar", stp, p.val(stp))
            p.wait("scalar", "mm", p.val("mm"))
            p.do("scalar", lambda e, ev_=ev_, mo=mo: e.activation(
                out=ev_[:], in_=ps[:], func=AF.Identity, bias=bias[:, mo:mo + 1]), inc=evac)
            if not is_v:
                p.wait("sync", evac, p.val(evac))
                p.do("sync", lambda e, ev_=ev_, mo=mo, tc=tc: e.dma_start(
                    out=dst[mo * 128:(mo + 1) * 128, tc * TC:(tc + 1) * TC], in_=ev_[:]),
                    inc=stp, n=16)
            else:
                p.wait("tensor", evac, p.val(evac))
                p.wait("tensor", "ve2", p.val("ve2"))
                for j in range(4):
                    p.do("tensor", lambda e, j=j, ev_=ev_: e.transpose(
                        psb[:, j * 128:(j + 1) * 128], ev_[:, j * 128:(j + 1) * 128], ident[:]),
                        inc=("mm2" if j == 3 else None))
                p.wait("vector", "mm2", p.val("mm2"))
                p.wait("vector", stp, p.val(stp))
                for j in range(4):
                    p.do("vector", lambda e, j=j, evt_=evt_: e.tensor_copy(
                        evt_[:, j * 128:(j + 1) * 128], psb[:, j * 128:(j + 1) * 128]),
                        inc=("ve2" if j == 3 else None))
                p.wait("sync", "ve2", p.val("ve2"))
                for j in range(4):
                    for hh in range(2):
                        p.do("sync", lambda e, mo=mo, tc=tc, j=j, hh=hh, evt_=evt_: e.dma_start(
                            out=dst[2 * mo + hh, tc * TC + j * 128: tc * TC + (j + 1) * 128, :],
                            in_=evt_[:, j * 128 + hh * 64: j * 128 + hh * 64 + 64]),
                            inc=stp, n=16)
        return epi

    emit_proj(p, "a2q", x_sb, NCH, p.drams["w_l0a_q"], D, mk_epi(bq, q_fm, False, "a2q"))
    emit_proj(p, "a2k", x_sb, NCH, p.drams["w_l0a_k"], D, mk_epi(bk, k_fm, False, "a2k"))
    emit_proj(p, "a2v", x_sb, NCH, p.drams["w_l0a_v"], D, mk_epi(bv, v_tm, True, "a2v"))
    p.wait("sync", "a2st0", p.val("a2st0"))
    p.wait("sync", "a2st1", p.val("a2st1"))


def st_causal_attn(p, tag, q_fm, k_fm, v_tm, o_fm, n_heads, d_head, inv_scale,
                   mask_name, k_dram_rows=None):
    ident = load_const(p, f"{tag}id", "c_ident_bf", [128, 128], BF16)
    mask = load_const(p, f"{tag}mk", mask_name, [128, 128])
    wait_consts(p, "tensor", "vector")
    shared_kv = (k_dram_rows == d_head)
    kq_parts = d_head
    k_sb = p.sbuf(f"{tag}_k", [kq_parts, T], BF16)
    v_sb = p.sbuf(f"{tag}_v", [128, 32, d_head + 1], BF16)
    q_sb = [p.sbuf(f"{tag}_q{i}", [kq_parts, 128], BF16) for i in range(2)]
    S_sb = p.sbuf(f"{tag}_S", [128, T], F32)
    P_sb = p.sbuf(f"{tag}_P", [128, T], BF16)
    S_ps = [p.psum(f"{tag}_sp{i}", [128, TC], F32) for i in range(2)]
    PT_ps = [p.psum(f"{tag}_ptp{i}", [128, 1024], BF16) for i in range(2)]
    PT_sb = p.sbuf(f"{tag}_pt", [128, 2, 128], BF16)
    AV_ps = p.psum(f"{tag}_av", [128, TC], F32)
    AVc = p.sbuf(f"{tag}_avc", [128, d_head + 1], F32)
    O32 = p.sbuf(f"{tag}_o32", [128, d_head], F32)
    Obf = p.sbuf(f"{tag}_obf", [128, d_head], BF16)
    OT_ps = p.psum(f"{tag}_otp", [128, 1024], BF16)
    OT_sb = [p.sbuf(f"{tag}_ot{i}", [d_head, 128], BF16) for i in range(2)]
    nm = p.sbuf(f"{tag}_nm", [128, 1], F32)
    nmr = p.sbuf(f"{tag}_nmr", [128, 1], F32)
    kl, vl, ql = f"{tag}_kld", f"{tag}_vld", f"{tag}_qld"
    hd = f"{tag}_hd"
    hdbase = p.val(hd)
    for h in range(n_heads):
        if shared_kv:
            kv_load = (h == 0)
            kv_idx = 0
        else:
            kv_load = True
            kv_idx = h
        if kv_load:
            if h > 0:
                p.wait("sync", hd, hdbase + h)
            krow = 0 if shared_kv else h * d_head
            p.do("sync", lambda e, krow=krow: e.dma_start(
                out=k_sb[:], in_=k_fm[krow:krow + d_head, :]), inc=kl, n=16)
            p.do("sync", lambda e, kv_idx=kv_idx: e.dma_start(
                out=v_sb[:, :, 0:d_head],
                in_=v_tm[kv_idx].rearrange("(tt p) d -> p tt d", p=128)), inc=vl, n=16)
            p.wait("vector", vl, p.val(vl))
            p.do("vector", lambda e: e.memset(v_sb[:, :, d_head:d_head + 1], 1.0), inc="ve")
        p.wait("tensor", kl, p.val(kl))
        p.wait("tensor", vl, p.val(vl))
        p.wait("tensor", "ve", p.val("ve"))
        for qs in range(32):
            qb = qs % 2
            p.do("sync", lambda e, h=h, qs=qs, qb=qb: e.dma_start(
                out=q_sb[qb][:], in_=q_fm[h * d_head:(h + 1) * d_head, qs * 128:(qs + 1) * 128]),
                inc=ql, n=16)
            p.wait("tensor", ql, p.val(ql))
            w_tot = (qs + 1) * 128
            n_kt = (w_tot + TC - 1) // TC
            for kt in range(n_kt):
                w = min(TC, w_tot - kt * TC)
                S = S_ps[kt % 2]
                p.wait("tensor", "ac", p.val("ac"))
                p.do("tensor", lambda e, S=S, qb=qb, kt=kt, w=w: e.matmul(
                    S[:, 0:w], q_sb[qb][:], k_sb[:, kt * TC:kt * TC + w],
                    start=True, stop=True), inc="mm")
                p.wait("scalar", "mm", p.val("mm"))
                p.do("scalar", lambda e, S=S, kt=kt, w=w: e.activation(
                    out=S_sb[:, kt * TC:kt * TC + w], in_=S[:, 0:w], func=AF.Identity,
                    scale=float(inv_scale)), inc="ac")
            p.wait("vector", "ac", p.val("ac"))
            p.do("vector", lambda e, w_tot=w_tot: e.scalar_tensor_tensor(
                S_sb[:, w_tot - 128:w_tot], mask[:], float(inv_scale),
                S_sb[:, w_tot - 128:w_tot], ALU.mult, ALU.add))
            p.do("vector", lambda e, w_tot=w_tot: e.reduce_max(
                nm[:], S_sb[:, 0:w_tot], axis=AX.X, negate=True), inc="ve")
            p.wait("scalar", "ve", p.val("ve"))
            p.wait("scalar", "mm3", p.val("mm3"))
            p.do("scalar", lambda e, w_tot=w_tot: e.activation(
                out=P_sb[:, 0:w_tot], in_=S_sb[:, 0:w_tot], func=AF.Exp, bias=nm[:]),
                inc="ac")
            p.wait("tensor", "ac", p.val("ac"))
            for j in range(w_tot // 128):
                tb = j % 2
                if j >= 2:
                    p.wait("tensor", "ve2", p.val("ve2"))
                p.do("tensor", lambda e, j=j, tb=tb: e.transpose(
                    PT_ps[tb][:, 0:128], P_sb[:, j * 128:(j + 1) * 128], ident[:]), inc="mm2")
                p.wait("vector", "mm2", p.val("mm2"))
                p.do("vector", lambda e, tb=tb: e.tensor_copy(
                    PT_sb[:, tb, :], PT_ps[tb][:, 0:128]), inc="ve2")
                p.wait("tensor", "ve2", p.val("ve2"))
                p.do("tensor", lambda e, j=j, tb=tb, w_tot=w_tot: e.matmul(
                    AV_ps[:, 0:d_head + 1], PT_sb[:, tb, :], v_sb[:, j, :],
                    start=(j == 0), stop=(j == w_tot // 128 - 1)),
                    inc=("mm3" if j == w_tot // 128 - 1 else None))
            if "dbg_S" in p.drams and h == 0 and qs == 0:
                p.wait("sync", "ac", p.val("ac"))
                p.do("sync", lambda e: e.dma_start(out=p.drams["dbg_S"][:], in_=S_sb[:, 0:128]), inc="dbg", n=16)
                dbgP = p.sbuf(f"{tag}_dbgP", [128, 128], F32)
                p.do("vector", lambda e: e.tensor_copy(dbgP[:], P_sb[:, 0:128]), inc="ve9")
                p.wait("sync", "ve9", p.val("ve9"))
                p.do("sync", lambda e: e.dma_start(out=p.drams["dbg_P"][:], in_=dbgP[:]), inc="dbg", n=16)
            p.wait("vector", "mm3", p.val("mm3"))
            if "dbg_S" in p.drams and h == 0 and qs == 0:
                dbgA = p.sbuf(f"{tag}_dbgA", [128, d_head + 1], F32)
                p.do("vector", lambda e: e.tensor_copy(dbgA[:], AV_ps[:, 0:d_head + 1]), inc="ve9")
                p.wait("sync", "ve9", p.val("ve9"))
                p.do("sync", lambda e: e.dma_start(out=p.drams["dbg_AV"][:], in_=dbgA[:]), inc="dbg", n=16)
            p.do("vector", lambda e: e.tensor_copy(AVc[:], AV_ps[:, 0:d_head + 1]), inc="vb")
            p.wait("vector", "vb", p.val("vb"))
            p.do("vector", lambda e: e.reciprocal(nmr[:], AVc[:, d_head:d_head + 1]), inc="vb")
            p.wait("vector", "vb", p.val("vb"))
            p.do("vector", lambda e: e.tensor_scalar_mul(
                O32[:], AVc[:, 0:d_head], nmr[:]), inc="vb")
            p.wait("vector", "vb", p.val("vb"))
            p.do("vector", lambda e: e.tensor_copy(Obf[:], O32[:]), inc="ve")
            if "dbg_S" in p.drams and h == 0 and qs == 0:
                p.wait("sync", "ve", p.val("ve"))
                p.do("sync", lambda e: e.dma_start(out=p.drams["dbg_N"][:, 0:d_head + 1], in_=AVc[:]), inc="dbg", n=16)
                p.do("sync", lambda e: e.dma_start(out=p.drams["dbg_N"][:, d_head + 1:d_head + 2], in_=nmr[:], allow_slow_non_contiguous=True), inc="dbg", n=16)
                p.do("sync", lambda e: e.dma_start(out=p.drams["dbg_N"][:, d_head + 2:2 * d_head + 2], in_=O32[:]), inc="dbg", n=16)
            p.wait("tensor", "ve", p.val("ve"))
            p.do("tensor", lambda e: e.transpose(OT_ps[0:d_head, 0:128], Obf[:], ident[:]), inc="mm2")
            p.wait("vector", "mm2", p.val("mm2"))
            spar = qs % 2
            p.wait("vector", f"{tag}_st{spar}", p.val(f"{tag}_st{spar}"))
            p.do("vector", lambda e, spar=spar: e.tensor_copy(
                OT_sb[spar][:], OT_ps[0:d_head, 0:128]), inc="ve3")
            p.wait("sync", "ve3", p.val("ve3"))
            p.do("sync", lambda e, h=h, qs=qs, spar=spar: e.dma_start(
                out=o_fm[h * d_head:(h + 1) * d_head, qs * 128:(qs + 1) * 128],
                in_=OT_sb[spar][:]), inc=f"{tag}_st{spar}", n=16)
        p.do("sync", lambda e: e.nop(), inc=hd)
    p.wait("sync", f"{tag}_st0", p.val(f"{tag}_st0"))
    p.wait("sync", f"{tag}_st1", p.val(f"{tag}_st1"))


def st_proj_simple(p, tag, x_dram_name, w_name, nout, out_name, bias_name=None,
                   out_dtype="bf16", kch=NCH, act=None, out_guard=0, add_from=None,
                   w_row0=0, also_bf16=None):
    x_sb, ldcnt = st_load_stream(p, tag, p.drams[x_dram_name], kch=kch)
    p.wait("tensor", f"{tag}_xld", ldcnt)
    bias = load_const(p, f"{tag}_b", bias_name, [128, (nout + 127) // 128]) if bias_name else None
    wait_consts(p, "scalar")
    ev32 = [p.sbuf(f"{tag}_e32{i}", [128, TC], F32) for i in range(2)]
    evb = [p.sbuf(f"{tag}_eb{i}", [128, TC], BF16) for i in range(2)]
    dst = p.drams[out_name]
    dst_bf = p.drams[also_bf16] if also_bf16 else None
    addt = [p.sbuf(f"{tag}_add{i}", [128, TC], F32) for i in range(2)] if add_from else None

    def epi(mo, tc, ps, mwid, par=0):
        ev32_, evb_, addt_ = ev32[par], evb[par], (addt[par] if add_from else None)
        stp = f"{tag}st{par}"
        p.wait("scalar", "mm", p.val("mm"))
        p.wait("scalar", stp, p.val(stp))
        kw = {}
        if bias is not None:
            kw["bias"] = bias[:, mo:mo + 1]
        fn = act if act is not None else AF.Identity
        if add_from is not None:
            p.wait("sync", "mm", p.val("mm"))
            p.do("sync", lambda e, mo=mo, tc=tc, mwid=mwid, addt_=addt_: e.dma_start(
                out=addt_[0:mwid, :],
                in_=p.drams[add_from][mo * 128:mo * 128 + mwid, tc * TC:(tc + 1) * TC]),
                inc=f"{tag}_ald", n=16)
            p.wait("scalar", f"{tag}_ald", p.val(f"{tag}_ald"))
        p.do("scalar", lambda e, mo=mo, mwid=mwid, ev32_=ev32_, evb_=evb_: e.activation(
            out=(ev32_ if out_dtype == "f32" else evb_)[0:mwid, :], in_=ps[0:mwid, :], func=fn, **kw),
            inc=("ac" if (add_from or also_bf16) else f"{tag}_ev"))
        srct = ev32_ if out_dtype == "f32" else evb_
        if add_from is not None:
            p.wait("vector", "ac", p.val("ac"))
            p.do("vector", lambda e, mwid=mwid, srct=srct, addt_=addt_: e.tensor_add(
                srct[0:mwid, :], srct[0:mwid, :], addt_[0:mwid, :]),
                inc=("ve" if also_bf16 else f"{tag}_ev"))
            if also_bf16 is not None:
                p.do("vector", lambda e, mwid=mwid, srct=srct, evb_=evb_: e.tensor_copy(
                    evb_[0:mwid, :], srct[0:mwid, :]), inc=f"{tag}_ev")
            p.wait("sync", f"{tag}_ev", p.val(f"{tag}_ev"))
        else:
            if also_bf16 is not None:
                p.wait("vector", "ac", p.val("ac"))
                p.do("vector", lambda e, mwid=mwid, srct=srct, evb_=evb_: e.tensor_copy(
                    evb_[0:mwid, :], srct[0:mwid, :]), inc=f"{tag}_ev")
                p.wait("sync", f"{tag}_ev", p.val(f"{tag}_ev"))
            else:
                p.wait("sync", f"{tag}_ev", p.val(f"{tag}_ev"))
        p.do("sync", lambda e, mo=mo, tc=tc, mwid=mwid, srct=srct: e.dma_start(
            out=dst[mo * 128:mo * 128 + mwid, out_guard + tc * TC:out_guard + (tc + 1) * TC],
            in_=srct[0:mwid, :]), inc=stp, n=16)
        if also_bf16 is not None:
            p.do("sync", lambda e, mo=mo, tc=tc, mwid=mwid, evb_=evb_: e.dma_start(
                out=dst_bf[mo * 128:mo * 128 + mwid, out_guard + tc * TC:out_guard + (tc + 1) * TC],
                in_=evb_[0:mwid, :]), inc=stp, n=16)

    emit_proj(p, tag, x_sb, kch, p.drams[w_name], nout, epi, w_row0=w_row0)
    p.wait("sync", f"{tag}st0", p.val(f"{tag}st0"))
    p.wait("sync", f"{tag}st1", p.val(f"{tag}st1"))



def tm_epilogue(p, tag, ev, evt, psb, ident, dst_tm, bias):
    def epi(mo, tc, ps, mwid, par=0):
        ev_, evt_ = ev[par], evt[par]
        stp = f"{tag}st{par}"
        p.wait("scalar", stp, p.val(stp))
        p.wait("scalar", "mm", p.val("mm"))
        p.do("scalar", lambda e, ev_=ev_, mo=mo: e.activation(
            out=ev_[:], in_=ps[:], func=AF.Identity, bias=bias[:, mo:mo + 1]),
            inc=f"{tag}_ev")
        p.wait("tensor", f"{tag}_ev", p.val(f"{tag}_ev"))
        p.wait("tensor", "ve2", p.val("ve2"))
        for j in range(4):
            p.do("tensor", lambda e, j=j, ev_=ev_: e.transpose(
                psb[:, j * 128:(j + 1) * 128], ev_[:, j * 128:(j + 1) * 128], ident[:]),
                inc=("mm2" if j == 3 else None))
        p.wait("vector", "mm2", p.val("mm2"))
        p.wait("vector", stp, p.val(stp))
        for j in range(4):
            p.do("vector", lambda e, j=j, evt_=evt_: e.tensor_copy(
                evt_[:, j * 128:(j + 1) * 128], psb[:, j * 128:(j + 1) * 128]),
                inc=("ve2" if j == 3 else None))
        p.wait("sync", "ve2", p.val("ve2"))
        mo8 = mo % 8
        for j in range(4):
            p.do("sync", lambda e, mo8=mo8, tc=tc, j=j, evt_=evt_: e.dma_start(
                out=dst_tm(mo)[tc * TC + j * 128: tc * TC + (j + 1) * 128,
                               mo8 * 128:(mo8 + 1) * 128],
                in_=evt_[:, j * 128:(j + 1) * 128]), inc=stp, n=16)
    return epi


def st_l0f_qkv(p):
    x_sb, ldcnt = st_load_stream(p, "b1", p.drams["xn_bf"])
    p.wait("tensor", "b1_xld", ldcnt)
    bias = load_const(p, "b1_b", "b_l0f_qkv", [128, 24])
    ident = load_const(p, "b1_id", "c_ident_bf", [128, 128], BF16)
    wait_consts(p, "scalar", "tensor")
    ev = [p.sbuf(f"b1_ev{i}", [128, TC], BF16) for i in range(2)]
    evt = [p.sbuf(f"b1_evt{i}", [128, TC], BF16) for i in range(2)]
    psb = p.psum("b1_trp", [128, 1024], BF16)
    dsts = [p.drams["q0_tm"], p.drams["k0_tm"], p.drams["v0_tm"]]
    epi = tm_epilogue(p, "b1", ev, evt, psb, ident, lambda mo: dsts[mo // 8], bias)
    emit_proj(p, "b1", x_sb, NCH, p.drams["w_l0f_qkv"], 3 * D, epi)
    p.wait("sync", "b1st0", p.val("b1st0"))
    p.wait("sync", "b1st1", p.val("b1st1"))


def st_l0f_mha(p):
    q0, k0a, v0a, o0 = (p.drams["q0_tm"], p.drams["k0_all"], p.drams["v0_all"],
                        p.drams["o0_fm"])
    ident = load_const(p, "b2_id", "c_ident_bf", [128, 128], BF16)
    wait_consts(p, "tensor")
    qt = [p.sbuf(f"b2_q{i}", [128, D], BF16) for i in range(2)]
    kv = [p.sbuf(f"b2_kv{i}", [128, D], BF16) for i in range(2)]
    vt32 = p.sbuf("b2_v32", [128, D], F32)
    prod = p.sbuf("b2_pr", [128, D], F32)
    sc = p.sbuf("b2_sc", [128, 32], F32)
    ex = p.sbuf("b2_ex", [128, 32], F32)
    nmx4 = p.sbuf("b2_nm", [128, 4], F32)
    bh4 = p.sbuf("b2_bh", [128, 4], F32)
    s4 = p.sbuf("b2_s4", [128, 4], F32)
    rs4 = p.sbuf("b2_rs", [128, 4], F32)
    oacc = p.sbuf("b2_o", [128, D], F32)
    tmp = p.sbuf("b2_t", [128, 256], F32)
    obf = p.sbuf("b2_ob", [128, D], BF16)
    trp = p.psum("b2_trp", [128, 1024], BF16)
    trs = p.sbuf("b2_trs", [128, 128], BF16)
    ql, kl = "b2_ql", "b2_kl"
    for tt in range(32):
        qb = tt % 2
        p.wait("sync", "b2_pr1", p.val("b2_pr1"))
        p.do("sync", lambda e, tt=tt, qb=qb: e.dma_start(
            out=qt[qb][:], in_=q0[tt * 128:(tt + 1) * 128, :]), inc=ql, n=16)
        for m in range(8):
            mb = m % 2
            if m >= 2:
                p.wait("sync", "b2_pr1", p.val("b2_pr1"))
            p.do("sync", lambda e, m=m, mb=mb, tt=tt: e.dma_start(
                out=kv[mb][:], in_=k0a[m, tt * 128:(tt + 1) * 128, :]), inc=kl, n=16)
            p.wait("vector", kl, p.val(kl))
            p.wait("vector", ql, p.val(ql))
            p.do("vector", lambda e, mb=mb, qb=qb: e.tensor_tensor(
                prod[:], qt[qb][:], kv[mb][:], ALU.mult), inc="vb")
            p.wait("vector", "vb", p.val("vb"))
            p.do("vector", lambda e, m=m: e.tensor_reduce(
                sc.rearrange("p (h m) -> p h m", m=8)[:, :, m], prod[:].rearrange(
                    "p (h d) -> p h d", d=256), axis=AX.X, op=ALU.add),
                inc="b2_pr1")
            if "dbg_pr" in p.drams and tt == 0 and m == 0:
                p.wait("sync", "b2_pr1", p.val("b2_pr1"))
                p.do("sync", lambda e: e.dma_start(out=p.drams["dbg_pr"][0], in_=prod[:]), inc="dbg", n=16)
                qf = p.sbuf("b2_dbgq", [128, D], F32)
                kf = p.sbuf("b2_dbgk", [128, D], F32)
                p.do("vector", lambda e, qb=qb: e.tensor_copy(qf[:], qt[qb][:]))
                p.do("vector", lambda e, mb=mb: e.tensor_copy(kf[:], kv[mb][:]), inc="ve9")
                p.wait("sync", "ve9", p.val("ve9"))
                p.do("sync", lambda e: e.dma_start(out=p.drams["dbg_pr"][1], in_=qf[:]), inc="dbg", n=16)
                p.do("sync", lambda e: e.dma_start(out=p.drams["dbg_pr"][2], in_=kf[:]), inc="dbg", n=16)
                p.wait("vector", "dbg", p.val("dbg"))
        p.wait("vector", "b2_pr1", p.val("b2_pr1"))
        for h in range(4):
            p.do("vector", lambda e, h=h: e.reduce_max(
                nmx4[:, h:h + 1], sc[:, h * 8:(h + 1) * 8], axis=AX.X, negate=True),
                inc=("vb" if h == 3 else None))
        p.wait("vector", "vb", p.val("vb"))
        p.do("vector", lambda e: e.tensor_scalar_mul(bh4[:], nmx4[:], 1.0 / 16.0), inc="vb")
        p.wait("scalar", "vb", p.val("vb"))
        for h in range(4):
            p.do("scalar", lambda e, h=h: e.activation(
                out=ex[:, h * 8:(h + 1) * 8], in_=sc[:, h * 8:(h + 1) * 8], func=AF.Exp,
                scale=1.0 / 16.0, bias=bh4[:, h:h + 1]), inc="ac")
        p.wait("vector", "ac", p.val("ac"))
        for h in range(4):
            p.do("vector", lambda e, h=h: e.reduce_sum(
                s4[:, h:h + 1], ex[:, h * 8:(h + 1) * 8], axis=AX.X),
                inc=("vb" if h == 3 else None))
        p.wait("vector", "vb", p.val("vb"))
        p.do("vector", lambda e: e.reciprocal(rs4[:], s4[:]), inc="vb")
        if "dbg_sc" in p.drams and tt == 0:
            p.wait("sync", "vb", p.val("vb"))
            p.wait("sync", "ac", p.val("ac"))
            p.do("sync", lambda e: e.dma_start(out=p.drams["dbg_sc"][:, 0:32], in_=sc[:]), inc="dbg", n=16)
            p.do("sync", lambda e: e.dma_start(out=p.drams["dbg_sc"][:, 32:64], in_=ex[:]), inc="dbg", n=16)
            p.do("sync", lambda e: e.dma_start(out=p.drams["dbg_sc"][:, 64:68], in_=s4[:]), inc="dbg", n=16)
            p.wait("vector", "dbg", p.val("dbg"))
            p.wait("scalar", "dbg", p.val("dbg"))
        p.do("vector", lambda e: e.memset(oacc[:], 0.0))
        p.wait("sync", "b2_pr1", p.val("b2_pr1"))
        for m in range(8):
            mb = m % 2
            p.wait("sync", "b2_pr2", p.val("b2_pr2"))
            p.do("sync", lambda e, m=m, mb=mb, tt=tt: e.dma_start(
                out=kv[mb][:], in_=v0a[m, tt * 128:(tt + 1) * 128, :]), inc=kl, n=16)
            p.wait("vector", kl, p.val(kl))
            p.do("vector", lambda e, mb=mb: e.tensor_copy(vt32[:], kv[mb][:]), inc="b2_pr2")
            for h in range(4):
                p.do("vector", lambda e, h=h, m=m: e.tensor_scalar_mul(
                    tmp[:], vt32[:, h * 256:(h + 1) * 256], ex[:, h * 8 + m:h * 8 + m + 1]))
                p.do("vector", lambda e, h=h: e.tensor_add(
                    oacc[:, h * 256:(h + 1) * 256], oacc[:, h * 256:(h + 1) * 256], tmp[:]))
        p.wait("vector", "vb", p.val("vb"))
        for h in range(4):
            p.do("vector", lambda e, h=h: e.tensor_scalar_mul(
                oacc[:, h * 256:(h + 1) * 256], oacc[:, h * 256:(h + 1) * 256],
                rs4[:, h:h + 1]))
        p.do("vector", lambda e: e.tensor_copy(obf[:], oacc[:]), inc="ve")
        p.wait("tensor", "ve", p.val("ve"))
        for k in range(8):
            p.wait("tensor", "b2_tev", p.val("b2_tev"))
            p.do("tensor", lambda e, k=k: e.transpose(
                trp[:, 0:128], obf[:, k * 128:(k + 1) * 128], ident[:]), inc="mm2")
            p.wait("vector", "mm2", p.val("mm2"))
            p.wait("vector", "b2_st", p.val("b2_st"))
            p.do("vector", lambda e: e.tensor_copy(trs[:], trp[:, 0:128]), inc="b2_tev")
            p.wait("sync", "b2_tev", p.val("b2_tev"))
            p.do("sync", lambda e, k=k, tt=tt: e.dma_start(
                out=o0[k * 128:(k + 1) * 128, tt * 128:(tt + 1) * 128], in_=trs[:]),
                inc="b2_st", n=16)
    p.wait("sync", "b2_st", p.val("b2_st"))


def st_moe_route_a(p):
    y1, wg_d = p.drams["y1_bf"], p.drams["w_moe_g"]
    ltri = load_const(p, "b3_lt", "c_ltri", [128, 128])
    iota8 = load_const(p, "b3_io", "c_iota8", [128, 8])
    wg = p.sbuf("b3_wg", [128, NCH, 8], BF16)
    p.do("gpsimd", lambda e: e.dma_start(
        out=wg[:], in_=wg_d.rearrange("(c k) m -> k c m", k=128)), inc="b3_wld", n=16)
    wait_consts(p, "tensor", "vector")
    p.wait("tensor", "b3_wld", p.val("b3_wld"))
    y1t = [p.sbuf(f"b3_y{i}", [128, NCH, 128], BF16) for i in range(2)]
    ones1r = p.sbuf("b3_1r", [1, 128], F32)
    ones128 = p.sbuf("b3_1c", [128, 1], F32)
    run = p.sbuf("b3_run", [1, 8], F32)
    g32 = p.sbuf("b3_g", [128, 8], F32)
    nmx = p.sbuf("b3_nm", [128, 1], F32)
    mx = p.sbuf("b3_mx", [128, 1], F32)
    eq = p.sbuf("b3_eq", [128, 8], F32)
    ex8 = p.sbuf("b3_ex", [128, 8], F32)
    s8 = p.sbuf("b3_s8", [128, 1], F32)
    H = p.sbuf("b3_H", [128, 8], F32)
    Ef = p.sbuf("b3_E", [128, 8], F32)
    tsel = p.sbuf("b3_ts", [128, 8], F32)
    loc_acc = p.sbuf("b3_loc", [128, 32], F32)
    idx_acc = p.sbuf("b3_idx", [128, 32], F32)
    gate_acc = p.sbuf("b3_gv", [128, 32], F32)
    ps_g = p.psum("b3_psg", [128, TC], F32)
    ps_E = p.psum("b3_psE", [128, TC], F32)
    ps_t = p.psum("b3_pst", [1, TC], F32)
    p.do("vector", lambda e: e.memset(ones1r[:], 1.0))
    p.do("vector", lambda e: e.memset(ones128[:], 1.0))
    p.do("vector", lambda e: e.memset(run[:], 0.0), inc="ve")
    p.wait("tensor", "ve", p.val("ve"))
    mm_after_tile = []
    for tt in range(32):
        yb = tt % 2
        if tt >= 2:
            p.wait("sync", "mm", mm_after_tile[tt - 2])
        p.do("sync", lambda e, tt=tt, yb=yb: e.dma_start(
            out=y1t[yb][:], in_=y1[:, tt * 128:(tt + 1) * 128]
                .rearrange("(c p) t -> p c t", p=128)), inc="b3_yld", n=16)
        p.wait("tensor", "b3_yld", p.val("b3_yld"))
        p.wait("tensor", "ve", p.val("ve"))
        for k in range(NCH):
            p.do("tensor", lambda e, k=k, yb=yb: e.matmul(
                ps_g[:, 0:8], y1t[yb][:, k, :], wg[:, k, :],
                start=(k == 0), stop=(k == NCH - 1)), inc=("mm" if k == NCH - 1 else None))
        p.wait("vector", "mm", p.val("mm"))
        def gdo(fn):
            p.do("vector", fn, inc="vb")
            p.wait("vector", "vb", p.val("vb"))
        gdo(lambda e: e.tensor_copy(g32[:], ps_g[:, 0:8]))
        if "dbg_g" in p.drams and tt == 0:
            p.wait("sync", "vb", p.val("vb"))
            p.do("sync", lambda e: e.dma_start(out=p.drams["dbg_g"][:], in_=g32[:]), inc="dbg", n=16)
            p.wait("vector", "dbg", p.val("dbg"))
            p.wait("tensor", "dbg", p.val("dbg"))
        gdo(lambda e: e.reduce_max(nmx[:], g32[:], axis=AX.X, negate=True))
        gdo(lambda e: e.tensor_scalar_mul(mx[:], nmx[:], -1.0))
        gdo(lambda e: e.tensor_single_scalar(eq[:], g32[:], mx[:], ALU.is_equal))
        gdo(lambda e: e.tensor_scalar(eq[:], eq[:], -1e9, 1e9, ALU.mult, ALU.add))
        gdo(lambda e: e.tensor_add(eq[:], eq[:], iota8[:]))
        gdo(lambda e, tt=tt: e.tensor_reduce(
            idx_acc[:, tt:tt + 1], eq[:], axis=AX.X, op=ALU.min))
        p.wait("scalar", "vb", p.val("vb"))
        p.do("scalar", lambda e: e.activation(out=ex8[:], in_=g32[:], func=AF.Exp,
                                              bias=nmx[:]), inc="ac")
        p.wait("vector", "ac", p.val("ac"))
        gdo(lambda e: e.reduce_sum(s8[:], ex8[:], axis=AX.X))
        gdo(lambda e, tt=tt: e.reciprocal(gate_acc[:, tt:tt + 1], s8[:]))
        gdo(lambda e, tt=tt: e.tensor_single_scalar(
            H[:], iota8[:], idx_acc[:, tt:tt + 1], ALU.is_equal))
        if "dbg_g" in p.drams and tt == 0:
            p.wait("sync", "vb", p.val("vb"))
            p.do("sync", lambda e: e.dma_start(out=p.drams["dbg_r"][:, 0:1], in_=nmx[:], allow_slow_non_contiguous=True), inc="dbg", n=16)
            p.do("sync", lambda e: e.dma_start(out=p.drams["dbg_r"][:, 1:2], in_=mx[:], allow_slow_non_contiguous=True), inc="dbg", n=16)
            p.do("sync", lambda e: e.dma_start(out=p.drams["dbg_r"][:, 2:10], in_=eq[:], allow_slow_non_contiguous=True), inc="dbg", n=16)
            p.do("sync", lambda e: e.dma_start(out=p.drams["dbg_r"][:, 10:18], in_=H[:], allow_slow_non_contiguous=True), inc="dbg", n=16)
            p.do("sync", lambda e: e.dma_start(out=p.drams["dbg_r"][:, 18:19], in_=idx_acc[:, 0:1], allow_slow_non_contiguous=True), inc="dbg", n=16)
            p.wait("vector", "dbg", p.val("dbg"))
            p.wait("tensor", "dbg", p.val("dbg"))
        p.wait("tensor", "vb", p.val("vb"))
        p.do("tensor", lambda e: e.matmul(ps_E[:, 0:8], ltri[:], H[:], start=True, stop=False))
        p.do("tensor", lambda e: e.matmul(ps_E[:, 0:8], ones1r[:], run[:], start=False, stop=True))
        p.do("tensor", lambda e: e.matmul(ps_t[:, 0:8], ones128[:], H[:], start=True, stop=True), inc="mm")
        p.wait("vector", "mm", p.val("mm"))
        gdo(lambda e: e.tensor_copy(Ef[:], ps_E[:, 0:8]))
        gdo(lambda e: e.tensor_mul(tsel[:], Ef[:], H[:]))
        gdo(lambda e, tt=tt: e.reduce_sum(loc_acc[:, tt:tt + 1], tsel[:], axis=AX.X))
        p.do("vector", lambda e: e.tensor_add(run[:], run[:], ps_t[0:1, 0:8]), inc="ve")
        mm_after_tile.append(p.val("mm"))
    p.wait("sync", "ve", p.val("ve"))
    p.do("sync", lambda e: e.dma_start(out=p.drams["r_loc"][:], in_=loc_acc[:]), inc="b3_st", n=16)
    p.do("sync", lambda e: e.dma_start(out=p.drams["r_idx"][:], in_=idx_acc[:]), inc="b3_st", n=16)
    p.do("sync", lambda e: e.dma_start(out=p.drams["r_gate"][:], in_=gate_acc[:]), inc="b3_st", n=16)
    p.do("sync", lambda e: e.dma_start(out=p.drams["r_hist"][:], in_=run[:]), inc="b3_st", n=16)
    p.wait("sync", "b3_st", p.val("b3_st"))


def st_moe_route_b(p):
    iota8 = load_const(p, "b4_io", "c_iota8", [128, 8])
    hist = load_const(p, "b4_h", "hist_all", [8, 8])
    csel = load_const(p, "b4_cs", "c_coresel", [8, 1])
    locs = load_const(p, "b4_lc", "r_loc", [128, 32])
    idxs = load_const(p, "b4_ix", "r_idx", [128, 32])
    wait_consts(p, "tensor", "vector")
    ones1r = p.sbuf("b4_1r", [1, 128], F32)
    co_row = p.sbuf("b4_cor", [1, 8], F32)
    co_bc = p.sbuf("b4_cob", [128, 8], F32)
    H = p.sbuf("b4_H", [128, 8], F32)
    t8 = p.sbuf("b4_t8", [128, 8], F32)
    col = p.sbuf("b4_col", [128, 6], F32)
    gk = p.sbuf("b4_gk", [128, 32], F32)
    gates_sb = load_const(p, "b4_gt", "r_gate", [128, 32])
    desti = p.sbuf("b4_di", [128, 32], I32)
    y1t = [p.sbuf(f"b4_y{i}", [128, D], BF16) for i in range(2)]
    ps1 = p.psum("b4_ps1", [1, TC], F32)
    ps2 = p.psum("b4_ps2", [128, TC], F32)
    disp = p.drams["moe_disp"]
    scbase = p.val("b4_sc")
    p.do("vector", lambda e: e.memset(ones1r[:], 1.0), inc="ve")
    p.wait("tensor", "ve", p.val("ve"))
    p.do("tensor", lambda e: e.matmul(ps1[:, 0:8], csel[:], hist[:], start=True, stop=True), inc="mm")
    p.wait("vector", "mm", p.val("mm"))
    p.do("vector", lambda e: e.tensor_copy(co_row[:], ps1[0:1, 0:8]), inc="ve")
    p.wait("tensor", "ve", p.val("ve"))
    p.do("tensor", lambda e: e.matmul(ps2[:, 0:8], ones1r[:], co_row[:], start=True, stop=True), inc="mm")
    p.wait("vector", "mm", p.val("mm"))
    p.do("vector", lambda e: e.tensor_copy(co_bc[:], ps2[:, 0:8]), inc="ve")
    def gdo(fn):
        p.do("vector", fn, inc="vb")
        p.wait("vector", "vb", p.val("vb"))
    for tt in range(32):
        yb = tt % 2
        gdo(lambda e, tt=tt: e.tensor_single_scalar(
            H[:], iota8[:], idxs[:, tt:tt + 1], ALU.is_equal))
        gdo(lambda e: e.tensor_mul(t8[:], H[:], co_bc[:]))
        gdo(lambda e: e.reduce_sum(col[:, 0:1], t8[:], axis=AX.X))
        gdo(lambda e, tt=tt: e.tensor_add(col[:, 1:2], locs[:, tt:tt + 1], col[:, 0:1]))
        gdo(lambda e: e.tensor_single_scalar(col[:, 2:3], col[:, 1:2], float(T * NCORE // 8), ALU.is_lt))
        gdo(lambda e, tt=tt: e.tensor_single_scalar(col[:, 3:4], locs[:, tt:tt + 1], float(MOE_C), ALU.is_lt))
        gdo(lambda e: e.tensor_mul(col[:, 2:3], col[:, 2:3], col[:, 3:4]))
        gdo(lambda e: e.tensor_scalar(col[:, 5:6], col[:, 2:3], -float(8 * MOE_C), float(8 * MOE_C), ALU.mult, ALU.add))
        gdo(lambda e, tt=tt: e.tensor_single_scalar(col[:, 4:5], idxs[:, tt:tt + 1], float(MOE_C), ALU.mult))
        gdo(lambda e, tt=tt: e.tensor_add(col[:, 4:5], col[:, 4:5], locs[:, tt:tt + 1]))
        gdo(lambda e: e.tensor_mul(col[:, 4:5], col[:, 4:5], col[:, 2:3]))
        gdo(lambda e: e.tensor_add(col[:, 4:5], col[:, 4:5], col[:, 5:6]))
        gdo(lambda e, tt=tt: e.tensor_mul(gk[:, tt:tt + 1], gates_sb[:, tt:tt + 1], col[:, 2:3]))
        p.do("vector", lambda e, tt=tt: e.tensor_copy(desti[:, tt:tt + 1], col[:, 4:5]), inc="ve")
        p.do("sync", lambda e, tt=tt, yb=yb: e.dma_start(
            out=y1t[yb][:], in_=p.drams["y1_tm"][tt * 128:(tt + 1) * 128, :]), inc="b4_yld", n=16)
        p.wait("gpsimd", "b4_yld", p.val("b4_yld"))
        p.wait("gpsimd", "ve", p.val("ve"))
        p.do("gpsimd", lambda e, tt=tt, yb=yb: e.indirect_dma_start(
            out=disp[:], out_offset=IndirectOffsetOnAxis(desti[:, tt:tt + 1], 0),
            in_=y1t[yb][:], in_offset=None), inc="b4_sc", n=16)
        if tt >= 1:
            p.wait("sync", "b4_sc", scbase + 16 * tt)
    p.wait("sync", "b4_sc", p.val("b4_sc"))
    p.do("sync", lambda e: e.dma_start(out=p.drams["r_dest"][:], in_=desti[:]), inc="b4_st", n=16)
    p.do("sync", lambda e: e.dma_start(out=p.drams["r_gatek"][:], in_=gk[:]), inc="b4_st", n=16)
    p.wait("sync", "b4_st", p.val("b4_st"))


B5_NEXP = 8
B5_DO_Y = True
B5_DO_H = True
B5_HMODE = 0


def st_moe_ffn(p):
    ident = load_const(p, "b5_id", "c_ident_bf", [128, 128], BF16)
    identf = load_const(p, "b5_idf", "c_ident_f32", [128, 128])
    b1v = p.sbuf("b5_b1", [128, 8, 32], F32)
    p.do("sync", lambda e: e.dma_start(
        out=b1v[:], in_=p.drams["b_moe_1"].rearrange("e p c -> p e c")), inc="cld", n=16)
    b2v = p.sbuf("b5_b2", [128, 8, 8], F32)
    p.do("sync", lambda e: e.dma_start(
        out=b2v[:], in_=p.drams["b_moe_2"].rearrange("e p c -> p e c")), inc="cld", n=16)
    wait_consts(p, "scalar", "tensor", "vector")
    disp, ydr = p.drams["moe_disp"], p.drams["moe_y"]
    dtile = p.sbuf("b5_dt", [128, D], BF16)
    dfm = p.sbuf("b5_dfm", [128, NCH, MOE_C], BF16)
    hfm = p.sbuf("b5_hfm", [128, 32, MOE_C], BF16)
    ev = p.sbuf("b5_ev", [128, TC], BF16)
    ev32 = p.sbuf("b5_e32", [128, TC], F32)
    evt = p.sbuf("b5_evt", [128, TC], F32)
    trp = p.psum("b5_trp", [128, 1024], BF16)
    trpf = p.psum("b5_trpf", [128, TC], F32)
    shared_banks = [p.psum(f"b5_pb{i}", [128, TC], F32) for i in range(2)]
    dcpbase = p.val("b5_dcp")
    shared_state = [None, None]
    wb_h = [p.sbuf(f"b5_wh{i}", [128, NCH, 128], BF16) for i in range(2)]
    wb_y = [p.sbuf(f"b5_wy{i}", [128, 32, 128], BF16) for i in range(2)]
    for e_i in range(B5_NEXP):
        for st_ in range(MOE_C // 128):
            p.wait("sync", "b5_dcp", dcpbase + NCH * (e_i * (MOE_C // 128) + st_))
            p.do("sync", lambda e, e_i=e_i, st_=st_: e.dma_start(
                out=dtile[:], in_=disp[e_i * MOE_C + st_ * 128: e_i * MOE_C + (st_ + 1) * 128, :]),
                inc="b5_dld", n=16)
            p.wait("tensor", "b5_dld", p.val("b5_dld"))
            for k in range(NCH):
                p.wait("tensor", "b5_dcp", p.val("b5_dcp"))
                p.do("tensor", lambda e, k=k: e.transpose(
                    trp[:, 0:128], dtile[:, k * 128:(k + 1) * 128], ident[:]), inc="mm2")
                p.wait("vector", "mm2", p.val("mm2"))
                p.do("vector", lambda e, k=k, st_=st_: e.tensor_copy(
                    dfm[:, k, st_ * 128:(st_ + 1) * 128], trp[:, 0:128]), inc="b5_dcp")
        p.wait("tensor", "b5_dcp", p.val("b5_dcp"))
        def epi_h(mo, tc, ps, mwid, par=0, e_i=e_i):
            p.wait("scalar", "mm", p.val("mm"))
            p.do("scalar", lambda e, mo=mo, tc=tc: e.activation(
                out=hfm[:, mo, tc * TC:(tc + 1) * TC], in_=ps[:], func=AF.Identity,
                bias=b1v[:, e_i, mo:mo + 1]), inc=f"b5h{e_i}_ev")
        if B5_DO_H:
            emit_proj(p, f"b5h{e_i}", dfm, NCH, p.drams["w_moe_1"], 4096, epi_h,
                      tc_list=list(range(MOE_C // TC)), banks=shared_banks,
                      bank_state=shared_state, wbufs=wb_h, w_row0=e_i * 1024)
        p.wait("tensor", f"b5h{e_i}_ev", p.val(f"b5h{e_i}_ev"))
        def epi_y(mo, tc, ps, mwid, par=0, e_i=e_i):
            p.wait("scalar", "mm", p.val("mm"))
            p.wait("scalar", "b5_yst", p.val("b5_yst"))
            p.do("scalar", lambda e, mo=mo: e.activation(
                out=ev32[:], in_=ps[:], func=AF.Identity,
                bias=b2v[:, e_i, mo:mo + 1]), inc="ac")
            p.wait("tensor", "ac", p.val("ac"))
            p.wait("tensor", "b5_ytev", p.val("b5_ytev"))
            for j in range(4):
                p.do("tensor", lambda e, j=j: e.transpose(
                    trpf[:, 0:128], ev32[:, j * 128:(j + 1) * 128], identf[:]), inc="mm2")
                p.wait("vector", "mm2", p.val("mm2"))
                p.wait("vector", "b5_yst", p.val("b5_yst"))
                p.do("vector", lambda e, j=j: e.tensor_copy(
                    evt[:, j * 128:(j + 1) * 128], trpf[:, 0:128]), inc="b5_ytev")
                p.wait("tensor", "b5_ytev", p.val("b5_ytev"))
            p.wait("sync", "b5_ytev", p.val("b5_ytev"))
            for j in range(4):
                p.do("sync", lambda e, e_i=e_i, mo=mo, tc=tc, j=j: e.dma_start(
                    out=ydr[e_i * MOE_C + tc * TC + j * 128: e_i * MOE_C + tc * TC + (j + 1) * 128,
                            mo * 128:(mo + 1) * 128],
                    in_=evt[:, j * 128:(j + 1) * 128]), inc="b5_yst", n=16)
            p.do("sync", lambda e: e.nop(), inc=f"b5y{e_i}_ev")
        if B5_DO_Y:
            emit_proj(p, f"b5y{e_i}", hfm, 32, p.drams["w_moe_2"], D, epi_y,
                      tc_list=list(range(MOE_C // TC)), banks=shared_banks,
                      bank_state=shared_state, wbufs=wb_y, w_row0=e_i * 4096)
    p.wait("sync", "b5_yst", p.val("b5_yst"))


def st_moe_gather(p):
    identf = load_const(p, "b6_idf", "c_ident_f32", [128, 128])
    dsti = load_const(p, "b6_di", "r_dest", [128, 32], I32)
    gatec = load_const(p, "b6_gv", "r_gatek", [128, 32])
    wait_consts(p, "tensor", "vector", "gpsimd")
    zrow = p.sbuf("b6_z", [1, D], F32)
    p.do("vector", lambda e: e.memset(zrow[:], 0.0), inc="ve")
    p.wait("sync", "ve", p.val("ve"))
    p.do("sync", lambda e: e.dma_start(
        out=p.drams["moe_y"][8 * MOE_C:8 * MOE_C + 1, :], in_=zrow[:]), inc="b6_zs", n=16)
    p.wait("gpsimd", "b6_zs", p.val("b6_zs"))
    mo_t = p.sbuf("b6_mt", [128, D], F32)
    y1t = p.sbuf("b6_y1", [128, NCH, 128], F32)
    osum = p.sbuf("b6_os", [128, NCH, 128], F32)
    trpf = p.psum("b6_trp", [128, TC], F32)
    ydr, y2 = p.drams["moe_y"], p.drams["y2pre"]
    for tt in range(32):
        p.wait("gpsimd", "b6_mtc", p.val("b6_mtc"))
        p.do("gpsimd", lambda e, tt=tt: e.indirect_dma_start(
            out=mo_t[:], out_offset=None, in_=ydr[:],
            in_offset=IndirectOffsetOnAxis(dsti[:, tt:tt + 1], 0)), inc="b6_g", n=16)
        p.wait("sync", "ve", p.val("ve"))
        p.do("sync", lambda e, tt=tt: e.dma_start(
            out=y1t[:], in_=p.drams["y1_f32"][:, tt * 128:(tt + 1) * 128]
                .rearrange("(c p) t -> p c t", p=128)), inc="b6_yld", n=16)
        p.wait("vector", "b6_g", p.val("b6_g"))
        p.do("vector", lambda e, tt=tt: e.tensor_scalar_mul(
            mo_t[:], mo_t[:], gatec[:, tt:tt + 1]), inc="vb")
        p.wait("tensor", "vb", p.val("vb"))
        for k in range(NCH):
            p.wait("tensor", "b6_tev", p.val("b6_tev"))
            p.do("tensor", lambda e, k=k: e.transpose(
                trpf[:, 0:128], mo_t[:, k * 128:(k + 1) * 128], identf[:]), inc="mm2")
            p.wait("vector", "mm2", p.val("mm2"))
            if k == 0:
                p.wait("vector", "b6_yld", p.val("b6_yld"))
                p.wait("vector", "b6_st", p.val("b6_st"))
            p.do("vector", lambda e, k=k: e.tensor_add(
                osum[:, k, :], y1t[:, k, :], trpf[:, 0:128]),
                inc="b6_tev" if k < NCH - 1 else "ve")
            if k == NCH - 1:
                p.do("vector", lambda e: e.engine_nop(), inc="b6_tev")
            p.wait("tensor", "b6_tev", p.val("b6_tev"))
        p.do("vector", lambda e: e.engine_nop(), inc="b6_mtc")
        p.wait("sync", "ve", p.val("ve"))
        p.do("sync", lambda e, tt=tt: e.dma_start(
            out=y2[:, tt * 128:(tt + 1) * 128].rearrange("(c p) t -> p c t", p=128),
            in_=osum[:]), inc="b6_st", n=16)
    p.wait("sync", "b6_st", p.val("b6_st"))


def st_combine(p, tag, x_name, s_name, g_name, out_name):
    xt = p.sbuf(f"{tag}_x", [128, NCH, TC], F32)
    st_ = p.sbuf(f"{tag}_s", [128, NCH, TC], F32)
    gt = p.sbuf(f"{tag}_g", [128, NCH, TC], F32)
    ot = p.sbuf(f"{tag}_o", [128, NCH, TC], F32)
    stbase = p.val(f"{tag}_st")
    for tc in range(NTC):
        if tc >= 1:
            p.wait("sync", f"{tag}_st", stbase + 16 * tc)
        for nm_, tile in ((x_name, xt), (s_name, st_), (g_name, gt)):
            p.do("sync", lambda e, nm_=nm_, tile=tile, tc=tc: e.dma_start(
                out=tile[:], in_=p.drams[nm_][:, tc * TC:(tc + 1) * TC]
                    .rearrange("(c p) t -> p c t", p=128)), inc=f"{tag}_ld", n=16)
        p.wait("vector", f"{tag}_ld", p.val(f"{tag}_ld"))
        if tc >= 1:
            p.wait("vector", f"{tag}_st", stbase + 16 * tc)
        for k in range(NCH):
            p.do("vector", lambda e, k=k: e.tensor_mul(ot[:, k, :], st_[:, k, :], gt[:, k, :]))
            p.do("vector", lambda e, k=k: e.tensor_add(ot[:, k, :], ot[:, k, :], xt[:, k, :]),
                 inc=("ve" if k == NCH - 1 else None))
        p.wait("sync", "ve", p.val("ve"))
        p.do("sync", lambda e, tc=tc: e.dma_start(
            out=p.drams[out_name][:, tc * TC:(tc + 1) * TC]
                .rearrange("(c p) t -> p c t", p=128), in_=ot[:]), inc=f"{tag}_st", n=16)
    p.wait("sync", f"{tag}_st", p.val(f"{tag}_st"))



FF2 = 2816


def st_l1_kv(p):
    x_sb, ldc = st_load_stream(p, "d2kv", p.drams["xn1_bf"])
    p.wait("tensor", "d2kv_xld", ldc)
    ident = load_const(p, "d2kv_id", "c_ident_bf", [128, 128], BF16)
    wait_consts(p, "tensor", "scalar")
    ev = [p.sbuf(f"d2kv_ev{i}", [128, TC], BF16) for i in range(2)]
    evt = [p.sbuf(f"d2kv_evt{i}", [128, TC], BF16) for i in range(2)]
    psb = p.psum("d2kv_trp", [128, 1024], BF16)

    def epi_k(mo, tc, ps, mwid, par=0):
        ev_ = ev[par]
        stp = f"d2kvst{par}"
        p.wait("scalar", "mm", p.val("mm"))
        p.wait("scalar", stp, p.val(stp))
        p.do("scalar", lambda e, ev_=ev_: e.activation(out=ev_[:], in_=ps[:], func=AF.Identity),
             inc="d2kvk_ev")
        p.wait("sync", "d2kvk_ev", p.val("d2kvk_ev"))
        p.do("sync", lambda e, tc=tc, ev_=ev_: e.dma_start(
            out=p.drams["l1_k"][:, tc * TC:(tc + 1) * TC], in_=ev_[:]), inc=stp, n=16)

    def epi_v(mo, tc, ps, mwid, par=0):
        ev_, evt_ = ev[par], evt[par]
        stp = f"d2kvst{par}"
        p.wait("scalar", "mm", p.val("mm"))
        p.wait("scalar", stp, p.val(stp))
        p.do("scalar", lambda e, ev_=ev_: e.activation(out=ev_[:], in_=ps[:], func=AF.Identity),
             inc="d2kvv_ev")
        p.wait("tensor", "d2kvv_ev", p.val("d2kvv_ev"))
        p.wait("tensor", "ve2", p.val("ve2"))
        for j in range(4):
            p.do("tensor", lambda e, j=j, ev_=ev_: e.transpose(
                psb[:, j * 128:(j + 1) * 128], ev_[:, j * 128:(j + 1) * 128], ident[:]),
                inc=("mm2" if j == 3 else None))
        p.wait("vector", "mm2", p.val("mm2"))
        p.wait("vector", stp, p.val(stp))
        for j in range(4):
            p.do("vector", lambda e, j=j, evt_=evt_: e.tensor_copy(
                evt_[:, j * 128:(j + 1) * 128], psb[:, j * 128:(j + 1) * 128]),
                inc=("ve2" if j == 3 else None))
        p.wait("sync", "ve2", p.val("ve2"))
        for j in range(4):
            p.do("sync", lambda e, tc=tc, j=j, evt_=evt_: e.dma_start(
                out=p.drams["l1_v"][0, tc * TC + j * 128:tc * TC + (j + 1) * 128, :],
                in_=evt_[:, j * 128:(j + 1) * 128]), inc=stp, n=16)

    emit_proj(p, "d2kvk", x_sb, NCH, p.drams["w_l1a_k"], 128, epi_k)
    emit_proj(p, "d2kvv", x_sb, NCH, p.drams["w_l1a_v"], 128, epi_v)
    p.wait("sync", "d2kvst0", p.val("d2kvst0"))
    p.wait("sync", "d2kvst1", p.val("d2kvst1"))


def st_l1f_qkv(p, i, x_bf_name):
    x_sb, ldc = st_load_stream(p, "ea", p.drams[x_bf_name], shift_half=True)
    p.wait("tensor", "ea_xld", ldc)
    ident = load_const(p, "ea_id", "c_ident_bf", [128, 128], BF16)
    hmask = load_const(p, "ea_hm", "c_halfmask", [128, 2])
    hsel = load_const(p, "ea_hs", "c_halfsel", [2, 128])
    qsl = p.sbuf("ea_qs", [128, 1], F32)
    ksl = p.sbuf("ea_ks", [128, 1], F32)
    p.do("sync", lambda e: e.dma_start(
        out=qsl[:], in_=p.drams["s_l1f_q"][i * 128:(i + 1) * 128, :]), inc="cld", n=16)
    p.do("sync", lambda e: e.dma_start(
        out=ksl[:], in_=p.drams["s_l1f_k"][i * 128:(i + 1) * 128, :]), inc="cld", n=16)
    tabs = p.sbuf("ea_tab", [128, 4, T], F32)
    for ti, nm_ in enumerate(["c_rot_cs", "c_rot_ss", "c_rot_ci", "c_rot_si"]):
        p.do("sync", lambda e, ti=ti, nm_=nm_: e.dma_start(
            out=tabs[:, ti, :], in_=p.drams[nm_][:]), inc="cld", n=16)
    wait_consts(p, "vector", "tensor", "scalar")
    qf = p.sbuf("ea_qf", [128, TC], F32)
    q2 = p.sbuf("ea_q2", [128, TC], F32)
    r2 = p.sbuf("ea_r2", [2, TC], F32)
    t1 = p.sbuf("ea_t1", [128, TC], F32)
    rot = p.sbuf("ea_rot", [128, TC], F32)
    m1 = p.sbuf("ea_m1", [128, TC], F32)
    ev = [p.sbuf(f"ea_ev{i}", [128, TC], BF16) for i in range(2)]
    evt = [p.sbuf(f"ea_evt{i}", [128, TC], BF16) for i in range(2)]
    ps2 = p.psum("ea_ps2", [2, TC], F32)
    psb128 = p.psum("ea_psb", [128, TC], F32)
    psbt = p.psum("ea_trp", [128, 1024], BF16)

    def epi(mo, tc, ps, mwid, par=0):
        grp, mo8 = mo // 8, mo % 8
        ev_, evt_ = ev[par], evt[par]
        stp = f"east{par}"
        p.wait("scalar", "mm", p.val("mm"))
        if grp == 2:
            p.wait("scalar", stp, p.val(stp))
            p.do("scalar", lambda e, ev_=ev_: e.activation(out=ev_[:], in_=ps[:], func=AF.Identity),
                 inc="ea_ev")
            p.wait("tensor", "ea_ev", p.val("ea_ev"))
            p.wait("tensor", "ve2", p.val("ve2"))
            for j in range(4):
                p.do("tensor", lambda e, j=j, ev_=ev_: e.transpose(
                    psbt[:, j * 128:(j + 1) * 128], ev_[:, j * 128:(j + 1) * 128], ident[:]),
                    inc=("mm2" if j == 3 else None))
            p.wait("vector", "mm2", p.val("mm2"))
            p.wait("vector", stp, p.val(stp))
            for j in range(4):
                p.do("vector", lambda e, j=j, evt_=evt_: e.tensor_copy(
                    evt_[:, j * 128:(j + 1) * 128], psbt[:, j * 128:(j + 1) * 128]),
                    inc=("ve2" if j == 3 else None))
            p.wait("sync", "ve2", p.val("ve2"))
            for j in range(4):
                for hh in range(2):
                    p.do("sync", lambda e, mo8=mo8, tc=tc, j=j, hh=hh, evt_=evt_: e.dma_start(
                        out=p.drams["l1f_v"][2 * mo8 + hh,
                                             tc * TC + j * 128:tc * TC + (j + 1) * 128, :],
                        in_=evt_[:, j * 128 + hh * 64:j * 128 + hh * 64 + 64]),
                        inc=stp, n=16)
            return
        p.wait("scalar", "eaqf", p.val("eaqf"))
        p.do("scalar", lambda e: e.activation(out=qf[:], in_=ps[:], func=AF.Identity),
             inc="ea_ev")
        p.do("scalar", lambda e: e.activation(out=q2[:], in_=qf[:], func=AF.Square), inc="ac")
        p.wait("tensor", "ac", p.val("ac"))
        p.wait("tensor", "vb", p.val("vb"))
        p.do("tensor", lambda e: e.matmul(ps2[:], hmask[:], q2[:], start=True, stop=True),
             inc="mm2")
        p.wait("scalar", "mm2", p.val("mm2"))
        p.do("scalar", lambda e: e.activation(out=r2[:], in_=ps2[:], func=AF.Sqrt), inc="ac")
        p.wait("vector", "ac", p.val("ac"))
        p.do("vector", lambda e: e.tensor_scalar_max(r2[:], r2[:], 1e-12), inc="vb")
        p.wait("vector", "vb", p.val("vb"))
        p.do("vector", lambda e: e.reciprocal(r2[:], r2[:]), inc="vb")
        p.wait("tensor", "vb", p.val("vb"))
        p.wait("tensor", "eaqf", p.val("eaqf"))
        p.do("tensor", lambda e: e.matmul(psb128[:], hsel[:], r2[:], start=True, stop=True),
             inc="mm2")
        p.wait("vector", "mm2", p.val("mm2"))
        p.do("vector", lambda e: e.tensor_mul(t1[:], qf[:], psb128[:]), inc="vb")
        p.wait("vector", "vb", p.val("vb"))
        sv = qsl if grp == 0 else ksl
        p.do("vector", lambda e, sv=sv: e.tensor_scalar_mul(t1[:], t1[:], sv[:]), inc="eaqf")
        p.wait("vector", "eaqf", p.val("eaqf"))
        for (a, b_) in ((0, 32), (32, 0), (64, 96), (96, 64)):
            p.do("vector", lambda e, a=a, b_=b_: e.tensor_copy(
                rot[a:a + 32, :], t1[b_:b_ + 32, :]))
        tq = 0 if grp == 0 else 2
        p.do("vector", lambda e, tq=tq, tc=tc: e.tensor_mul(
            m1[:], t1[:], tabs[:, tq, tc * TC:(tc + 1) * TC]), inc="vb")
        p.wait("vector", "vb", p.val("vb"))
        p.do("vector", lambda e, tq=tq, tc=tc: e.tensor_mul(
            rot[:], rot[:], tabs[:, tq + 1, tc * TC:(tc + 1) * TC]), inc="vb")
        p.wait("vector", "vb", p.val("vb"))
        p.wait("vector", stp, p.val(stp))
        p.do("vector", lambda e: e.tensor_add(m1[:], m1[:], rot[:]), inc="vb")
        p.wait("vector", "vb", p.val("vb"))
        p.do("vector", lambda e, ev_=ev_: e.tensor_copy(ev_[:], m1[:]), inc="ve2x")
        p.wait("sync", "ve2x", p.val("ve2x"))
        dst = p.drams["l1f_q"] if grp == 0 else p.drams["l1f_k"]
        p.do("sync", lambda e, dst=dst, mo8=mo8, tc=tc, ev_=ev_: e.dma_start(
            out=dst[mo8 * 128:(mo8 + 1) * 128, tc * TC:(tc + 1) * TC], in_=ev_[:]),
            inc=stp, n=16)

    emit_proj(p, "ea", x_sb, NCH, p.drams["w_l1f_qkv"], 3 * D, epi, w_row0=i * D)
    p.wait("sync", "east0", p.val("east0"))
    p.wait("sync", "east1", p.val("east1"))


def st_l1f_lattn(p):
    ident = load_const(p, "eb_id", "c_ident_bf", [128, 128], BF16)
    lmask = p.sbuf("eb_lm", [128, 4, 512], F32)
    p.do("sync", lambda e: e.dma_start(
        out=lmask[:], in_=p.drams["c_lmask"].rearrange("v p s -> p v s")), inc="cld", n=16)
    wait_consts(p, "tensor", "vector")
    q_sb = p.sbuf("eb_q", [64, T], BF16)
    k_sb = p.sbuf("eb_k", [64, T], BF16)
    v_sb = p.sbuf("eb_v", [128, 32, 65], BF16)
    S_sb = p.sbuf("eb_S", [128, 512], F32)
    P_sb = p.sbuf("eb_P", [128, 512], BF16)
    PT_sb = p.sbuf("eb_pt", [128, 2, 128], BF16)
    AVc = p.sbuf("eb_avc", [128, 65], F32)
    O32 = p.sbuf("eb_o32", [128, 64], F32)
    Obf = p.sbuf("eb_obf", [128, 64], BF16)
    OT_sb = [p.sbuf(f"eb_ot{i}", [64, 128], BF16) for i in range(2)]
    nm = p.sbuf("eb_nm", [128, 1], F32)
    nmr = p.sbuf("eb_nmr", [128, 1], F32)
    S_ps = p.psum("eb_sp", [128, TC], F32)
    PT_ps = [p.psum(f"eb_ptp{x}", [128, 1024], BF16) for x in range(2)]
    AV_ps = p.psum("eb_av", [128, TC], F32)
    OT_ps = p.psum("eb_otp", [128, 1024], BF16)
    o_fm = p.drams["l1f_o"]
    hdbase = p.val("eb_hd")
    for h in range(16):
        if h > 0:
            p.wait("sync", "eb_hd", hdbase + h)
        p.do("sync", lambda e, h=h: e.dma_start(
            out=q_sb[:], in_=p.drams["l1f_q"][h * 64:(h + 1) * 64, :]), inc="eb_ld", n=16)
        p.do("sync", lambda e, h=h: e.dma_start(
            out=k_sb[:], in_=p.drams["l1f_k"][h * 64:(h + 1) * 64, :]), inc="eb_ld", n=16)
        p.do("sync", lambda e, h=h: e.dma_start(
            out=v_sb[:, :, 0:64],
            in_=p.drams["l1f_v"][h].rearrange("(tt p) d -> p tt d", p=128)), inc="eb_ld", n=16)
        p.wait("vector", "eb_ld", p.val("eb_ld"))
        p.do("vector", lambda e: e.memset(v_sb[:, :, 64:65], 1.0), inc="ve")
        p.wait("tensor", "eb_ld", p.val("eb_ld"))
        p.wait("tensor", "ve", p.val("ve"))
        for w in range(16):
            for qsub in range(2):
                q0c = w * 256 + qsub * 128
                win0 = (w == 0)
                s_lo = 256 if win0 else 0
                kt0 = (w * 256 - 256) + s_lo
                nsl = 512 - s_lo
                p.wait("tensor", "ac", p.val("ac"))
                p.do("tensor", lambda e, q0c=q0c, kt0=kt0, nsl=nsl: e.matmul(
                    S_ps[:, 0:nsl], q_sb[:, q0c:q0c + 128], k_sb[:, kt0:kt0 + nsl],
                    start=True, stop=True), inc="mm")
                p.wait("scalar", "mm", p.val("mm"))
                p.wait("scalar", "vb", p.val("vb"))
                p.do("scalar", lambda e, s_lo=s_lo, nsl=nsl: e.activation(
                    out=S_sb[:, s_lo:512], in_=S_ps[:, 0:nsl], func=AF.Identity,
                    scale=8.0), inc="ac")
                p.wait("vector", "ac", p.val("ac"))
                mv = qsub * 2 + (1 if win0 else 0)
                p.do("vector", lambda e, mv=mv, s_lo=s_lo: e.tensor_add(
                    S_sb[:, s_lo:512], S_sb[:, s_lo:512], lmask[:, mv, s_lo:512]))
                p.do("vector", lambda e, s_lo=s_lo: e.reduce_max(
                    nm[:], S_sb[:, s_lo:512], axis=AX.X, negate=True), inc="vb")
                p.wait("scalar", "vb", p.val("vb"))
                p.wait("scalar", "mm3", p.val("mm3"))
                p.do("scalar", lambda e, s_lo=s_lo: e.activation(
                    out=P_sb[:, s_lo:512], in_=S_sb[:, s_lo:512], func=AF.Exp,
                    bias=nm[:]), inc="ac")
                p.wait("tensor", "ac", p.val("ac"))
                jlist = list(range(s_lo // 128, 4))
                for ji, j in enumerate(jlist):
                    tb = j % 2
                    if ji >= 2:
                        p.wait("tensor", "ve2", p.val("ve2"))
                    p.do("tensor", lambda e, j=j, tb=tb: e.transpose(
                        PT_ps[tb][:, 0:128], P_sb[:, j * 128:(j + 1) * 128], ident[:]),
                        inc="mm2")
                    p.wait("vector", "mm2", p.val("mm2"))
                    p.do("vector", lambda e, tb=tb: e.tensor_copy(
                        PT_sb[:, tb, :], PT_ps[tb][:, 0:128]), inc="ve2")
                    p.wait("tensor", "ve2", p.val("ve2"))
                    vtile = 2 * w - 2 + j
                    p.do("tensor", lambda e, tb=tb, vtile=vtile, ji=ji, last=(ji == len(jlist) - 1): e.matmul(
                        AV_ps[:, 0:65], PT_sb[:, tb, :], v_sb[:, vtile, :],
                        start=(ji == 0), stop=last),
                        inc=("mm3" if ji == len(jlist) - 1 else None))
                p.wait("vector", "mm3", p.val("mm3"))
                p.do("vector", lambda e: e.tensor_copy(AVc[:], AV_ps[:, 0:65]), inc="vb")
                p.wait("vector", "vb", p.val("vb"))
                p.do("vector", lambda e: e.reciprocal(nmr[:], AVc[:, 64:65]), inc="vb")
                p.wait("vector", "vb", p.val("vb"))
                p.do("vector", lambda e: e.tensor_scalar_mul(O32[:], AVc[:, 0:64], nmr[:]), inc="vb")
                p.wait("vector", "vb", p.val("vb"))
                p.do("vector", lambda e: e.tensor_copy(Obf[:], O32[:]), inc="ve")
                p.wait("tensor", "ve", p.val("ve"))
                p.do("tensor", lambda e: e.transpose(OT_ps[0:64, 0:128], Obf[:], ident[:]),
                     inc="mm2")
                p.wait("vector", "mm2", p.val("mm2"))
                spar = qsub
                p.wait("vector", f"eb_st{spar}", p.val(f"eb_st{spar}"))
                p.do("vector", lambda e, spar=spar: e.tensor_copy(
                    OT_sb[spar][:], OT_ps[0:64, 0:128]), inc="ve3")
                p.wait("sync", "ve3", p.val("ve3"))
                p.do("sync", lambda e, h=h, q0c=q0c, spar=spar: e.dma_start(
                    out=o_fm[h * 64:(h + 1) * 64, q0c:q0c + 128], in_=OT_sb[spar][:]),
                    inc=f"eb_st{spar}", n=16)
        p.do("sync", lambda e: e.nop(), inc="eb_hd")
    p.wait("sync", "eb_st0", p.val("eb_st0"))
    p.wait("sync", "eb_st1", p.val("eb_st1"))


def st_l1f_ffn(p, i, xn_name, res_name, out_f32_name, out_bf_name):
    w1a = p.sbuf("ee_w1a", [128, NCH, FF2], BF16)
    w1g = p.sbuf("ee_w1g", [128, NCH, FF2], BF16)
    w2 = p.sbuf("ee_w2", [128, FF2 // 128, 1024], BF16)
    p.do("gpsimd", lambda e: e.dma_start(
        out=w1a[:], in_=p.drams["w_l1f_1a"][i * D:(i + 1) * D, :]
            .rearrange("(c k) m -> k c m", k=128)), inc="ee_wld", n=16)
    p.do("gpsimd", lambda e: e.dma_start(
        out=w1g[:], in_=p.drams["w_l1f_1g"][i * D:(i + 1) * D, :]
            .rearrange("(c k) m -> k c m", k=128)), inc="ee_wld", n=16)
    p.do("gpsimd", lambda e: e.dma_start(
        out=w2[:], in_=p.drams["w_l1f_2p"][i * FF2:(i + 1) * FF2, :]
            .rearrange("(c k) m -> k c m", k=128)), inc="ee_wld", n=16)
    b1a = load_const(p, "ee_b1a", "b_l1f_1a", [128, 22], rows=(i * 128, (i + 1) * 128))
    b1g = load_const(p, "ee_b1g", "b_l1f_1g", [128, 22], rows=(i * 128, (i + 1) * 128))
    b2 = load_const(p, "ee_b2", "b_l1f_2", [128, 8], rows=(i * 128, (i + 1) * 128))
    wait_consts(p, "scalar")
    p.wait("tensor", "ee_wld", p.val("ee_wld"))
    xt = [p.sbuf(f"ee_x{x}", [128, NCH, TC], BF16) for x in range(2)]
    hh = p.sbuf("ee_hh", [128, FF2 // 128, TC], BF16)
    qa = [p.sbuf(f"ee_qa{i}", [128, TC], F32) for i in range(2)]
    gg = [p.sbuf(f"ee_gg{i}", [128, TC], F32) for i in range(2)]
    yv = [p.sbuf(f"ee_yv{i}", [128, TC], F32) for i in range(2)]
    ybf = [p.sbuf(f"ee_ybf{i}", [128, TC], BF16) for i in range(2)]
    rest = [p.sbuf(f"ee_res{i}", [128, TC], F32) for i in range(2)]
    psA = [p.psum(f"ee_psA{i}", [128, TC], F32) for i in range(2)]
    psB = [p.psum(f"ee_psB{i}", [128, TC], F32) for i in range(2)]
    psY = [p.psum(f"ee_psY{i}", [128, TC], F32) for i in range(2)]
    mm_done_tile = []
    for tc in range(NTC):
        xb = tc % 2
        if tc >= 2:
            p.wait("sync", "mm", mm_done_tile[tc - 2])
        p.do("sync", lambda e, tc=tc, xb=xb: e.dma_start(
            out=xt[xb][:], in_=p.drams[xn_name][:, tc * TC:(tc + 1) * TC]
                .rearrange("(c p) t -> p c t", p=128)), inc="ee_xld", n=16)
        p.wait("tensor", "ee_xld", p.val("ee_xld"))
        if tc >= 1:
            p.wait("vector", "mm", mm_done_tile[tc - 1])
        for mo in range(FF2 // 128):
            mp = mo % 2
            p.wait("tensor", f"eeh{mp}", p.val(f"eeh{mp}"))
            for k in range(NCH):
                p.do("tensor", lambda e, k=k, mo=mo, xb=xb, mp=mp: e.matmul(
                    psA[mp][:], w1a[:, k, mo * 128:(mo + 1) * 128], xt[xb][:, k, :],
                    start=(k == 0), stop=(k == NCH - 1)))
            for k in range(NCH):
                p.do("tensor", lambda e, k=k, mo=mo, xb=xb, mp=mp: e.matmul(
                    psB[mp][:], w1g[:, k, mo * 128:(mo + 1) * 128], xt[xb][:, k, :],
                    start=(k == 0), stop=(k == NCH - 1)),
                    inc=("mm" if k == NCH - 1 else None))
            p.wait("scalar", "mm", p.val("mm"))
            p.do("scalar", lambda e, mo=mo, mp=mp: e.activation(
                out=qa[mp][:], in_=psA[mp][:], func=AF.Identity, bias=b1a[:, mo:mo + 1]), inc="ac")
            p.do("scalar", lambda e, mo=mo, mp=mp: e.activation(
                out=gg[mp][:], in_=psB[mp][:], func=AF.Gelu, bias=b1g[:, mo:mo + 1]), inc=f"eeq{mp}")
            p.wait("vector", f"eeq{mp}", p.val(f"eeq{mp}"))
            p.do("vector", lambda e, mo=mo, mp=mp: e.tensor_tensor(
                hh[:, mo, :], qa[mp][:], gg[mp][:], ALU.mult), inc=f"eeh{mp}")
        for mp in range(2):
            p.wait("tensor", f"eeh{mp}", p.val(f"eeh{mp}"))
        for mo2 in range(NCH):
            mp = mo2 % 2
            p.wait("tensor", f"eey{mp}", p.val(f"eey{mp}"))
            for k in range(FF2 // 128):
                p.do("tensor", lambda e, k=k, mo2=mo2, mp=mp: e.matmul(
                    psY[mp][:], w2[:, k, mo2 * 128:(mo2 + 1) * 128], hh[:, k, :],
                    start=(k == 0), stop=(k == FF2 // 128 - 1)),
                    inc=("mm" if k == FF2 // 128 - 1 else None))
            p.wait("sync", f"eest{mp}", p.val(f"eest{mp}"))
            p.do("sync", lambda e, mo2=mo2, tc=tc, mp=mp: e.dma_start(
                out=rest[mp][:], in_=p.drams[res_name][mo2 * 128:(mo2 + 1) * 128,
                                                       tc * TC:(tc + 1) * TC]), inc="ee_rld", n=16)
            p.wait("scalar", "mm", p.val("mm"))
            p.wait("scalar", f"eest{mp}", p.val(f"eest{mp}"))
            p.do("scalar", lambda e, mo2=mo2, mp=mp: e.activation(
                out=yv[mp][:], in_=psY[mp][:], func=AF.Identity, bias=b2[:, mo2:mo2 + 1]), inc=f"eey{mp}")
            p.wait("vector", f"eey{mp}", p.val(f"eey{mp}"))
            p.wait("vector", "ee_rld", p.val("ee_rld"))
            p.do("vector", lambda e, mp=mp: e.tensor_add(yv[mp][:], yv[mp][:], rest[mp][:]), inc="vb")
            p.wait("vector", "vb", p.val("vb"))
            p.do("vector", lambda e, mp=mp: e.tensor_copy(ybf[mp][:], yv[mp][:]), inc="ve")
            p.wait("sync", "ve", p.val("ve"))
            p.do("sync", lambda e, mo2=mo2, tc=tc, mp=mp: e.dma_start(
                out=p.drams[out_f32_name][mo2 * 128:(mo2 + 1) * 128, tc * TC:(tc + 1) * TC],
                in_=yv[mp][:]), inc=f"eest{mp}", n=16)
            p.do("sync", lambda e, mo2=mo2, tc=tc, mp=mp: e.dma_start(
                out=p.drams[out_bf_name][mo2 * 128:(mo2 + 1) * 128, tc * TC:(tc + 1) * TC],
                in_=ybf[mp][:]), inc=f"eest{mp}", n=16)
        mm_done_tile.append(p.val("mm"))
    p.wait("sync", "eest0", p.val("eest0"))
    p.wait("sync", "eest1", p.val("eest1"))


def st_silu_combine(p, tag, x_name, s_name, g_name, out_name):
    xt = p.sbuf(f"{tag}_x", [128, NCH, TC], F32)
    st_ = p.sbuf(f"{tag}_s", [128, NCH, TC], F32)
    gt = p.sbuf(f"{tag}_g", [128, NCH, TC], F32)
    ot = p.sbuf(f"{tag}_o", [128, NCH, TC], F32)
    stbase = p.val(f"{tag}_st")
    for tc in range(NTC):
        if tc >= 1:
            p.wait("sync", f"{tag}_st", stbase + 16 * tc)
        for nm_, tile in ((x_name, xt), (s_name, st_), (g_name, gt)):
            p.do("sync", lambda e, nm_=nm_, tile=tile, tc=tc: e.dma_start(
                out=tile[:], in_=p.drams[nm_][:, tc * TC:(tc + 1) * TC]
                    .rearrange("(c p) t -> p c t", p=128)), inc=f"{tag}_ld", n=16)
        p.wait("scalar", f"{tag}_ld", p.val(f"{tag}_ld"))
        if tc >= 1:
            p.wait("scalar", f"{tag}_st", stbase + 16 * tc)
        for k in range(NCH):
            p.do("scalar", lambda e, k=k: e.activation(
                out=gt[:, k, :], in_=gt[:, k, :], func=AF.Silu),
                inc=("ac" if k == NCH - 1 else None))
        p.wait("vector", "ac", p.val("ac"))
        p.wait("vector", f"{tag}_ld", p.val(f"{tag}_ld"))
        if tc >= 1:
            p.wait("vector", f"{tag}_st", stbase + 16 * tc)
        for k in range(NCH):
            p.do("vector", lambda e, k=k: e.tensor_mul(ot[:, k, :], st_[:, k, :], gt[:, k, :]))
            p.do("vector", lambda e, k=k: e.tensor_add(ot[:, k, :], ot[:, k, :], xt[:, k, :]),
                 inc=("ve" if k == NCH - 1 else None))
        p.wait("sync", "ve", p.val("ve"))
        p.do("sync", lambda e, tc=tc: e.dma_start(
            out=p.drams[out_name][:, tc * TC:(tc + 1) * TC]
                .rearrange("(c p) t -> p c t", p=128), in_=ot[:]), inc=f"{tag}_st", n=16)
    p.wait("sync", f"{tag}_st", p.val(f"{tag}_st"))



MID_SPECS = {
    "x_fm": ([D, T], F32),
    "xn_f32": ([D, T], F32),
    "xn_bf": ([D, T], BF16),
    "l0_q": ([D, T], BF16),
    "l0_k": ([D, T], BF16),
    "l0_v": ([16, T, 64], BF16),
    "l0_o": ([D, T], BF16),
    "s_fm": ([D, T], F32),
    "q0_tm": ([T, D], BF16),
    "k0_tm": ([T, D], BF16),
    "v0_tm": ([T, D], BF16),
    "k0_all": ([NCORE, T, D], BF16),
    "v0_all": ([NCORE, T, D], BF16),
    "o0_fm": ([D, T], BF16),
    "y1pre": ([D, T], F32),
    "y1_f32": ([D, T], F32),
    "y1_bf": ([D, T], BF16),
    "y1_tm": ([T, D], BF16),
    "r_hist": ([1, 8], F32),
    "hist_all": ([NCORE, 8], F32),
    "r_loc": ([128, 32], F32),
    "r_idx": ([128, 32], F32),
    "r_gate": ([128, 32], F32),
    "r_dest": ([128, 32], I32),
    "moe_disp": ([8 * MOE_C + 1, D], BF16),
    "moe_y": ([8 * MOE_C + 1, D], F32),
    "r_gatek": ([128, 32], F32),
    "y2pre": ([D, T], F32),
    "g_fm": ([D, T], F32),
    "x1_fm": ([D, T], F32),
    "xn1_f32": ([D, T], F32),
    "xn1_bf": ([D, T], BF16),
    "l1_k": ([128, T], BF16),
    "l1_v": ([1, T, 128], BF16),
    "l1_q": ([D, T], BF16),
    "l1_o": ([D, T], BF16),
    "s1_fm": ([D, T], F32),
    "l1f_q": ([D, T], BF16),
    "l1f_k": ([D, T], BF16),
    "l1f_v": ([16, T, 64], BF16),
    "l1f_o": ([D, T], BF16),
    "l1mid_f32": ([D, T], F32),
    "l1mid_bf": ([D, T], BF16),
    "l1ln_bf": ([D, T], BF16),
    "l1x_a_f32": ([D, T], F32),
    "l1x_a_bf": ([D, T], BF16),
    "l1x_b_f32": ([D, T], F32),
    "l1x_b_bf": ([D, T], BF16),
    "out_fm": ([D, T], F32),
}

REP_MAP = {"k0_all": "k0_tm", "v0_all": "v0_tm", "hist_all": "r_hist"}

_NP2MYBIR = {
    np.dtype(np.float32): F32,
    np.dtype(bfloat16): BF16,
    np.dtype(np.int32): I32,
}


class Launch:
    def __init__(self, name, builder, ins, outs):
        self.name, self.builder, self.ins, self.outs = name, builder, ins, outs


def stage_launches():
    L = []
    L.append(Launch("a1", lambda p: st_ln(p, "a1", p.drams["x_fm"],
                                          p.drams["xn_f32"], p.drams["xn_bf"]),
                    ["x_fm"], ["xn_f32", "xn_bf"]))
    L.append(Launch("a2", st_l0_qkv,
                    ["xn_bf", "w_l0a_q", "w_l0a_k", "w_l0a_v",
                     "b_l0a_q", "b_l0a_k", "b_l0a_v", "c_ident_bf"],
                    ["l0_q", "l0_k", "l0_v"]))
    L.append(Launch("a3", lambda p: st_causal_attn(
                        p, "a3", p.drams["l0_q"], p.drams["l0_k"], p.drams["l0_v"],
                        p.drams["l0_o"], 16, 64, 0.125, "c_mask0", k_dram_rows=1024),
                    ["l0_q", "l0_k", "l0_v", "c_ident_bf", "c_mask0"], ["l0_o"]))
    L.append(Launch("a4", lambda p: st_proj_simple(
                        p, "a4", "l0_o", "w_l0a_o", D, "s_fm", "b_l0a_o", out_dtype="f32"),
                    ["l0_o", "w_l0a_o", "b_l0a_o"], ["s_fm"]))
    L.append(Launch("b1", st_l0f_qkv,
                    ["xn_bf", "w_l0f_qkv", "b_l0f_qkv", "c_ident_bf"],
                    ["q0_tm", "k0_tm", "v0_tm"]))
    L.append(Launch("b2", st_l0f_mha,
                    ["q0_tm", "k0_all", "v0_all", "c_ident_bf"], ["o0_fm"]))
    L.append(Launch("b2b", lambda p: st_proj_simple(
                        p, "b2b", "o0_fm", "w_l0f_o", D, "y1pre", "b_l0f_o",
                        out_dtype="f32", add_from="xn_f32"),
                    ["o0_fm", "w_l0f_o", "b_l0f_o", "xn_f32"], ["y1pre"]))
    L.append(Launch("b2c", lambda p: st_ln(p, "b2c", p.drams["y1pre"],
                                           p.drams["y1_f32"], p.drams["y1_bf"],
                                           g_name="g_l0f_1", b_name="bb_l0f_1",
                                           out_tm=p.drams["y1_tm"]),
                    ["y1pre", "g_l0f_1", "bb_l0f_1", "c_ident_bf"],
                    ["y1_f32", "y1_bf", "y1_tm"]))
    L.append(Launch("b3", st_moe_route_a,
                    ["y1_bf", "w_moe_g", "c_ltri", "c_iota8"],
                    ["r_loc", "r_idx", "r_gate", "r_hist"]))
    L.append(Launch("b4", st_moe_route_b,
                    ["y1_tm", "r_loc", "r_idx", "r_gate", "hist_all", "c_coresel", "c_iota8"],
                    ["moe_disp", "r_dest", "r_gatek"]))
    L.append(Launch("b5", st_moe_ffn,
                    ["moe_disp", "w_moe_1", "w_moe_2", "b_moe_1", "b_moe_2",
                     "c_ident_bf", "c_ident_f32"],
                    ["moe_y"]))
    L.append(Launch("b6", st_moe_gather,
                    ["moe_y", "y1_f32", "r_dest", "r_gatek", "c_ident_f32"],
                    ["y2pre"]))
    L.append(Launch("b7", lambda p: st_ln(p, "b7", p.drams["y2pre"],
                                          p.drams["g_fm"], None,
                                          g_name="g_l0f_2", b_name="bb_l0f_2",
                                          post_act=AF.Silu),
                    ["y2pre", "g_l0f_2", "bb_l0f_2"], ["g_fm"]))
    L.append(Launch("c1", lambda p: st_combine(p, "c1", "x_fm", "s_fm", "g_fm", "x1_fm"),
                    ["x_fm", "s_fm", "g_fm"], ["x1_fm"]))
    L.append(Launch("d1", lambda p: st_ln(p, "d1", p.drams["x1_fm"],
                                          p.drams["xn1_f32"], p.drams["xn1_bf"]),
                    ["x1_fm"], ["xn1_f32", "xn1_bf"]))
    L.append(Launch("d2kv", st_l1_kv,
                    ["xn1_bf", "w_l1a_k", "w_l1a_v", "c_ident_bf"], ["l1_k", "l1_v"]))
    L.append(Launch("d2q", lambda p: st_proj_simple(
                        p, "d2q", "xn1_bf", "w_l1a_q", D, "l1_q"),
                    ["xn1_bf", "w_l1a_q"], ["l1_q"]))
    L.append(Launch("d2c", lambda p: st_causal_attn(
                        p, "d2c", p.drams["l1_q"], p.drams["l1_k"], p.drams["l1_v"],
                        p.drams["l1_o"], 8, 128, 1.0 / 32.0, "c_maskinf", k_dram_rows=128),
                    ["l1_q", "l1_k", "l1_v", "c_ident_bf", "c_maskinf"], ["l1_o"]))
    L.append(Launch("d2d", lambda p: st_proj_simple(
                        p, "d2d", "l1_o", "w_l1a_p", D, "s1_fm", "b_l1a_p", out_dtype="f32"),
                    ["l1_o", "w_l1a_p", "b_l1a_p"], ["s1_fm"]))
    lat_ins = ["l1f_q", "l1f_k", "l1f_v", "c_ident_bf", "c_lmask"]
    qkv_ins = ["w_l1f_qkv", "s_l1f_q", "s_l1f_k", "c_ident_bf", "c_halfmask", "c_halfsel",
               "c_rot_cs", "c_rot_ss", "c_rot_ci", "c_rot_si"]
    for i in range(6):
        xin_f32 = "xn1_f32" if i == 0 else (f"l1x_{'a' if i % 2 == 1 else 'b'}_f32")
        xin_bf = "xn1_bf" if i == 0 else (f"l1x_{'a' if i % 2 == 1 else 'b'}_bf")
        xout_f32 = f"l1x_{'a' if i % 2 == 0 else 'b'}_f32"
        xout_bf = f"l1x_{'a' if i % 2 == 0 else 'b'}_bf"
        L.append(Launch(f"e{i}a", lambda p, i=i, xb=xin_bf: st_l1f_qkv(p, i, xb),
                        [xin_bf] + qkv_ins, ["l1f_q", "l1f_k", "l1f_v"]))
        L.append(Launch(f"e{i}b", st_l1f_lattn, lat_ins, ["l1f_o"]))
        L.append(Launch(f"e{i}c", lambda p, i=i, xf=xin_f32: st_proj_simple(
                            p, "ec", "l1f_o", "w_l1f_o", D, "l1mid_f32",
                            out_dtype="f32", add_from=xf, w_row0=i * D,
                            also_bf16="l1mid_bf"),
                        ["l1f_o", "w_l1f_o", xin_f32], ["l1mid_f32", "l1mid_bf"]))
        L.append(Launch(f"e{i}d", lambda p, i=i: st_ln(
                            p, "ed", p.drams["l1mid_f32"], None, p.drams["l1ln_bf"],
                            g_name="g_l1f", b_name="bb_l1f",
                            g_rows=(i * 128, (i + 1) * 128), shift_half=True),
                        ["l1mid_f32", "g_l1f", "bb_l1f"], ["l1ln_bf"]))
        L.append(Launch(f"e{i}e", lambda p, i=i, of=xout_f32, ob=xout_bf: st_l1f_ffn(
                            p, i, "l1ln_bf", "l1mid_f32", of, ob),
                        ["l1ln_bf", "l1mid_f32", "w_l1f_1a", "w_l1f_1g", "w_l1f_2p",
                         "b_l1f_1a", "b_l1f_1g", "b_l1f_2"],
                        [xout_f32, xout_bf]))
    L.append(Launch("c2", lambda p: st_silu_combine(
                        p, "c2", "x1_fm", "s1_fm", "l1x_b_f32", "out_fm"),
                    ["x1_fm", "s1_fm", "l1x_b_f32"], ["out_fm"]))
    return L


def run_pipeline(inputs, upto=None, mids_init=None):
    C = host_constants()
    W = host_weights(inputs)
    x = np.asarray(inputs["L0_attn_x"], dtype=np.float32)
    mids = {"x_fm": [np.ascontiguousarray(x[c].T) for c in range(NCORE)]}
    if mids_init:
        mids.update(mids_init)
    static = {}
    static.update(C)
    static.update(W)
    percore = {"c_coresel": []}
    for c in range(NCORE):
        sel = np.zeros((8, 1), np.float32)
        sel[:c, 0] = 1.0
        percore["c_coresel"].append(sel)
    for launch in stage_launches():
        if upto is not None and launch.name not in upto:
            continue
        p = Prog()
        for name in launch.ins:
            if name in static:
                a = static[name]
                p.dram(name, a.shape, _NP2MYBIR[a.dtype], kind="ExternalInput")
            elif name in percore:
                a = percore[name][0]
                p.dram(name, a.shape, _NP2MYBIR[a.dtype], kind="ExternalInput")
            else:
                shape, dt = MID_SPECS[name]
                p.dram(name, shape, dt, kind="ExternalInput")
        for name in launch.outs:
            shape, dt = MID_SPECS[name]
            p.dram(name, shape, dt, kind="ExternalOutput")
        launch.builder(p)
        nc = p.finalize()
        in_maps = []
        for c in range(NCORE):
            m = {}
            for name in launch.ins:
                if name in static:
                    m[name] = static[name]
                elif name in percore:
                    m[name] = percore[name][c]
                elif name in REP_MAP:
                    shape, dt = MID_SPECS[name]
                    npdt = {F32: np.float32, BF16: bfloat16, I32: np.int32}[dt]
                    m[name] = np.concatenate(
                        [np.asarray(a).reshape(1, -1) for a in mids[REP_MAP[name]]], 0
                    ).reshape(shape).astype(npdt, copy=False)
                else:
                    m[name] = mids[name][c]
            in_maps.append(m)
        res = run_bass_kernel_spmd(nc, in_maps, core_ids=list(range(NCORE)))
        for name in launch.outs:
            mids[name] = [res.results[c][name] for c in range(NCORE)]
    return mids


def emit_allgather(p, pairs):
    for src_name, dst_name in pairs:
        sap = p.drams[src_name][:]
        dap = p.drams[dst_name]
        if len(dap.shape) == 3:
            dap = dap.rearrange("m t d -> (m t) d")
        else:
            dap = dap[:]
        p.do("gpsimd", lambda e, sap=sap, dap=dap: e.collective_compute(
            "AllGather", ALU.bypass, replica_groups=[list(range(NCORE))],
            ins=[sap.opt()], outs=[dap.opt()]), inc="cc")
    p.wait("gpsimd", "cc", p.val("cc"))
    p.do("gpsimd", lambda e: e.nop(), inc="stg")
    v = p.val("stg")
    for eng in ENGINES:
        p.wait(eng, "stg", v)


GATHER_AFTER = {"b1": [("k0_tm", "k0_all"), ("v0_tm", "v0_all")],
                "b3": [("r_hist", "hist_all")]}

FUSED = True


def build_fused(static, debug_outs=()):
    p = Prog()
    for name, a in static.items():
        p.dram(name, a.shape, _NP2MYBIR[a.dtype], kind="ExternalInput")
    p.dram("c_coresel", [8, 1], F32, kind="ExternalInput")
    for name, (shape, dt) in MID_SPECS.items():
        if name == "x_fm":
            p.dram(name, shape, dt, kind="ExternalInput")
        elif name == "out_fm" or name in debug_outs:
            p.dram(name, shape, dt, kind="ExternalOutput")
        else:
            p.dram(name, shape, dt, kind="Internal")
    for launch in stage_launches():
        with p.region():
            launch.builder(p)
        p.barrier_all()
        if launch.name in GATHER_AFTER:
            emit_allgather(p, GATHER_AFTER[launch.name])
    return p.finalize()


def kernel(**inputs):
    if not FUSED:
        mids = run_pipeline(inputs)
        return np.stack([np.ascontiguousarray(np.asarray(mids["out_fm"][c], np.float32).T)
                         for c in range(NCORE)])
    C = host_constants()
    W = host_weights(inputs)
    static = {}
    static.update(C)
    static.update(W)
    nc = build_fused(static)
    x = np.asarray(inputs["L0_attn_x"], dtype=np.float32)
    in_maps = []
    for c in range(NCORE):
        sel = np.zeros((8, 1), np.float32)
        sel[:c, 0] = 1.0
        m = dict(static)
        m["c_coresel"] = sel
        m["x_fm"] = np.ascontiguousarray(x[c].T)
        in_maps.append(m)
    res = run_bass_kernel_spmd(nc, in_maps, core_ids=list(range(NCORE)))
    return np.stack([np.ascontiguousarray(np.asarray(res.results[c]["out_fm"], np.float32).T)
                     for c in range(NCORE)])


# version 06 (final)
# speedup vs baseline: 1.2979x; 1.0608x over previous
import contextlib
import numpy as np
import ml_dtypes
import concourse.bass as bass
import concourse.mybir as mybir
from concourse.bass_utils import run_bass_kernel_spmd
from concourse.bass_core_rust import IndirectOffsetOnAxis

F32 = mybir.dt.float32
BF16 = mybir.dt.bfloat16
I32 = mybir.dt.int32
AF = mybir.ActivationFunctionType
ALU = mybir.AluOpType
AX = mybir.AxisListType
bfloat16 = ml_dtypes.bfloat16

T = 4096
D = 1024
NCH = D // 128
TC = 512
NTC = T // TC
NCORE = 8
MOE_C = 1024
NEG0 = -100000.0
NEGINF = -1e30

ENGINES = ("tensor", "vector", "scalar", "sync", "gpsimd")


GLOBAL_SEMS = {"mm", "mm2", "mm3", "ve", "ve2", "ve3", "ve9", "vb", "ac", "acpa",
               "vpa", "st", "stg", "cc", "cld", "fcinit", "dbg"}


class Prog:
    def __init__(self):
        self.nc = bass.Bass("TRN2", target_bir_lowering=False)
        self.stack = contextlib.ExitStack()
        self.streams = {e: [] for e in ENGINES}
        self._sems = {}
        self._sem_counts = {}
        self._free_handles = []
        self._region_sems = []
        self._region_id = 0
        self.drams = {}
        self._regions = []
        self._name_ctr = 0

    def dram(self, name, shape, dtype, kind="Internal"):
        ap = self.nc.dram_tensor(name, list(shape), dtype, kind=kind).ap()
        self.drams[name] = ap
        return ap

    def _uname(self, name):
        self._name_ctr += 1
        return f"{name}__r{self._name_ctr}"

    def sbuf(self, name, shape, dtype):
        st = self._regions[-1] if self._regions else self.stack
        nm = self._uname(name) if self._regions else name
        return st.enter_context(self.nc.sbuf_tensor(nm, list(shape), dtype))

    def psum(self, name, shape, dtype=F32):
        st = self._regions[-1] if self._regions else self.stack
        nm = self._uname(name) if self._regions else name
        return st.enter_context(self.nc.psum_tensor(nm, list(shape), dtype))

    @contextlib.contextmanager
    def region(self):
        st = contextlib.ExitStack()
        self._regions.append(st)
        self._region_id += 1
        self._region_sems = []
        saved_fc = getattr(self, "_fconsts", {})
        self._fconsts = {}
        try:
            yield
        finally:
            self._regions.pop()
            st.close()
            self._fconsts = saved_fc
            for key in self._region_sems:
                self._free_handles.append(self._sems.pop(key))
            self._region_sems = []

    def barrier_all(self):
        self.do("sync", lambda e: e.nop(), inc="stg")
        v = self.val("stg")
        for eng in ENGINES:
            self.wait(eng, "stg", v)

    def _sem(self, name):
        key = name if (name in GLOBAL_SEMS or not self._regions) else (self._region_id, name)
        if key not in self._sems:
            if key != name and self._free_handles:
                h = self._free_handles.pop()
            else:
                h = self.stack.enter_context(self.nc.semaphore(str(key)))
                self._sem_counts[id(h)] = 0
            self._sems[key] = h
            if key != name:
                self._region_sems.append(key)
        return self._sems[key]

    def val(self, name):
        return self._sem_counts[id(self._sem(name))]

    def wait(self, engine, name, value=None):
        h = self._sem(name)
        v = self._sem_counts[id(h)] if value is None else value
        self.streams[engine].append(lambda eng, h=h, v=v: eng.wait_ge(h, v))

    def do(self, engine, fn, inc=None, n=1):
        if inc is None:
            self.streams[engine].append(fn)
        else:
            h = self._sem(inc)
            self._sem_counts[id(h)] += n
            self.streams[engine].append(
                lambda eng, fn=fn, h=h, n=n: fn(eng).then_inc(h, n))

    def fconst(self, val, dtype=F32):
        key = (val, dtype)
        if not hasattr(self, "_fconsts"):
            self._fconsts = {}
        if key not in self._fconsts:
            t = self.sbuf(f"fc_{len(self._fconsts)}", [128, 1], dtype)
            self.do("vector", lambda e, t=t, val=val: e.memset(t[:], val), inc="fcinit")
            self._fconsts[key] = t
        return self._fconsts[key]

    def finalize(self):
        with self.nc.Block() as blk:
            for e in ENGINES:
                fns = self.streams[e]
                if fns:
                    getattr(blk, e)(lambda eng, fns=fns: [f(eng) for f in fns])
        self.stack.close()
        return self.nc



def rotary_tables_np():
    DH, WINDOW = 64, 256
    f32 = np.float32
    half = np.arange(0, DH, 2, dtype=f32)
    inv_freq = (1.0 / 10000.0 ** (half / DH)).astype(f32)
    t = np.arange(T, dtype=f32)
    freqs = (t[:, None] * inv_freq[None, :]).astype(f32)
    freqs = np.concatenate([freqs, freqs], -1)
    base_scale = ((half + 0.4 * DH) / (1.4 * DH)).astype(f32)
    power = ((t - T // 2) / (WINDOW // 2)).astype(f32)
    scale = (base_scale[None, :] ** power[:, None]).astype(f32)
    scale = np.concatenate([scale, scale], -1)
    cos, sin = np.cos(freqs).astype(f32), np.sin(freqs).astype(f32)
    inv = (1.0 / scale).astype(f32)
    sgn = np.concatenate([-np.ones(32, f32), np.ones(32, f32)])
    CS = np.ascontiguousarray((cos * scale).T)
    SS = np.ascontiguousarray((sin * scale).T * sgn[:, None])
    CI = np.ascontiguousarray((cos * inv).T)
    SI = np.ascontiguousarray((sin * inv).T * sgn[:, None])
    dup = lambda A: np.ascontiguousarray(np.concatenate([A, A], axis=0))
    return dup(CS), dup(SS), dup(CI), dup(SI)


def host_constants():
    f32 = np.float32
    C = {}
    C["c_ident_bf"] = np.eye(128, dtype=bfloat16)
    C["c_ident_f32"] = np.eye(128, dtype=f32)
    C["c_ltri"] = np.triu(np.ones((128, 128), f32), 1)
    tri = np.triu(np.ones((128, 128), bool), 1)
    C["c_mask0"] = np.where(tri, f32(NEG0), f32(0)).astype(f32)
    C["c_maskinf"] = np.where(tri, f32(NEGINF), f32(0)).astype(f32)
    oi = np.arange(128)[:, None]
    s_ = np.arange(512)[None, :]
    masks = []
    for qsub in (0, 1):
        o = oi + qsub * 128
        band = (s_ < o) | (s_ > o + 256)
        masks.append(np.where(band, f32(NEGINF), f32(0)).astype(f32))
        masks.append(np.where(band | (s_ < 256), f32(NEGINF), f32(0)).astype(f32))
    C["c_lmask"] = np.ascontiguousarray(np.stack(masks))
    C["c_iota8"] = np.tile(np.arange(8, dtype=f32)[None, :], (128, 1))
    hm = np.zeros((128, 2), f32)
    hm[:64, 0] = 1.0
    hm[64:, 1] = 1.0
    C["c_halfmask"] = hm
    C["c_halfsel"] = np.ascontiguousarray(hm.T)
    CS, SS, CI, SI = rotary_tables_np()
    C["c_rot_cs"], C["c_rot_ss"], C["c_rot_ci"], C["c_rot_si"] = CS, SS, CI, SI
    return C


def vec_pm(v):
    n = v.shape[0] // 128
    return np.ascontiguousarray(v.reshape(n, 128).T)


def host_weights(I):
    W = {}
    W["w_l0a_q"] = np.ascontiguousarray(I["L0_attn_Q_w"].transpose(1, 0, 2).reshape(D, D))
    W["w_l0a_k"] = np.ascontiguousarray(I["L0_attn_K_w"].transpose(1, 0, 2).reshape(D, D))
    W["w_l0a_v"] = np.ascontiguousarray(I["L0_attn_V_w"].transpose(1, 0, 2).reshape(D, D))
    W["w_l0a_o"] = np.ascontiguousarray(I["L0_attn_O_w"].reshape(D, D))
    W["b_l0a_q"] = vec_pm(np.asarray(I["L0_attn_Q_b"]).reshape(-1))
    W["b_l0a_k"] = vec_pm(np.asarray(I["L0_attn_K_b"]).reshape(-1))
    W["b_l0a_v"] = vec_pm(np.asarray(I["L0_attn_V_b"]).reshape(-1))
    W["b_l0a_o"] = vec_pm(np.asarray(I["L0_attn_O_b"]))
    W["w_l0f_qkv"] = np.ascontiguousarray(I["L0_ffn_in_proj_w"].T)
    W["b_l0f_qkv"] = vec_pm(np.asarray(I["L0_ffn_in_proj_b"]))
    W["w_l0f_o"] = np.ascontiguousarray(I["L0_ffn_out_proj_w"].T)
    W["b_l0f_o"] = vec_pm(np.asarray(I["L0_ffn_out_proj_b"]))
    W["g_l0f_1"] = vec_pm(np.asarray(I["L0_ffn_ln1_g"]))
    W["bb_l0f_1"] = vec_pm(np.asarray(I["L0_ffn_ln1_b"]))
    W["g_l0f_2"] = vec_pm(np.asarray(I["L0_ffn_ln2_g"]))
    W["bb_l0f_2"] = vec_pm(np.asarray(I["L0_ffn_ln2_b"]))
    W["w_moe_g"] = np.ascontiguousarray(I["L0_ffn_wg"])
    W["w_moe_1"] = np.ascontiguousarray(I["L0_ffn_w1"]).reshape(8 * 1024, 4096)
    W["w_moe_2"] = np.ascontiguousarray(I["L0_ffn_w2"]).reshape(8 * 4096, 1024)
    W["b_moe_1"] = np.stack([vec_pm(np.asarray(I["L0_ffn_b1"][e])) for e in range(8)])
    W["b_moe_2"] = np.stack([vec_pm(np.asarray(I["L0_ffn_b2"][e])) for e in range(8)])
    W["w_l1a_q"] = np.ascontiguousarray(I["L1_attn_Wq"].transpose(1, 0, 2).reshape(D, D))
    W["w_l1a_k"] = np.ascontiguousarray(I["L1_attn_Wk"])
    W["w_l1a_v"] = np.ascontiguousarray(I["L1_attn_Wv"])
    W["w_l1a_p"] = np.ascontiguousarray(I["L1_attn_Wp"])
    W["b_l1a_p"] = vec_pm(np.asarray(I["L1_attn_bp"]))
    W["w_l1f_qkv"] = np.ascontiguousarray(I["L1_ffn_Wqkv"]).reshape(6 * D, 3 * D)
    W["w_l1f_o"] = np.ascontiguousarray(I["L1_ffn_Wo"]).reshape(6 * D, D)
    DI = 2730
    w1 = np.asarray(I["L1_ffn_W1"])
    w2 = np.asarray(I["L1_ffn_W2"])
    b1 = np.asarray(I["L1_ffn_b1"])
    w1a = np.zeros((6, D, FF2), np.float32); w1a[:, :, :DI] = w1[:, :, :DI]
    w1g = np.zeros((6, D, FF2), np.float32); w1g[:, :, :DI] = w1[:, :, DI:]
    w2p = np.zeros((6, FF2, D), np.float32); w2p[:, :DI, :] = w2
    b1a = np.zeros((6, FF2), np.float32); b1a[:, :DI] = b1[:, :DI]
    b1g = np.zeros((6, FF2), np.float32); b1g[:, :DI] = b1[:, DI:]
    W["w_l1f_1a"] = np.ascontiguousarray(w1a).reshape(6 * D, FF2)
    W["w_l1f_1g"] = np.ascontiguousarray(w1g).reshape(6 * D, FF2)
    W["w_l1f_2p"] = np.ascontiguousarray(w2p).reshape(6 * FF2, D)
    W["b_l1f_1a"] = np.concatenate([vec_pm(b1a[i]) for i in range(6)], 0)
    W["b_l1f_1g"] = np.concatenate([vec_pm(b1g[i]) for i in range(6)], 0)
    W["b_l1f_2"] = np.concatenate([vec_pm(np.asarray(I["L1_ffn_b2"][i])) for i in range(6)], 0)
    W["g_l1f"] = np.concatenate([vec_pm(np.asarray(I["L1_ffn_ln_g"][i])) for i in range(6)], 0)
    W["bb_l1f"] = np.concatenate([vec_pm(np.asarray(I["L1_ffn_ln_b"][i])) for i in range(6)], 0)
    qs = np.asarray(I["L1_ffn_q_scale"]); ks = np.asarray(I["L1_ffn_k_scale"])
    W["s_l1f_q"] = np.ascontiguousarray(np.concatenate([qs, qs], 1).reshape(6 * 128, 1))
    W["s_l1f_k"] = np.ascontiguousarray(np.concatenate([ks, ks], 1).reshape(6 * 128, 1))
    return W



def load_const(p, tag, dram_name, shape, dtype=F32, rows=None):
    t = p.sbuf(tag, shape, dtype)
    src_ap = p.drams[dram_name][:] if rows is None else p.drams[dram_name][rows[0]:rows[1], :]
    p.do("sync", lambda e, src_ap=src_ap: e.dma_start(out=t[:], in_=src_ap), inc="cld", n=16)
    return t


def wait_consts(p, *engines):
    for e in engines:
        p.wait(e, "cld", p.val("cld"))


def st_load_stream(p, tag, dram_bf16, kch=NCH, guard=0, ncols=T, shift_half=False):
    x = p.sbuf(f"{tag}_X", [128, kch, ncols], BF16)
    n = (ncols + TC - 1) // TC
    xbase = p.val(f"{tag}_xld")
    nld = 0
    for tc in range(n):
        w = min(TC, ncols - tc * TC)
        if not shift_half:
            p.do("sync", lambda e, tc=tc, w=w: e.dma_start(
                out=x[:, :, tc * TC:tc * TC + w],
                in_=dram_bf16[:, guard + tc * TC: guard + tc * TC + w]
                    .rearrange("(c p) t -> p c t", p=128)), inc=f"{tag}_xld", n=16)
            nld += 1
        else:
            h = kch // 2
            p.do("sync", lambda e, tc=tc, w=w, h=h: e.dma_start(
                out=x[:, 0:h, tc * TC:tc * TC + w],
                in_=dram_bf16[0:h * 128, tc * TC: tc * TC + w]
                    .rearrange("(c p) t -> p c t", p=128)), inc=f"{tag}_xld", n=16)
            if tc == 0:
                p.do("sync", lambda e, w=w, h=h: e.dma_start(
                    out=x[:, h:kch, 1:w],
                    in_=dram_bf16[h * 128:kch * 128, 0:w - 1]
                        .rearrange("(c p) t -> p c t", p=128)), inc=f"{tag}_xld", n=16)
                p.wait("vector", f"{tag}_xld", xbase + 16 * (nld + 2))
                p.do("vector", lambda e, h=h: e.memset(x[:, h:kch, 0:1], 0.0), inc=f"{tag}_shz")
            else:
                p.do("sync", lambda e, tc=tc, w=w, h=h: e.dma_start(
                    out=x[:, h:kch, tc * TC:tc * TC + w],
                    in_=dram_bf16[h * 128:kch * 128, tc * TC - 1: tc * TC + w - 1]
                        .rearrange("(c p) t -> p c t", p=128)), inc=f"{tag}_xld", n=16)
            nld += 2
    if shift_half:
        p.wait("tensor", f"{tag}_shz", p.val(f"{tag}_shz"))
    return x, p.val(f"{tag}_xld")



def emit_proj(p, tag, x_sb, kch, w_dram, nout, epilogue, n_banks=2,
              tc_list=None, x_col0=0, w_row0=0, mo_list=None,
              banks=None, bank_state=None, wbufs=None):
    n_mo = (nout + 127) // 128
    if mo_list is None:
        mo_list = list(range(n_mo))
    if tc_list is None:
        tc_list = list(range(NTC))
    if banks is None:
        banks = [p.psum(f"{tag}_pb{i}", [128, TC], F32) for i in range(n_banks)]
    n_banks = len(banks)
    if bank_state is None:
        bank_state = [None] * n_banks
    if wbufs is None:
        wbufs = [p.sbuf(f"{tag}_w{i}", [128, kch, 128], BF16) for i in range(2)]
    wbuf_mm_target = [0, 0]
    ld, evac = f"{tag}_wld", f"{tag}_ev"
    ldbase = p.val(ld)
    bi = 0
    nld = 0
    for wi, mo in enumerate(mo_list):
        mwid = min(128, nout - mo * 128)
        buf = wbufs[wi % 2]
        if wi >= 2:
            p.wait("gpsimd", "mm", wbuf_mm_target[wi % 2])
        p.do("gpsimd", lambda e, buf=buf, mo=mo, mwid=mwid: e.dma_start(
            out=buf[:, :, 0:mwid],
            in_=w_dram[w_row0:w_row0 + kch * 128, mo * 128:mo * 128 + mwid]
                .rearrange("(c k) m -> k c m", k=128)), inc=ld, n=16)
        nld += 1
        p.wait("tensor", ld, ldbase + 16 * nld)
        for tc in tc_list:
            if bank_state[bi % n_banks] is not None:
                gsem, gval = bank_state[bi % n_banks]
                p.wait("tensor", gsem, gval)
            ps = banks[bi % n_banks]
            for k in range(kch):
                p.do("tensor", lambda e, ps=ps, buf=buf, k=k, mwid=mwid, tc=tc: e.matmul(
                    ps[0:mwid, :], buf[:, k, 0:mwid],
                    x_sb[:, k, x_col0 + tc * TC: x_col0 + (tc + 1) * TC],
                    start=(k == 0), stop=(k == kch - 1)),
                    inc=("mm" if k == kch - 1 else None))
            tgt = p.val(evac) + 1
            epilogue(mo, tc, ps, mwid, bi % 2)
            assert p.val(evac) == tgt, f"{tag}: epilogue must inc {evac} exactly once"
            bank_state[bi % n_banks] = (evac, tgt)
            bi += 1
        wbuf_mm_target[wi % 2] = p.val("mm")
    return evac



def st_ln(p, tag, x_dram, out_f32, out_bf16, g_sb=None, b_sb=None,
          eps=1e-5, x_guard=0, out_guard=0, g_name=None, b_name=None,
          out_tm=None, post_act=None, shift_half=False, g_rows=None):
    if g_name is not None:
        g_sb = load_const(p, f"{tag}_g", g_name, [128, NCH], rows=g_rows)
        b_sb = load_const(p, f"{tag}_bb", b_name, [128, NCH], rows=g_rows)
        wait_consts(p, "vector")
    xt = [p.sbuf(f"{tag}_x{i}", [128, NCH, TC], F32) for i in range(2)]
    ot = p.sbuf(f"{tag}_o", [128, NCH, TC], F32)
    ob = p.sbuf(f"{tag}_ob", [128, NCH, TC], BF16)
    sq = p.sbuf(f"{tag}_sq", [128, NCH, TC], F32)
    sqs = p.sbuf(f"{tag}_sqs", [1, TC], F32)
    strow = p.sbuf(f"{tag}_st", [1, 4 * TC], F32)
    ones = p.sbuf(f"{tag}_1s", [128, 1], F32)
    onesr = p.sbuf(f"{tag}_1r", [1, 128], F32)
    p.do("vector", lambda e: e.memset(ones[:], 1.0))
    p.do("vector", lambda e: e.memset(onesr[:], 1.0), inc="ve")
    epst = p.fconst(float(eps))
    if out_tm is not None:
        ident_tm = load_const(p, f"{tag}_idtm", "c_ident_bf", [128, 128], BF16)
        wait_consts(p, "tensor")
        tm_ps = p.psum(f"{tag}_tmps", [128, 1024], BF16)
        tm_sb = p.sbuf(f"{tag}_tmsb", [128, 128], BF16)
    ps_s = p.psum(f"{tag}_pss", [1, TC], F32)
    ps_s2 = p.psum(f"{tag}_pss2", [1, TC], F32)
    ps_b = p.psum(f"{tag}_psb", [128, 2 * TC], F32)
    ldsem, stsem = f"{tag}_ld", f"{tag}_stores"
    p.wait("tensor", "ve", p.val("ve"))
    lnbase = p.val(ldsem)
    appbase = p.val(f"{tag}_app")
    nld_ln = 0
    for tc in range(NTC):
        x = xt[tc % 2]
        if tc >= 2:
            p.wait("sync", f"{tag}_app", appbase + tc - 1)
        if not shift_half:
            p.do("sync", lambda e, x=x, tc=tc: e.dma_start(
                out=x[:],
                in_=x_dram[:, x_guard + tc * TC: x_guard + (tc + 1) * TC]
                    .rearrange("(c p) t -> p c t", p=128)), inc=ldsem, n=16)
            nld_ln += 1
        else:
            hh_ = NCH // 2
            p.do("sync", lambda e, x=x, tc=tc, hh_=hh_: e.dma_start(
                out=x[:, 0:hh_, :],
                in_=x_dram[0:hh_ * 128, tc * TC:(tc + 1) * TC]
                    .rearrange("(c p) t -> p c t", p=128)), inc=ldsem, n=16)
            if tc == 0:
                p.do("sync", lambda e, x=x, hh_=hh_: e.dma_start(
                    out=x[:, hh_:NCH, 1:TC],
                    in_=x_dram[hh_ * 128:D, 0:TC - 1]
                        .rearrange("(c p) t -> p c t", p=128)), inc=ldsem, n=16)
                p.wait("vector", ldsem, lnbase + 16 * (nld_ln + 2))
                p.do("vector", lambda e, x=x, hh_=hh_: e.memset(x[:, hh_:NCH, 0:1], 0.0),
                     inc="ve")
            else:
                p.do("sync", lambda e, x=x, tc=tc, hh_=hh_: e.dma_start(
                    out=x[:, hh_:NCH, :],
                    in_=x_dram[hh_ * 128:D, tc * TC - 1:(tc + 1) * TC - 1]
                        .rearrange("(c p) t -> p c t", p=128)), inc=ldsem, n=16)
            nld_ln += 2
        p.wait("tensor", ldsem, lnbase + 16 * nld_ln)
        p.wait("tensor", "ve", p.val("ve"))
        for k in range(NCH):
            p.do("tensor", lambda e, x=x, k=k: e.matmul(
                ps_s[:], ones[:], x[:, k, :], start=(k == 0), stop=(k == NCH - 1)))
        p.wait("scalar", ldsem, lnbase + 16 * nld_ln)
        p.wait("scalar", "mm", p.val("mm"))
        for k in range(NCH):
            p.do("scalar", lambda e, x=x, k=k: e.activation(
                out=sq[:, k, :], in_=x[:, k, :], func=AF.Square),
                inc=("ac" if k == NCH - 1 else None))
        p.wait("tensor", "ac", p.val("ac"))
        for k in range(NCH):
            p.do("tensor", lambda e, k=k: e.matmul(
                ps_s2[:], ones[:], sq[:, k, :], start=(k == 0), stop=(k == NCH - 1)),
                inc=("mm" if k == NCH - 1 else None))
        p.wait("vector", "mm", p.val("mm"))
        mu, var = strow[:, 0:TC], strow[:, TC:2 * TC]
        sd, r = strow[:, 2 * TC:3 * TC], strow[:, 3 * TC:4 * TC]
        p.do("vector", lambda e: e.tensor_scalar_mul(mu, ps_s[:], 1.0 / D))
        p.do("vector", lambda e: e.tensor_scalar_mul(var, ps_s2[:], 1.0 / D))
        p.do("vector", lambda e: e.tensor_tensor(sqs[:], mu, mu, ALU.mult))
        p.do("vector", lambda e: e.tensor_sub(var, var, sqs[:]), inc="ve")
        p.wait("scalar", "ve", p.val("ve"))
        p.wait("scalar", "fcinit", p.val("fcinit"))
        p.do("scalar", lambda e: e.activation(out=sd, in_=var, func=AF.Sqrt, bias=epst[0:1, :]), inc="ac")
        p.wait("vector", "ac", p.val("ac"))
        p.do("vector", lambda e: e.reciprocal(r, sd), inc="ve")
        p.wait("tensor", "ve", p.val("ve"))
        p.do("tensor", lambda e: e.matmul(ps_b[:, 0:TC], onesr[:], mu, start=True, stop=True))
        p.do("tensor", lambda e: e.matmul(ps_b[:, TC:2 * TC], onesr[:], r, start=True, stop=True), inc="mm")
        p.wait("vector", "mm", p.val("mm"))
        if tc >= 1:
            p.wait("vector", stsem, p.val(stsem))
        for k in range(NCH):
            p.do("vector", lambda e, x=x, k=k: e.tensor_sub(ot[:, k, :], x[:, k, :], ps_b[:, 0:TC]))
            p.do("vector", lambda e, k=k: e.tensor_mul(ot[:, k, :], ot[:, k, :], ps_b[:, TC:2 * TC]))
            if g_sb is not None:
                p.do("vector", lambda e, k=k: e.tensor_scalar(
                    ot[:, k, :], ot[:, k, :], g_sb[:, k:k + 1], b_sb[:, k:k + 1], ALU.mult, ALU.add))
            if post_act is not None:
                p.do("vector", lambda e: e.engine_nop(), inc="vpa")
                p.wait("scalar", "vpa", p.val("vpa"))
                p.do("scalar", lambda e, k=k: e.activation(
                    out=ot[:, k, :], in_=ot[:, k, :], func=post_act), inc="acpa")
                p.wait("vector", "acpa", p.val("acpa"))
            p.do("vector", lambda e, k=k: e.tensor_copy(ob[:, k, :], ot[:, k, :]),
                 inc=("ve" if k == NCH - 1 else None))
        p.do("vector", lambda e: e.engine_nop(), inc=f"{tag}_app")
        p.wait("sync", "ve", p.val("ve"))
        if out_f32 is not None:
            p.do("sync", lambda e, tc=tc: e.dma_start(
                out=out_f32[:, out_guard + tc * TC: out_guard + (tc + 1) * TC]
                    .rearrange("(c p) t -> p c t", p=128), in_=ot[:]), inc=stsem, n=16)
        if out_bf16 is not None:
            p.do("sync", lambda e, tc=tc: e.dma_start(
                out=out_bf16[:, out_guard + tc * TC: out_guard + (tc + 1) * TC]
                    .rearrange("(c p) t -> p c t", p=128), in_=ob[:]), inc=stsem, n=16)
        if out_tm is not None:
            p.wait("tensor", "ve", p.val("ve"))
            for k in range(NCH):
                for j in range(4):
                    p.wait("tensor", f"{tag}_tmev", p.val(f"{tag}_tmev"))
                    p.do("tensor", lambda e, k=k, j=j: e.transpose(
                        tm_ps[:, 0:128], ob[:, k, j * 128:(j + 1) * 128], ident_tm[:]),
                        inc="mm2")
                    p.wait("vector", "mm2", p.val("mm2"))
                    p.wait("vector", stsem, p.val(stsem))
                    p.do("vector", lambda e: e.tensor_copy(tm_sb[:], tm_ps[:, 0:128]),
                         inc=f"{tag}_tmev")
                    p.wait("sync", f"{tag}_tmev", p.val(f"{tag}_tmev"))
                    p.do("sync", lambda e, k=k, j=j, tc=tc: e.dma_start(
                        out=out_tm[tc * TC + j * 128: tc * TC + (j + 1) * 128,
                                   k * 128:(k + 1) * 128], in_=tm_sb[:]), inc=stsem, n=16)
    p.wait("sync", stsem, p.val(stsem))



def st_l0_qkv(p):
    x_sb, ldcnt = st_load_stream(p, "a2", p.drams["xn_bf"])
    p.wait("tensor", "a2_xld", ldcnt)
    bq = load_const(p, "a2bq", "b_l0a_q", [128, 8])
    bk = load_const(p, "a2bk", "b_l0a_k", [128, 8])
    bv = load_const(p, "a2bv", "b_l0a_v", [128, 8])
    ident = load_const(p, "a2id", "c_ident_bf", [128, 128], BF16)
    wait_consts(p, "scalar", "tensor")
    ev = [p.sbuf(f"a2_ev{i}", [128, TC], BF16) for i in range(2)]
    evt = [p.sbuf(f"a2_evt{i}", [128, TC], BF16) for i in range(2)]
    psb = p.psum("a2_trp", [128, 1024], BF16)
    q_fm, k_fm, v_tm = p.drams["l0_q"], p.drams["l0_k"], p.drams["l0_v"]

    def mk_epi(bias, dst, is_v, tag):
        def epi(mo, tc, ps, mwid, par=0):
            evac = f"{tag}_ev"
            ev_, evt_ = ev[par], evt[par]
            stp = f"a2st{par}"
            p.wait("scalar", stp, p.val(stp))
            p.wait("scalar", "mm", p.val("mm"))
            p.do("scalar", lambda e, ev_=ev_, mo=mo: e.activation(
                out=ev_[:], in_=ps[:], func=AF.Identity, bias=bias[:, mo:mo + 1]), inc=evac)
            if not is_v:
                p.wait("sync", evac, p.val(evac))
                p.do("sync", lambda e, ev_=ev_, mo=mo, tc=tc: e.dma_start(
                    out=dst[mo * 128:(mo + 1) * 128, tc * TC:(tc + 1) * TC], in_=ev_[:]),
                    inc=stp, n=16)
            else:
                p.wait("tensor", evac, p.val(evac))
                p.wait("tensor", "ve2", p.val("ve2"))
                for j in range(4):
                    p.do("tensor", lambda e, j=j, ev_=ev_: e.transpose(
                        psb[:, j * 128:(j + 1) * 128], ev_[:, j * 128:(j + 1) * 128], ident[:]),
                        inc=("mm2" if j == 3 else None))
                p.wait("vector", "mm2", p.val("mm2"))
                p.wait("vector", stp, p.val(stp))
                for j in range(4):
                    p.do("vector", lambda e, j=j, evt_=evt_: e.tensor_copy(
                        evt_[:, j * 128:(j + 1) * 128], psb[:, j * 128:(j + 1) * 128]),
                        inc=("ve2" if j == 3 else None))
                p.wait("sync", "ve2", p.val("ve2"))
                for j in range(4):
                    for hh in range(2):
                        p.do("sync", lambda e, mo=mo, tc=tc, j=j, hh=hh, evt_=evt_: e.dma_start(
                            out=dst[2 * mo + hh, tc * TC + j * 128: tc * TC + (j + 1) * 128, :],
                            in_=evt_[:, j * 128 + hh * 64: j * 128 + hh * 64 + 64]),
                            inc=stp, n=16)
        return epi

    emit_proj(p, "a2q", x_sb, NCH, p.drams["w_l0a_q"], D, mk_epi(bq, q_fm, False, "a2q"))
    emit_proj(p, "a2k", x_sb, NCH, p.drams["w_l0a_k"], D, mk_epi(bk, k_fm, False, "a2k"))
    emit_proj(p, "a2v", x_sb, NCH, p.drams["w_l0a_v"], D, mk_epi(bv, v_tm, True, "a2v"))
    p.wait("sync", "a2st0", p.val("a2st0"))
    p.wait("sync", "a2st1", p.val("a2st1"))


def st_causal_attn(p, tag, q_fm, k_fm, v_tm, o_fm, n_heads, d_head, inv_scale,
                   mask_name, k_dram_rows=None):
    ident = load_const(p, f"{tag}id", "c_ident_bf", [128, 128], BF16)
    mask = load_const(p, f"{tag}mk", mask_name, [128, 128])
    wait_consts(p, "tensor", "vector")
    shared_kv = (k_dram_rows == d_head)
    k_sb = p.sbuf(f"{tag}_k", [d_head, T], BF16)
    v_sb = p.sbuf(f"{tag}_v", [128, 32, d_head + 1], BF16)
    q_sb = [p.sbuf(f"{tag}_q{i}", [d_head, 128], BF16) for i in range(2)]
    S_sb = [p.sbuf(f"{tag}_S{i}", [128, T], F32) for i in range(2)]
    P_sb = [p.sbuf(f"{tag}_P{i}", [128, T], BF16) for i in range(2)]
    S_ps = [p.psum(f"{tag}_sp{i}", [128, TC], F32) for i in range(2)]
    PT_ps = [p.psum(f"{tag}_ptp{i}", [128, 1024], BF16) for i in range(2)]
    PT_sb = p.sbuf(f"{tag}_pt", [128, 2, 128], BF16)
    AV_ps = p.psum(f"{tag}_av", [128, TC], F32)
    AVc = [p.sbuf(f"{tag}_avc{i}", [128, d_head + 1], F32) for i in range(2)]
    O32 = [p.sbuf(f"{tag}_o32{i}", [128, d_head], F32) for i in range(2)]
    Obf = [p.sbuf(f"{tag}_obf{i}", [128, d_head], BF16) for i in range(2)]
    OT_ps = p.psum(f"{tag}_otp", [128, 1024], BF16)
    OT_sb = [p.sbuf(f"{tag}_ot{i}", [d_head, 128], BF16) for i in range(2)]
    nm = [p.sbuf(f"{tag}_nm{i}", [128, 1], F32) for i in range(2)]
    nmr = [p.sbuf(f"{tag}_nmr{i}", [128, 1], F32) for i in range(2)]
    kl, vl, ql = f"{tag}_kld", f"{tag}_vld", f"{tag}_qld"
    sevac, sxp = f"{tag}_sev", f"{tag}_sxp"
    hd = f"{tag}_hd"
    hdbase = p.val(hd)
    sevac_base = p.val(sevac)
    s_no = 0
    for h in range(n_heads):
        kv_load = (h == 0) if shared_kv else True
        kv_idx = 0 if shared_kv else h
        if kv_load:
            if h > 0:
                p.wait("sync", hd, hdbase + h)
            krow = 0 if shared_kv else h * d_head
            p.do("sync", lambda e, krow=krow: e.dma_start(
                out=k_sb[:], in_=k_fm[krow:krow + d_head, :]), inc=kl, n=16)
            p.do("sync", lambda e, kv_idx=kv_idx: e.dma_start(
                out=v_sb[:, :, 0:d_head],
                in_=v_tm[kv_idx].rearrange("(tt p) d -> p tt d", p=128)), inc=vl, n=16)
            p.wait("vector", vl, p.val(vl))
            p.do("vector", lambda e: e.memset(v_sb[:, :, d_head:d_head + 1], 1.0), inc="ve")
        p.wait("tensor", kl, p.val(kl))
        p.wait("tensor", vl, p.val(vl))
        p.wait("tensor", "ve", p.val("ve"))
        smm = {}
        smmk = {}
        sbank = {}
        xrec = {}
        avrec = {}
        avcrec = {}
        for it in range(33):
            if it < 32:
                qs = it
                qb = qs % 2
                if qs >= 2:
                    p.wait("sync", "mm", smm[qs - 2])
                p.do("sync", lambda e, h=h, qs=qs, qb=qb: e.dma_start(
                    out=q_sb[qb][:],
                    in_=q_fm[h * d_head:(h + 1) * d_head, qs * 128:(qs + 1) * 128]),
                    inc=ql, n=16)
                p.wait("tensor", ql, p.val(ql))
                w_tot = (qs + 1) * 128
                n_kt = (w_tot + TC - 1) // TC
                for kt in range(n_kt):
                    w = min(TC, w_tot - kt * TC)
                    S = S_ps[s_no % 2]
                    sbank[(qs, kt)] = s_no % 2
                    s_no += 1
                    if s_no > 2:
                        p.wait("tensor", sevac, sevac_base + s_no - 2)
                    p.do("tensor", lambda e, S=S, qb=qb, kt=kt, w=w: e.matmul(
                        S[:, 0:w], q_sb[qb][:], k_sb[:, kt * TC:kt * TC + w],
                        start=True, stop=True), inc="mm")
                    smmk[(qs, kt)] = p.val("mm")
                smm[qs] = p.val("mm")
            if it == 0:
                continue
            j = it - 1
            par = j % 2
            w_tot = (j + 1) * 128
            n_kt = (w_tot + TC - 1) // TC
            if j >= 2:
                p.wait("scalar", sxp, xrec[j - 2])
            for kt in range(n_kt):
                w = min(TC, w_tot - kt * TC)
                S = S_ps[sbank[(j, kt)]]
                p.wait("scalar", "mm", smmk[(j, kt)])
                p.do("scalar", lambda e, S=S, kt=kt, w=w, par=par: e.activation(
                    out=S_sb[par][:, kt * TC:kt * TC + w], in_=S[:, 0:w], func=AF.Identity,
                    scale=float(inv_scale)), inc=sevac)
            p.wait("vector", sevac, p.val(sevac))
            p.do("vector", lambda e, w_tot=w_tot, par=par: e.scalar_tensor_tensor(
                S_sb[par][:, w_tot - 128:w_tot], mask[:], float(inv_scale),
                S_sb[par][:, w_tot - 128:w_tot], ALU.mult, ALU.add))
            p.do("vector", lambda e, w_tot=w_tot, par=par: e.reduce_max(
                nm[par][:], S_sb[par][:, 0:w_tot], axis=AX.X, negate=True), inc="ve")
            p.wait("scalar", "ve", p.val("ve"))
            if j >= 2:
                p.wait("scalar", "mm3", avrec[j - 2])
            p.do("scalar", lambda e, w_tot=w_tot, par=par: e.activation(
                out=P_sb[par][:, 0:w_tot], in_=S_sb[par][:, 0:w_tot], func=AF.Exp,
                bias=nm[par][:]), inc=sxp)
            xrec[j] = p.val(sxp)
            p.wait("tensor", sxp, xrec[j])
            if j >= 2:
                p.wait("tensor", "vb", avcrec[j - 2])
            for jj in range(w_tot // 128):
                tb = jj % 2
                if jj >= 2:
                    p.wait("tensor", "ve2", p.val("ve2"))
                p.do("tensor", lambda e, jj=jj, tb=tb, par=par: e.transpose(
                    PT_ps[tb][:, 0:128], P_sb[par][:, jj * 128:(jj + 1) * 128], ident[:]),
                    inc="mm2")
                p.wait("vector", "mm2", p.val("mm2"))
                p.do("vector", lambda e, tb=tb: e.tensor_copy(
                    PT_sb[:, tb, :], PT_ps[tb][:, 0:128]), inc="ve2")
                p.wait("tensor", "ve2", p.val("ve2"))
                p.do("tensor", lambda e, jj=jj, tb=tb, w_tot=w_tot: e.matmul(
                    AV_ps[:, 0:d_head + 1], PT_sb[:, tb, :], v_sb[:, jj, :],
                    start=(jj == 0), stop=(jj == w_tot // 128 - 1)),
                    inc=("mm3" if jj == w_tot // 128 - 1 else None))
            avrec[j] = p.val("mm3")
            p.wait("vector", "mm3", avrec[j])
            p.do("vector", lambda e, par=par: e.tensor_copy(
                AVc[par][:], AV_ps[:, 0:d_head + 1]), inc="vb")
            avcrec[j] = p.val("vb")
            p.wait("vector", "vb", p.val("vb"))
            p.do("vector", lambda e, par=par: e.reciprocal(
                nmr[par][:], AVc[par][:, d_head:d_head + 1]), inc="vb")
            p.wait("vector", "vb", p.val("vb"))
            p.do("vector", lambda e, par=par: e.tensor_scalar_mul(
                O32[par][:], AVc[par][:, 0:d_head], nmr[par][:]), inc="vb")
            p.wait("vector", "vb", p.val("vb"))
            p.do("vector", lambda e, par=par: e.tensor_copy(Obf[par][:], O32[par][:]), inc="ve")
            p.wait("tensor", "ve", p.val("ve"))
            p.wait("tensor", "ve3", p.val("ve3"))
            p.do("tensor", lambda e, par=par: e.transpose(
                OT_ps[0:d_head, 0:128], Obf[par][:], ident[:]), inc="mm2")
            p.wait("vector", "mm2", p.val("mm2"))
            spar = j % 2
            p.wait("vector", f"{tag}_st{spar}", p.val(f"{tag}_st{spar}"))
            p.do("vector", lambda e, spar=spar: e.tensor_copy(
                OT_sb[spar][:], OT_ps[0:d_head, 0:128]), inc="ve3")
            p.wait("sync", "ve3", p.val("ve3"))
            p.do("sync", lambda e, h=h, j=j, spar=spar: e.dma_start(
                out=o_fm[h * d_head:(h + 1) * d_head, j * 128:(j + 1) * 128],
                in_=OT_sb[spar][:]), inc=f"{tag}_st{spar}", n=16)
        p.do("sync", lambda e: e.nop(), inc=hd)
    p.wait("sync", f"{tag}_st0", p.val(f"{tag}_st0"))
    p.wait("sync", f"{tag}_st1", p.val(f"{tag}_st1"))


def st_proj_simple(p, tag, x_dram_name, w_name, nout, out_name, bias_name=None,
                   out_dtype="bf16", kch=NCH, act=None, out_guard=0, add_from=None,
                   w_row0=0, also_bf16=None):
    x_sb, ldcnt = st_load_stream(p, tag, p.drams[x_dram_name], kch=kch)
    p.wait("tensor", f"{tag}_xld", ldcnt)
    bias = load_const(p, f"{tag}_b", bias_name, [128, (nout + 127) // 128]) if bias_name else None
    wait_consts(p, "scalar")
    ev32 = [p.sbuf(f"{tag}_e32{i}", [128, TC], F32) for i in range(2)]
    evb = [p.sbuf(f"{tag}_eb{i}", [128, TC], BF16) for i in range(2)]
    dst = p.drams[out_name]
    dst_bf = p.drams[also_bf16] if also_bf16 else None
    addt = [p.sbuf(f"{tag}_add{i}", [128, TC], F32) for i in range(2)] if add_from else None

    def epi(mo, tc, ps, mwid, par=0):
        ev32_, evb_, addt_ = ev32[par], evb[par], (addt[par] if add_from else None)
        stp = f"{tag}st{par}"
        p.wait("scalar", "mm", p.val("mm"))
        p.wait("scalar", stp, p.val(stp))
        kw = {}
        if bias is not None:
            kw["bias"] = bias[:, mo:mo + 1]
        fn = act if act is not None else AF.Identity
        if add_from is not None:
            p.wait("sync", "mm", p.val("mm"))
            p.do("sync", lambda e, mo=mo, tc=tc, mwid=mwid, addt_=addt_: e.dma_start(
                out=addt_[0:mwid, :],
                in_=p.drams[add_from][mo * 128:mo * 128 + mwid, tc * TC:(tc + 1) * TC]),
                inc=f"{tag}_ald", n=16)
            p.wait("scalar", f"{tag}_ald", p.val(f"{tag}_ald"))
        p.do("scalar", lambda e, mo=mo, mwid=mwid, ev32_=ev32_, evb_=evb_: e.activation(
            out=(ev32_ if out_dtype == "f32" else evb_)[0:mwid, :], in_=ps[0:mwid, :], func=fn, **kw),
            inc=("ac" if (add_from or also_bf16) else f"{tag}_ev"))
        srct = ev32_ if out_dtype == "f32" else evb_
        if add_from is not None:
            p.wait("vector", "ac", p.val("ac"))
            p.do("vector", lambda e, mwid=mwid, srct=srct, addt_=addt_: e.tensor_add(
                srct[0:mwid, :], srct[0:mwid, :], addt_[0:mwid, :]),
                inc=("ve" if also_bf16 else f"{tag}_ev"))
            if also_bf16 is not None:
                p.do("vector", lambda e, mwid=mwid, srct=srct, evb_=evb_: e.tensor_copy(
                    evb_[0:mwid, :], srct[0:mwid, :]), inc=f"{tag}_ev")
            p.wait("sync", f"{tag}_ev", p.val(f"{tag}_ev"))
        else:
            if also_bf16 is not None:
                p.wait("vector", "ac", p.val("ac"))
                p.do("vector", lambda e, mwid=mwid, srct=srct, evb_=evb_: e.tensor_copy(
                    evb_[0:mwid, :], srct[0:mwid, :]), inc=f"{tag}_ev")
                p.wait("sync", f"{tag}_ev", p.val(f"{tag}_ev"))
            else:
                p.wait("sync", f"{tag}_ev", p.val(f"{tag}_ev"))
        p.do("sync", lambda e, mo=mo, tc=tc, mwid=mwid, srct=srct: e.dma_start(
            out=dst[mo * 128:mo * 128 + mwid, out_guard + tc * TC:out_guard + (tc + 1) * TC],
            in_=srct[0:mwid, :]), inc=stp, n=16)
        if also_bf16 is not None:
            p.do("sync", lambda e, mo=mo, tc=tc, mwid=mwid, evb_=evb_: e.dma_start(
                out=dst_bf[mo * 128:mo * 128 + mwid, out_guard + tc * TC:out_guard + (tc + 1) * TC],
                in_=evb_[0:mwid, :]), inc=stp, n=16)

    emit_proj(p, tag, x_sb, kch, p.drams[w_name], nout, epi, w_row0=w_row0)
    p.wait("sync", f"{tag}st0", p.val(f"{tag}st0"))
    p.wait("sync", f"{tag}st1", p.val(f"{tag}st1"))



def tm_epilogue(p, tag, ev, evt, psb, ident, dst_tm, bias):
    def epi(mo, tc, ps, mwid, par=0):
        ev_, evt_ = ev[par], evt[par]
        stp = f"{tag}st{par}"
        p.wait("scalar", stp, p.val(stp))
        p.wait("scalar", "mm", p.val("mm"))
        p.do("scalar", lambda e, ev_=ev_, mo=mo: e.activation(
            out=ev_[:], in_=ps[:], func=AF.Identity, bias=bias[:, mo:mo + 1]),
            inc=f"{tag}_ev")
        p.wait("tensor", f"{tag}_ev", p.val(f"{tag}_ev"))
        p.wait("tensor", "ve2", p.val("ve2"))
        for j in range(4):
            p.do("tensor", lambda e, j=j, ev_=ev_: e.transpose(
                psb[:, j * 128:(j + 1) * 128], ev_[:, j * 128:(j + 1) * 128], ident[:]),
                inc=("mm2" if j == 3 else None))
        p.wait("vector", "mm2", p.val("mm2"))
        p.wait("vector", stp, p.val(stp))
        for j in range(4):
            p.do("vector", lambda e, j=j, evt_=evt_: e.tensor_copy(
                evt_[:, j * 128:(j + 1) * 128], psb[:, j * 128:(j + 1) * 128]),
                inc=("ve2" if j == 3 else None))
        p.wait("sync", "ve2", p.val("ve2"))
        mo8 = mo % 8
        for j in range(4):
            p.do("sync", lambda e, mo8=mo8, tc=tc, j=j, evt_=evt_: e.dma_start(
                out=dst_tm(mo)[tc * TC + j * 128: tc * TC + (j + 1) * 128,
                               mo8 * 128:(mo8 + 1) * 128],
                in_=evt_[:, j * 128:(j + 1) * 128]), inc=stp, n=16)
    return epi


def st_l0f_qkv(p):
    x_sb, ldcnt = st_load_stream(p, "b1", p.drams["xn_bf"])
    p.wait("tensor", "b1_xld", ldcnt)
    bias = load_const(p, "b1_b", "b_l0f_qkv", [128, 24])
    ident = load_const(p, "b1_id", "c_ident_bf", [128, 128], BF16)
    wait_consts(p, "scalar", "tensor")
    ev = [p.sbuf(f"b1_ev{i}", [128, TC], BF16) for i in range(2)]
    evt = [p.sbuf(f"b1_evt{i}", [128, TC], BF16) for i in range(2)]
    psb = p.psum("b1_trp", [128, 1024], BF16)
    dsts = [p.drams["q0_tm"], p.drams["k0_tm"], p.drams["v0_tm"]]
    epi = tm_epilogue(p, "b1", ev, evt, psb, ident, lambda mo: dsts[mo // 8], bias)
    emit_proj(p, "b1", x_sb, NCH, p.drams["w_l0f_qkv"], 3 * D, epi)
    p.wait("sync", "b1st0", p.val("b1st0"))
    p.wait("sync", "b1st1", p.val("b1st1"))


def st_l0f_mha(p):
    q0, k0a, v0a, o0 = (p.drams["q0_tm"], p.drams["k0_all"], p.drams["v0_all"],
                        p.drams["o0_fm"])
    ident = load_const(p, "b2_id", "c_ident_bf", [128, 128], BF16)
    wait_consts(p, "tensor")
    qt = [p.sbuf(f"b2_q{i}", [128, D], BF16) for i in range(2)]
    kv = [p.sbuf(f"b2_kv{i}", [128, D], BF16) for i in range(2)]
    vt32 = p.sbuf("b2_v32", [128, D], F32)
    prod = p.sbuf("b2_pr", [128, D], F32)
    sc = p.sbuf("b2_sc", [128, 32], F32)
    ex = p.sbuf("b2_ex", [128, 32], F32)
    nmx4 = p.sbuf("b2_nm", [128, 4], F32)
    bh4 = p.sbuf("b2_bh", [128, 4], F32)
    s4 = p.sbuf("b2_s4", [128, 4], F32)
    rs4 = p.sbuf("b2_rs", [128, 4], F32)
    oacc = p.sbuf("b2_o", [128, D], F32)
    tmp = p.sbuf("b2_t", [128, 256], F32)
    obf = p.sbuf("b2_ob", [128, D], BF16)
    trp = p.psum("b2_trp", [128, 1024], BF16)
    trs = p.sbuf("b2_trs", [128, 128], BF16)
    ql, kl = "b2_ql", "b2_kl"
    for tt in range(32):
        qb = tt % 2
        p.wait("sync", "b2_pr1", p.val("b2_pr1"))
        p.do("sync", lambda e, tt=tt, qb=qb: e.dma_start(
            out=qt[qb][:], in_=q0[tt * 128:(tt + 1) * 128, :]), inc=ql, n=16)
        for m in range(8):
            mb = m % 2
            if m >= 2:
                p.wait("sync", "b2_pr1", p.val("b2_pr1"))
            p.do("sync", lambda e, m=m, mb=mb, tt=tt: e.dma_start(
                out=kv[mb][:], in_=k0a[m, tt * 128:(tt + 1) * 128, :]), inc=kl, n=16)
            p.wait("vector", kl, p.val(kl))
            p.wait("vector", ql, p.val(ql))
            p.do("vector", lambda e, mb=mb, qb=qb: e.tensor_tensor(
                prod[:], qt[qb][:], kv[mb][:], ALU.mult), inc="vb")
            p.wait("vector", "vb", p.val("vb"))
            p.do("vector", lambda e, m=m: e.tensor_reduce(
                sc.rearrange("p (h m) -> p h m", m=8)[:, :, m], prod[:].rearrange(
                    "p (h d) -> p h d", d=256), axis=AX.X, op=ALU.add),
                inc="b2_pr1")
            if "dbg_pr" in p.drams and tt == 0 and m == 0:
                p.wait("sync", "b2_pr1", p.val("b2_pr1"))
                p.do("sync", lambda e: e.dma_start(out=p.drams["dbg_pr"][0], in_=prod[:]), inc="dbg", n=16)
                qf = p.sbuf("b2_dbgq", [128, D], F32)
                kf = p.sbuf("b2_dbgk", [128, D], F32)
                p.do("vector", lambda e, qb=qb: e.tensor_copy(qf[:], qt[qb][:]))
                p.do("vector", lambda e, mb=mb: e.tensor_copy(kf[:], kv[mb][:]), inc="ve9")
                p.wait("sync", "ve9", p.val("ve9"))
                p.do("sync", lambda e: e.dma_start(out=p.drams["dbg_pr"][1], in_=qf[:]), inc="dbg", n=16)
                p.do("sync", lambda e: e.dma_start(out=p.drams["dbg_pr"][2], in_=kf[:]), inc="dbg", n=16)
                p.wait("vector", "dbg", p.val("dbg"))
        p.wait("vector", "b2_pr1", p.val("b2_pr1"))
        for h in range(4):
            p.do("vector", lambda e, h=h: e.reduce_max(
                nmx4[:, h:h + 1], sc[:, h * 8:(h + 1) * 8], axis=AX.X, negate=True),
                inc=("vb" if h == 3 else None))
        p.wait("vector", "vb", p.val("vb"))
        p.do("vector", lambda e: e.tensor_scalar_mul(bh4[:], nmx4[:], 1.0 / 16.0), inc="vb")
        p.wait("scalar", "vb", p.val("vb"))
        for h in range(4):
            p.do("scalar", lambda e, h=h: e.activation(
                out=ex[:, h * 8:(h + 1) * 8], in_=sc[:, h * 8:(h + 1) * 8], func=AF.Exp,
                scale=1.0 / 16.0, bias=bh4[:, h:h + 1]), inc="ac")
        p.wait("vector", "ac", p.val("ac"))
        for h in range(4):
            p.do("vector", lambda e, h=h: e.reduce_sum(
                s4[:, h:h + 1], ex[:, h * 8:(h + 1) * 8], axis=AX.X),
                inc=("vb" if h == 3 else None))
        p.wait("vector", "vb", p.val("vb"))
        p.do("vector", lambda e: e.reciprocal(rs4[:], s4[:]), inc="vb")
        if "dbg_sc" in p.drams and tt == 0:
            p.wait("sync", "vb", p.val("vb"))
            p.wait("sync", "ac", p.val("ac"))
            p.do("sync", lambda e: e.dma_start(out=p.drams["dbg_sc"][:, 0:32], in_=sc[:]), inc="dbg", n=16)
            p.do("sync", lambda e: e.dma_start(out=p.drams["dbg_sc"][:, 32:64], in_=ex[:]), inc="dbg", n=16)
            p.do("sync", lambda e: e.dma_start(out=p.drams["dbg_sc"][:, 64:68], in_=s4[:]), inc="dbg", n=16)
            p.wait("vector", "dbg", p.val("dbg"))
            p.wait("scalar", "dbg", p.val("dbg"))
        p.do("vector", lambda e: e.memset(oacc[:], 0.0))
        p.wait("sync", "b2_pr1", p.val("b2_pr1"))
        for m in range(8):
            mb = m % 2
            p.wait("sync", "b2_pr2", p.val("b2_pr2"))
            p.do("sync", lambda e, m=m, mb=mb, tt=tt: e.dma_start(
                out=kv[mb][:], in_=v0a[m, tt * 128:(tt + 1) * 128, :]), inc=kl, n=16)
            p.wait("vector", kl, p.val(kl))
            p.do("vector", lambda e, mb=mb: e.tensor_copy(vt32[:], kv[mb][:]), inc="b2_pr2")
            for h in range(4):
                p.do("vector", lambda e, h=h, m=m: e.tensor_scalar_mul(
                    tmp[:], vt32[:, h * 256:(h + 1) * 256], ex[:, h * 8 + m:h * 8 + m + 1]))
                p.do("vector", lambda e, h=h: e.tensor_add(
                    oacc[:, h * 256:(h + 1) * 256], oacc[:, h * 256:(h + 1) * 256], tmp[:]))
        p.wait("vector", "vb", p.val("vb"))
        for h in range(4):
            p.do("vector", lambda e, h=h: e.tensor_scalar_mul(
                oacc[:, h * 256:(h + 1) * 256], oacc[:, h * 256:(h + 1) * 256],
                rs4[:, h:h + 1]))
        p.do("vector", lambda e: e.tensor_copy(obf[:], oacc[:]), inc="ve")
        p.wait("tensor", "ve", p.val("ve"))
        for k in range(8):
            p.wait("tensor", "b2_tev", p.val("b2_tev"))
            p.do("tensor", lambda e, k=k: e.transpose(
                trp[:, 0:128], obf[:, k * 128:(k + 1) * 128], ident[:]), inc="mm2")
            p.wait("vector", "mm2", p.val("mm2"))
            p.wait("vector", "b2_st", p.val("b2_st"))
            p.do("vector", lambda e: e.tensor_copy(trs[:], trp[:, 0:128]), inc="b2_tev")
            p.wait("sync", "b2_tev", p.val("b2_tev"))
            p.do("sync", lambda e, k=k, tt=tt: e.dma_start(
                out=o0[k * 128:(k + 1) * 128, tt * 128:(tt + 1) * 128], in_=trs[:]),
                inc="b2_st", n=16)
    p.wait("sync", "b2_st", p.val("b2_st"))


def st_moe_route_a(p):
    y1, wg_d = p.drams["y1_bf"], p.drams["w_moe_g"]
    ltri = load_const(p, "b3_lt", "c_ltri", [128, 128])
    iota8 = load_const(p, "b3_io", "c_iota8", [128, 8])
    wg = p.sbuf("b3_wg", [128, NCH, 8], BF16)
    p.do("gpsimd", lambda e: e.dma_start(
        out=wg[:], in_=wg_d.rearrange("(c k) m -> k c m", k=128)), inc="b3_wld", n=16)
    wait_consts(p, "tensor", "vector")
    p.wait("tensor", "b3_wld", p.val("b3_wld"))
    y1t = [p.sbuf(f"b3_y{i}", [128, NCH, 128], BF16) for i in range(2)]
    ones1r = p.sbuf("b3_1r", [1, 128], F32)
    ones128 = p.sbuf("b3_1c", [128, 1], F32)
    run = p.sbuf("b3_run", [1, 8], F32)
    g32 = p.sbuf("b3_g", [128, 8], F32)
    nmx = p.sbuf("b3_nm", [128, 1], F32)
    mx = p.sbuf("b3_mx", [128, 1], F32)
    eq = p.sbuf("b3_eq", [128, 8], F32)
    ex8 = p.sbuf("b3_ex", [128, 8], F32)
    s8 = p.sbuf("b3_s8", [128, 1], F32)
    H = p.sbuf("b3_H", [128, 8], F32)
    Ef = p.sbuf("b3_E", [128, 8], F32)
    tsel = p.sbuf("b3_ts", [128, 8], F32)
    loc_acc = p.sbuf("b3_loc", [128, 32], F32)
    idx_acc = p.sbuf("b3_idx", [128, 32], F32)
    gate_acc = p.sbuf("b3_gv", [128, 32], F32)
    ps_g = p.psum("b3_psg", [128, TC], F32)
    ps_E = p.psum("b3_psE", [128, TC], F32)
    ps_t = p.psum("b3_pst", [1, TC], F32)
    p.do("vector", lambda e: e.memset(ones1r[:], 1.0))
    p.do("vector", lambda e: e.memset(ones128[:], 1.0))
    p.do("vector", lambda e: e.memset(run[:], 0.0), inc="ve")
    p.wait("tensor", "ve", p.val("ve"))
    mm_after_tile = []
    for tt in range(32):
        yb = tt % 2
        if tt >= 2:
            p.wait("sync", "mm", mm_after_tile[tt - 2])
        p.do("sync", lambda e, tt=tt, yb=yb: e.dma_start(
            out=y1t[yb][:], in_=y1[:, tt * 128:(tt + 1) * 128]
                .rearrange("(c p) t -> p c t", p=128)), inc="b3_yld", n=16)
        p.wait("tensor", "b3_yld", p.val("b3_yld"))
        p.wait("tensor", "ve", p.val("ve"))
        for k in range(NCH):
            p.do("tensor", lambda e, k=k, yb=yb: e.matmul(
                ps_g[:, 0:8], y1t[yb][:, k, :], wg[:, k, :],
                start=(k == 0), stop=(k == NCH - 1)), inc=("mm" if k == NCH - 1 else None))
        p.wait("vector", "mm", p.val("mm"))
        def gdo(fn):
            p.do("vector", fn, inc="vb")
            p.wait("vector", "vb", p.val("vb"))
        gdo(lambda e: e.tensor_copy(g32[:], ps_g[:, 0:8]))
        if "dbg_g" in p.drams and tt == 0:
            p.wait("sync", "vb", p.val("vb"))
            p.do("sync", lambda e: e.dma_start(out=p.drams["dbg_g"][:], in_=g32[:]), inc="dbg", n=16)
            p.wait("vector", "dbg", p.val("dbg"))
            p.wait("tensor", "dbg", p.val("dbg"))
        gdo(lambda e: e.reduce_max(nmx[:], g32[:], axis=AX.X, negate=True))
        gdo(lambda e: e.tensor_scalar_mul(mx[:], nmx[:], -1.0))
        gdo(lambda e: e.tensor_single_scalar(eq[:], g32[:], mx[:], ALU.is_equal))
        gdo(lambda e: e.tensor_scalar(eq[:], eq[:], -1e9, 1e9, ALU.mult, ALU.add))
        gdo(lambda e: e.tensor_add(eq[:], eq[:], iota8[:]))
        gdo(lambda e, tt=tt: e.tensor_reduce(
            idx_acc[:, tt:tt + 1], eq[:], axis=AX.X, op=ALU.min))
        p.wait("scalar", "vb", p.val("vb"))
        p.do("scalar", lambda e: e.activation(out=ex8[:], in_=g32[:], func=AF.Exp,
                                              bias=nmx[:]), inc="ac")
        p.wait("vector", "ac", p.val("ac"))
        gdo(lambda e: e.reduce_sum(s8[:], ex8[:], axis=AX.X))
        gdo(lambda e, tt=tt: e.reciprocal(gate_acc[:, tt:tt + 1], s8[:]))
        gdo(lambda e, tt=tt: e.tensor_single_scalar(
            H[:], iota8[:], idx_acc[:, tt:tt + 1], ALU.is_equal))
        if "dbg_g" in p.drams and tt == 0:
            p.wait("sync", "vb", p.val("vb"))
            p.do("sync", lambda e: e.dma_start(out=p.drams["dbg_r"][:, 0:1], in_=nmx[:], allow_slow_non_contiguous=True), inc="dbg", n=16)
            p.do("sync", lambda e: e.dma_start(out=p.drams["dbg_r"][:, 1:2], in_=mx[:], allow_slow_non_contiguous=True), inc="dbg", n=16)
            p.do("sync", lambda e: e.dma_start(out=p.drams["dbg_r"][:, 2:10], in_=eq[:], allow_slow_non_contiguous=True), inc="dbg", n=16)
            p.do("sync", lambda e: e.dma_start(out=p.drams["dbg_r"][:, 10:18], in_=H[:], allow_slow_non_contiguous=True), inc="dbg", n=16)
            p.do("sync", lambda e: e.dma_start(out=p.drams["dbg_r"][:, 18:19], in_=idx_acc[:, 0:1], allow_slow_non_contiguous=True), inc="dbg", n=16)
            p.wait("vector", "dbg", p.val("dbg"))
            p.wait("tensor", "dbg", p.val("dbg"))
        p.wait("tensor", "vb", p.val("vb"))
        p.do("tensor", lambda e: e.matmul(ps_E[:, 0:8], ltri[:], H[:], start=True, stop=False))
        p.do("tensor", lambda e: e.matmul(ps_E[:, 0:8], ones1r[:], run[:], start=False, stop=True))
        p.do("tensor", lambda e: e.matmul(ps_t[:, 0:8], ones128[:], H[:], start=True, stop=True), inc="mm")
        p.wait("vector", "mm", p.val("mm"))
        gdo(lambda e: e.tensor_copy(Ef[:], ps_E[:, 0:8]))
        gdo(lambda e: e.tensor_mul(tsel[:], Ef[:], H[:]))
        gdo(lambda e, tt=tt: e.reduce_sum(loc_acc[:, tt:tt + 1], tsel[:], axis=AX.X))
        p.do("vector", lambda e: e.tensor_add(run[:], run[:], ps_t[0:1, 0:8]), inc="ve")
        mm_after_tile.append(p.val("mm"))
    p.wait("sync", "ve", p.val("ve"))
    p.do("sync", lambda e: e.dma_start(out=p.drams["r_loc"][:], in_=loc_acc[:]), inc="b3_st", n=16)
    p.do("sync", lambda e: e.dma_start(out=p.drams["r_idx"][:], in_=idx_acc[:]), inc="b3_st", n=16)
    p.do("sync", lambda e: e.dma_start(out=p.drams["r_gate"][:], in_=gate_acc[:]), inc="b3_st", n=16)
    p.do("sync", lambda e: e.dma_start(out=p.drams["r_hist"][:], in_=run[:]), inc="b3_st", n=16)
    p.wait("sync", "b3_st", p.val("b3_st"))


def st_moe_route_b(p):
    iota8 = load_const(p, "b4_io", "c_iota8", [128, 8])
    hist = load_const(p, "b4_h", "hist_all", [8, 8])
    csel = load_const(p, "b4_cs", "c_coresel", [8, 1])
    locs = load_const(p, "b4_lc", "r_loc", [128, 32])
    idxs = load_const(p, "b4_ix", "r_idx", [128, 32])
    wait_consts(p, "tensor", "vector")
    ones1r = p.sbuf("b4_1r", [1, 128], F32)
    co_row = p.sbuf("b4_cor", [1, 8], F32)
    co_bc = p.sbuf("b4_cob", [128, 8], F32)
    H = p.sbuf("b4_H", [128, 8], F32)
    t8 = p.sbuf("b4_t8", [128, 8], F32)
    col = p.sbuf("b4_col", [128, 6], F32)
    gk = p.sbuf("b4_gk", [128, 32], F32)
    gates_sb = load_const(p, "b4_gt", "r_gate", [128, 32])
    desti = p.sbuf("b4_di", [128, 32], I32)
    y1t = [p.sbuf(f"b4_y{i}", [128, D], BF16) for i in range(2)]
    ps1 = p.psum("b4_ps1", [1, TC], F32)
    ps2 = p.psum("b4_ps2", [128, TC], F32)
    disp = p.drams["moe_disp"]
    scbase = p.val("b4_sc")
    p.do("vector", lambda e: e.memset(ones1r[:], 1.0), inc="ve")
    p.wait("tensor", "ve", p.val("ve"))
    p.do("tensor", lambda e: e.matmul(ps1[:, 0:8], csel[:], hist[:], start=True, stop=True), inc="mm")
    p.wait("vector", "mm", p.val("mm"))
    p.do("vector", lambda e: e.tensor_copy(co_row[:], ps1[0:1, 0:8]), inc="ve")
    p.wait("tensor", "ve", p.val("ve"))
    p.do("tensor", lambda e: e.matmul(ps2[:, 0:8], ones1r[:], co_row[:], start=True, stop=True), inc="mm")
    p.wait("vector", "mm", p.val("mm"))
    p.do("vector", lambda e: e.tensor_copy(co_bc[:], ps2[:, 0:8]), inc="ve")
    def gdo(fn):
        p.do("vector", fn, inc="vb")
        p.wait("vector", "vb", p.val("vb"))
    for tt in range(32):
        yb = tt % 2
        gdo(lambda e, tt=tt: e.tensor_single_scalar(
            H[:], iota8[:], idxs[:, tt:tt + 1], ALU.is_equal))
        gdo(lambda e: e.tensor_mul(t8[:], H[:], co_bc[:]))
        gdo(lambda e: e.reduce_sum(col[:, 0:1], t8[:], axis=AX.X))
        gdo(lambda e, tt=tt: e.tensor_add(col[:, 1:2], locs[:, tt:tt + 1], col[:, 0:1]))
        gdo(lambda e: e.tensor_single_scalar(col[:, 2:3], col[:, 1:2], float(T * NCORE // 8), ALU.is_lt))
        gdo(lambda e, tt=tt: e.tensor_single_scalar(col[:, 3:4], locs[:, tt:tt + 1], float(MOE_C), ALU.is_lt))
        gdo(lambda e: e.tensor_mul(col[:, 2:3], col[:, 2:3], col[:, 3:4]))
        gdo(lambda e: e.tensor_scalar(col[:, 5:6], col[:, 2:3], -float(8 * MOE_C), float(8 * MOE_C), ALU.mult, ALU.add))
        gdo(lambda e, tt=tt: e.tensor_single_scalar(col[:, 4:5], idxs[:, tt:tt + 1], float(MOE_C), ALU.mult))
        gdo(lambda e, tt=tt: e.tensor_add(col[:, 4:5], col[:, 4:5], locs[:, tt:tt + 1]))
        gdo(lambda e: e.tensor_mul(col[:, 4:5], col[:, 4:5], col[:, 2:3]))
        gdo(lambda e: e.tensor_add(col[:, 4:5], col[:, 4:5], col[:, 5:6]))
        gdo(lambda e, tt=tt: e.tensor_mul(gk[:, tt:tt + 1], gates_sb[:, tt:tt + 1], col[:, 2:3]))
        p.do("vector", lambda e, tt=tt: e.tensor_copy(desti[:, tt:tt + 1], col[:, 4:5]), inc="ve")
        p.do("sync", lambda e, tt=tt, yb=yb: e.dma_start(
            out=y1t[yb][:], in_=p.drams["y1_tm"][tt * 128:(tt + 1) * 128, :]), inc="b4_yld", n=16)
        p.wait("gpsimd", "b4_yld", p.val("b4_yld"))
        p.wait("gpsimd", "ve", p.val("ve"))
        p.do("gpsimd", lambda e, tt=tt, yb=yb: e.indirect_dma_start(
            out=disp[:], out_offset=IndirectOffsetOnAxis(desti[:, tt:tt + 1], 0),
            in_=y1t[yb][:], in_offset=None), inc="b4_sc", n=16)
        if tt >= 1:
            p.wait("sync", "b4_sc", scbase + 16 * tt)
    p.wait("sync", "b4_sc", p.val("b4_sc"))
    p.do("sync", lambda e: e.dma_start(out=p.drams["r_dest"][:], in_=desti[:]), inc="b4_st", n=16)
    p.do("sync", lambda e: e.dma_start(out=p.drams["r_gatek"][:], in_=gk[:]), inc="b4_st", n=16)
    p.wait("sync", "b4_st", p.val("b4_st"))


B5_NEXP = 8
B5_DO_Y = True
B5_DO_H = True
B5_HMODE = 0


def st_moe_ffn(p):
    ident = load_const(p, "b5_id", "c_ident_bf", [128, 128], BF16)
    identf = load_const(p, "b5_idf", "c_ident_f32", [128, 128])
    b1v = p.sbuf("b5_b1", [128, 8, 32], F32)
    p.do("sync", lambda e: e.dma_start(
        out=b1v[:], in_=p.drams["b_moe_1"].rearrange("e p c -> p e c")), inc="cld", n=16)
    b2v = p.sbuf("b5_b2", [128, 8, 8], F32)
    p.do("sync", lambda e: e.dma_start(
        out=b2v[:], in_=p.drams["b_moe_2"].rearrange("e p c -> p e c")), inc="cld", n=16)
    wait_consts(p, "scalar", "tensor", "vector")
    disp, ydr = p.drams["moe_disp"], p.drams["moe_y"]
    dtile = p.sbuf("b5_dt", [128, D], BF16)
    dfm = p.sbuf("b5_dfm", [128, NCH, MOE_C], BF16)
    hfm = p.sbuf("b5_hfm", [128, 32, MOE_C], BF16)
    ev = p.sbuf("b5_ev", [128, TC], BF16)
    ev32 = p.sbuf("b5_e32", [128, TC], F32)
    evt = p.sbuf("b5_evt", [128, TC], F32)
    trp = p.psum("b5_trp", [128, 1024], BF16)
    trpf = p.psum("b5_trpf", [128, TC], F32)
    shared_banks = [p.psum(f"b5_pb{i}", [128, TC], F32) for i in range(2)]
    dcpbase = p.val("b5_dcp")
    shared_state = [None, None]
    wb_h = [p.sbuf(f"b5_wh{i}", [128, NCH, 128], BF16) for i in range(2)]
    wb_y = [p.sbuf(f"b5_wy{i}", [128, 32, 128], BF16) for i in range(2)]
    for e_i in range(B5_NEXP):
        for st_ in range(MOE_C // 128):
            p.wait("sync", "b5_dcp", dcpbase + NCH * (e_i * (MOE_C // 128) + st_))
            p.do("sync", lambda e, e_i=e_i, st_=st_: e.dma_start(
                out=dtile[:], in_=disp[e_i * MOE_C + st_ * 128: e_i * MOE_C + (st_ + 1) * 128, :]),
                inc="b5_dld", n=16)
            p.wait("tensor", "b5_dld", p.val("b5_dld"))
            for k in range(NCH):
                p.wait("tensor", "b5_dcp", p.val("b5_dcp"))
                p.do("tensor", lambda e, k=k: e.transpose(
                    trp[:, 0:128], dtile[:, k * 128:(k + 1) * 128], ident[:]), inc="mm2")
                p.wait("vector", "mm2", p.val("mm2"))
                p.do("vector", lambda e, k=k, st_=st_: e.tensor_copy(
                    dfm[:, k, st_ * 128:(st_ + 1) * 128], trp[:, 0:128]), inc="b5_dcp")
        p.wait("tensor", "b5_dcp", p.val("b5_dcp"))
        def epi_h(mo, tc, ps, mwid, par=0, e_i=e_i):
            p.wait("scalar", "mm", p.val("mm"))
            p.do("scalar", lambda e, mo=mo, tc=tc: e.activation(
                out=hfm[:, mo, tc * TC:(tc + 1) * TC], in_=ps[:], func=AF.Identity,
                bias=b1v[:, e_i, mo:mo + 1]), inc=f"b5h{e_i}_ev")
        if B5_DO_H:
            emit_proj(p, f"b5h{e_i}", dfm, NCH, p.drams["w_moe_1"], 4096, epi_h,
                      tc_list=list(range(MOE_C // TC)), banks=shared_banks,
                      bank_state=shared_state, wbufs=wb_h, w_row0=e_i * 1024)
        p.wait("tensor", f"b5h{e_i}_ev", p.val(f"b5h{e_i}_ev"))
        def epi_y(mo, tc, ps, mwid, par=0, e_i=e_i):
            p.wait("scalar", "mm", p.val("mm"))
            p.wait("scalar", "b5_yst", p.val("b5_yst"))
            p.do("scalar", lambda e, mo=mo: e.activation(
                out=ev32[:], in_=ps[:], func=AF.Identity,
                bias=b2v[:, e_i, mo:mo + 1]), inc="ac")
            p.wait("tensor", "ac", p.val("ac"))
            p.wait("tensor", "b5_ytev", p.val("b5_ytev"))
            for j in range(4):
                p.do("tensor", lambda e, j=j: e.transpose(
                    trpf[:, 0:128], ev32[:, j * 128:(j + 1) * 128], identf[:]), inc="mm2")
                p.wait("vector", "mm2", p.val("mm2"))
                p.wait("vector", "b5_yst", p.val("b5_yst"))
                p.do("vector", lambda e, j=j: e.tensor_copy(
                    evt[:, j * 128:(j + 1) * 128], trpf[:, 0:128]), inc="b5_ytev")
                p.wait("tensor", "b5_ytev", p.val("b5_ytev"))
            p.wait("sync", "b5_ytev", p.val("b5_ytev"))
            for j in range(4):
                p.do("sync", lambda e, e_i=e_i, mo=mo, tc=tc, j=j: e.dma_start(
                    out=ydr[e_i * MOE_C + tc * TC + j * 128: e_i * MOE_C + tc * TC + (j + 1) * 128,
                            mo * 128:(mo + 1) * 128],
                    in_=evt[:, j * 128:(j + 1) * 128]), inc="b5_yst", n=16)
            p.do("sync", lambda e: e.nop(), inc=f"b5y{e_i}_ev")
        if B5_DO_Y:
            emit_proj(p, f"b5y{e_i}", hfm, 32, p.drams["w_moe_2"], D, epi_y,
                      tc_list=list(range(MOE_C // TC)), banks=shared_banks,
                      bank_state=shared_state, wbufs=wb_y, w_row0=e_i * 4096)
    p.wait("sync", "b5_yst", p.val("b5_yst"))


def st_moe_gather(p):
    identf = load_const(p, "b6_idf", "c_ident_f32", [128, 128])
    dsti = load_const(p, "b6_di", "r_dest", [128, 32], I32)
    gatec = load_const(p, "b6_gv", "r_gatek", [128, 32])
    wait_consts(p, "tensor", "vector", "gpsimd")
    zrow = p.sbuf("b6_z", [1, D], F32)
    p.do("vector", lambda e: e.memset(zrow[:], 0.0), inc="ve")
    p.wait("sync", "ve", p.val("ve"))
    p.do("sync", lambda e: e.dma_start(
        out=p.drams["moe_y"][8 * MOE_C:8 * MOE_C + 1, :], in_=zrow[:]), inc="b6_zs", n=16)
    p.wait("gpsimd", "b6_zs", p.val("b6_zs"))
    mo_t = p.sbuf("b6_mt", [128, D], F32)
    y1t = p.sbuf("b6_y1", [128, NCH, 128], F32)
    osum = p.sbuf("b6_os", [128, NCH, 128], F32)
    trpf = p.psum("b6_trp", [128, TC], F32)
    ydr, y2 = p.drams["moe_y"], p.drams["y2pre"]
    for tt in range(32):
        p.wait("gpsimd", "b6_mtc", p.val("b6_mtc"))
        p.do("gpsimd", lambda e, tt=tt: e.indirect_dma_start(
            out=mo_t[:], out_offset=None, in_=ydr[:],
            in_offset=IndirectOffsetOnAxis(dsti[:, tt:tt + 1], 0)), inc="b6_g", n=16)
        p.wait("sync", "ve", p.val("ve"))
        p.do("sync", lambda e, tt=tt: e.dma_start(
            out=y1t[:], in_=p.drams["y1_f32"][:, tt * 128:(tt + 1) * 128]
                .rearrange("(c p) t -> p c t", p=128)), inc="b6_yld", n=16)
        p.wait("vector", "b6_g", p.val("b6_g"))
        p.do("vector", lambda e, tt=tt: e.tensor_scalar_mul(
            mo_t[:], mo_t[:], gatec[:, tt:tt + 1]), inc="vb")
        p.wait("tensor", "vb", p.val("vb"))
        for k in range(NCH):
            p.wait("tensor", "b6_tev", p.val("b6_tev"))
            p.do("tensor", lambda e, k=k: e.transpose(
                trpf[:, 0:128], mo_t[:, k * 128:(k + 1) * 128], identf[:]), inc="mm2")
            p.wait("vector", "mm2", p.val("mm2"))
            if k == 0:
                p.wait("vector", "b6_yld", p.val("b6_yld"))
                p.wait("vector", "b6_st", p.val("b6_st"))
            p.do("vector", lambda e, k=k: e.tensor_add(
                osum[:, k, :], y1t[:, k, :], trpf[:, 0:128]),
                inc="b6_tev" if k < NCH - 1 else "ve")
            if k == NCH - 1:
                p.do("vector", lambda e: e.engine_nop(), inc="b6_tev")
            p.wait("tensor", "b6_tev", p.val("b6_tev"))
        p.do("vector", lambda e: e.engine_nop(), inc="b6_mtc")
        p.wait("sync", "ve", p.val("ve"))
        p.do("sync", lambda e, tt=tt: e.dma_start(
            out=y2[:, tt * 128:(tt + 1) * 128].rearrange("(c p) t -> p c t", p=128),
            in_=osum[:]), inc="b6_st", n=16)
    p.wait("sync", "b6_st", p.val("b6_st"))


def st_combine(p, tag, x_name, s_name, g_name, out_name):
    xt = p.sbuf(f"{tag}_x", [128, NCH, TC], F32)
    st_ = p.sbuf(f"{tag}_s", [128, NCH, TC], F32)
    gt = p.sbuf(f"{tag}_g", [128, NCH, TC], F32)
    ot = p.sbuf(f"{tag}_o", [128, NCH, TC], F32)
    stbase = p.val(f"{tag}_st")
    for tc in range(NTC):
        if tc >= 1:
            p.wait("sync", f"{tag}_st", stbase + 16 * tc)
        for nm_, tile in ((x_name, xt), (s_name, st_), (g_name, gt)):
            p.do("sync", lambda e, nm_=nm_, tile=tile, tc=tc: e.dma_start(
                out=tile[:], in_=p.drams[nm_][:, tc * TC:(tc + 1) * TC]
                    .rearrange("(c p) t -> p c t", p=128)), inc=f"{tag}_ld", n=16)
        p.wait("vector", f"{tag}_ld", p.val(f"{tag}_ld"))
        if tc >= 1:
            p.wait("vector", f"{tag}_st", stbase + 16 * tc)
        for k in range(NCH):
            p.do("vector", lambda e, k=k: e.tensor_mul(ot[:, k, :], st_[:, k, :], gt[:, k, :]))
            p.do("vector", lambda e, k=k: e.tensor_add(ot[:, k, :], ot[:, k, :], xt[:, k, :]),
                 inc=("ve" if k == NCH - 1 else None))
        p.wait("sync", "ve", p.val("ve"))
        p.do("sync", lambda e, tc=tc: e.dma_start(
            out=p.drams[out_name][:, tc * TC:(tc + 1) * TC]
                .rearrange("(c p) t -> p c t", p=128), in_=ot[:]), inc=f"{tag}_st", n=16)
    p.wait("sync", f"{tag}_st", p.val(f"{tag}_st"))



FF2 = 2816


def st_l1_kv(p):
    x_sb, ldc = st_load_stream(p, "d2kv", p.drams["xn1_bf"])
    p.wait("tensor", "d2kv_xld", ldc)
    ident = load_const(p, "d2kv_id", "c_ident_bf", [128, 128], BF16)
    wait_consts(p, "tensor", "scalar")
    ev = [p.sbuf(f"d2kv_ev{i}", [128, TC], BF16) for i in range(2)]
    evt = [p.sbuf(f"d2kv_evt{i}", [128, TC], BF16) for i in range(2)]
    psb = p.psum("d2kv_trp", [128, 1024], BF16)

    def epi_k(mo, tc, ps, mwid, par=0):
        ev_ = ev[par]
        stp = f"d2kvst{par}"
        p.wait("scalar", "mm", p.val("mm"))
        p.wait("scalar", stp, p.val(stp))
        p.do("scalar", lambda e, ev_=ev_: e.activation(out=ev_[:], in_=ps[:], func=AF.Identity),
             inc="d2kvk_ev")
        p.wait("sync", "d2kvk_ev", p.val("d2kvk_ev"))
        p.do("sync", lambda e, tc=tc, ev_=ev_: e.dma_start(
            out=p.drams["l1_k"][:, tc * TC:(tc + 1) * TC], in_=ev_[:]), inc=stp, n=16)

    def epi_v(mo, tc, ps, mwid, par=0):
        ev_, evt_ = ev[par], evt[par]
        stp = f"d2kvst{par}"
        p.wait("scalar", "mm", p.val("mm"))
        p.wait("scalar", stp, p.val(stp))
        p.do("scalar", lambda e, ev_=ev_: e.activation(out=ev_[:], in_=ps[:], func=AF.Identity),
             inc="d2kvv_ev")
        p.wait("tensor", "d2kvv_ev", p.val("d2kvv_ev"))
        p.wait("tensor", "ve2", p.val("ve2"))
        for j in range(4):
            p.do("tensor", lambda e, j=j, ev_=ev_: e.transpose(
                psb[:, j * 128:(j + 1) * 128], ev_[:, j * 128:(j + 1) * 128], ident[:]),
                inc=("mm2" if j == 3 else None))
        p.wait("vector", "mm2", p.val("mm2"))
        p.wait("vector", stp, p.val(stp))
        for j in range(4):
            p.do("vector", lambda e, j=j, evt_=evt_: e.tensor_copy(
                evt_[:, j * 128:(j + 1) * 128], psb[:, j * 128:(j + 1) * 128]),
                inc=("ve2" if j == 3 else None))
        p.wait("sync", "ve2", p.val("ve2"))
        for j in range(4):
            p.do("sync", lambda e, tc=tc, j=j, evt_=evt_: e.dma_start(
                out=p.drams["l1_v"][0, tc * TC + j * 128:tc * TC + (j + 1) * 128, :],
                in_=evt_[:, j * 128:(j + 1) * 128]), inc=stp, n=16)

    emit_proj(p, "d2kvk", x_sb, NCH, p.drams["w_l1a_k"], 128, epi_k)
    emit_proj(p, "d2kvv", x_sb, NCH, p.drams["w_l1a_v"], 128, epi_v)
    p.wait("sync", "d2kvst0", p.val("d2kvst0"))
    p.wait("sync", "d2kvst1", p.val("d2kvst1"))


def st_l1f_qkv(p, i, x_bf_name):
    x_sb, ldc = st_load_stream(p, "ea", p.drams[x_bf_name], shift_half=True)
    p.wait("tensor", "ea_xld", ldc)
    ident = load_const(p, "ea_id", "c_ident_bf", [128, 128], BF16)
    hmask = load_const(p, "ea_hm", "c_halfmask", [128, 2])
    hsel = load_const(p, "ea_hs", "c_halfsel", [2, 128])
    qsl = p.sbuf("ea_qs", [128, 1], F32)
    ksl = p.sbuf("ea_ks", [128, 1], F32)
    p.do("sync", lambda e: e.dma_start(
        out=qsl[:], in_=p.drams["s_l1f_q"][i * 128:(i + 1) * 128, :]), inc="cld", n=16)
    p.do("sync", lambda e: e.dma_start(
        out=ksl[:], in_=p.drams["s_l1f_k"][i * 128:(i + 1) * 128, :]), inc="cld", n=16)
    tabs = p.sbuf("ea_tab", [128, 4, T], F32)
    for ti, nm_ in enumerate(["c_rot_cs", "c_rot_ss", "c_rot_ci", "c_rot_si"]):
        p.do("sync", lambda e, ti=ti, nm_=nm_: e.dma_start(
            out=tabs[:, ti, :], in_=p.drams[nm_][:]), inc="cld", n=16)
    wait_consts(p, "vector", "tensor", "scalar")
    qf = p.sbuf("ea_qf", [128, TC], F32)
    q2 = p.sbuf("ea_q2", [128, TC], F32)
    r2 = p.sbuf("ea_r2", [2, TC], F32)
    t1 = p.sbuf("ea_t1", [128, TC], F32)
    rot = p.sbuf("ea_rot", [128, TC], F32)
    m1 = p.sbuf("ea_m1", [128, TC], F32)
    ev = [p.sbuf(f"ea_ev{i}", [128, TC], BF16) for i in range(2)]
    evt = [p.sbuf(f"ea_evt{i}", [128, TC], BF16) for i in range(2)]
    ps2 = p.psum("ea_ps2", [2, TC], F32)
    psb128 = p.psum("ea_psb", [128, TC], F32)
    psbt = p.psum("ea_trp", [128, 1024], BF16)

    def epi(mo, tc, ps, mwid, par=0):
        grp, mo8 = mo // 8, mo % 8
        ev_, evt_ = ev[par], evt[par]
        stp = f"east{par}"
        p.wait("scalar", "mm", p.val("mm"))
        if grp == 2:
            p.wait("scalar", stp, p.val(stp))
            p.do("scalar", lambda e, ev_=ev_: e.activation(out=ev_[:], in_=ps[:], func=AF.Identity),
                 inc="ea_ev")
            p.wait("tensor", "ea_ev", p.val("ea_ev"))
            p.wait("tensor", "ve2", p.val("ve2"))
            for j in range(4):
                p.do("tensor", lambda e, j=j, ev_=ev_: e.transpose(
                    psbt[:, j * 128:(j + 1) * 128], ev_[:, j * 128:(j + 1) * 128], ident[:]),
                    inc=("mm2" if j == 3 else None))
            p.wait("vector", "mm2", p.val("mm2"))
            p.wait("vector", stp, p.val(stp))
            for j in range(4):
                p.do("vector", lambda e, j=j, evt_=evt_: e.tensor_copy(
                    evt_[:, j * 128:(j + 1) * 128], psbt[:, j * 128:(j + 1) * 128]),
                    inc=("ve2" if j == 3 else None))
            p.wait("sync", "ve2", p.val("ve2"))
            for j in range(4):
                for hh in range(2):
                    p.do("sync", lambda e, mo8=mo8, tc=tc, j=j, hh=hh, evt_=evt_: e.dma_start(
                        out=p.drams["l1f_v"][2 * mo8 + hh,
                                             tc * TC + j * 128:tc * TC + (j + 1) * 128, :],
                        in_=evt_[:, j * 128 + hh * 64:j * 128 + hh * 64 + 64]),
                        inc=stp, n=16)
            return
        p.wait("scalar", "eaqf", p.val("eaqf"))
        p.do("scalar", lambda e: e.activation(out=qf[:], in_=ps[:], func=AF.Identity),
             inc="ea_ev")
        p.do("scalar", lambda e: e.activation(out=q2[:], in_=qf[:], func=AF.Square), inc="ac")
        p.wait("tensor", "ac", p.val("ac"))
        p.wait("tensor", "vb", p.val("vb"))
        p.do("tensor", lambda e: e.matmul(ps2[:], hmask[:], q2[:], start=True, stop=True),
             inc="mm2")
        p.wait("scalar", "mm2", p.val("mm2"))
        p.do("scalar", lambda e: e.activation(out=r2[:], in_=ps2[:], func=AF.Sqrt), inc="ac")
        p.wait("vector", "ac", p.val("ac"))
        p.do("vector", lambda e: e.tensor_scalar_max(r2[:], r2[:], 1e-12), inc="vb")
        p.wait("vector", "vb", p.val("vb"))
        p.do("vector", lambda e: e.reciprocal(r2[:], r2[:]), inc="vb")
        p.wait("tensor", "vb", p.val("vb"))
        p.wait("tensor", "eaqf", p.val("eaqf"))
        p.do("tensor", lambda e: e.matmul(psb128[:], hsel[:], r2[:], start=True, stop=True),
             inc="mm2")
        p.wait("vector", "mm2", p.val("mm2"))
        p.do("vector", lambda e: e.tensor_mul(t1[:], qf[:], psb128[:]), inc="vb")
        p.wait("vector", "vb", p.val("vb"))
        sv = qsl if grp == 0 else ksl
        p.do("vector", lambda e, sv=sv: e.tensor_scalar_mul(t1[:], t1[:], sv[:]), inc="eaqf")
        p.wait("vector", "eaqf", p.val("eaqf"))
        for (a, b_) in ((0, 32), (32, 0), (64, 96), (96, 64)):
            p.do("vector", lambda e, a=a, b_=b_: e.tensor_copy(
                rot[a:a + 32, :], t1[b_:b_ + 32, :]))
        tq = 0 if grp == 0 else 2
        p.do("vector", lambda e, tq=tq, tc=tc: e.tensor_mul(
            m1[:], t1[:], tabs[:, tq, tc * TC:(tc + 1) * TC]), inc="vb")
        p.wait("vector", "vb", p.val("vb"))
        p.do("vector", lambda e, tq=tq, tc=tc: e.tensor_mul(
            rot[:], rot[:], tabs[:, tq + 1, tc * TC:(tc + 1) * TC]), inc="vb")
        p.wait("vector", "vb", p.val("vb"))
        p.wait("vector", stp, p.val(stp))
        p.do("vector", lambda e: e.tensor_add(m1[:], m1[:], rot[:]), inc="vb")
        p.wait("vector", "vb", p.val("vb"))
        p.do("vector", lambda e, ev_=ev_: e.tensor_copy(ev_[:], m1[:]), inc="ve2x")
        p.wait("sync", "ve2x", p.val("ve2x"))
        dst = p.drams["l1f_q"] if grp == 0 else p.drams["l1f_k"]
        p.do("sync", lambda e, dst=dst, mo8=mo8, tc=tc, ev_=ev_: e.dma_start(
            out=dst[mo8 * 128:(mo8 + 1) * 128, tc * TC:(tc + 1) * TC], in_=ev_[:]),
            inc=stp, n=16)

    emit_proj(p, "ea", x_sb, NCH, p.drams["w_l1f_qkv"], 3 * D, epi, w_row0=i * D)
    p.wait("sync", "east0", p.val("east0"))
    p.wait("sync", "east1", p.val("east1"))


def st_l1f_lattn(p):
    ident = load_const(p, "eb_id", "c_ident_bf", [128, 128], BF16)
    lmask = p.sbuf("eb_lm", [128, 4, 512], F32)
    p.do("sync", lambda e: e.dma_start(
        out=lmask[:], in_=p.drams["c_lmask"].rearrange("v p s -> p v s")), inc="cld", n=16)
    wait_consts(p, "tensor", "vector")
    q_sb = p.sbuf("eb_q", [64, T], BF16)
    k_sb = p.sbuf("eb_k", [64, T], BF16)
    v_sb = p.sbuf("eb_v", [128, 32, 65], BF16)
    S_sb = p.sbuf("eb_S", [128, 512], F32)
    P_sb = p.sbuf("eb_P", [128, 512], BF16)
    PT_sb = p.sbuf("eb_pt", [128, 2, 128], BF16)
    AVc = p.sbuf("eb_avc", [128, 65], F32)
    O32 = p.sbuf("eb_o32", [128, 64], F32)
    Obf = p.sbuf("eb_obf", [128, 64], BF16)
    OT_sb = [p.sbuf(f"eb_ot{i}", [64, 128], BF16) for i in range(2)]
    nm = p.sbuf("eb_nm", [128, 1], F32)
    nmr = p.sbuf("eb_nmr", [128, 1], F32)
    S_ps = p.psum("eb_sp", [128, TC], F32)
    PT_ps = [p.psum(f"eb_ptp{x}", [128, 1024], BF16) for x in range(2)]
    AV_ps = p.psum("eb_av", [128, TC], F32)
    OT_ps = p.psum("eb_otp", [128, 1024], BF16)
    o_fm = p.drams["l1f_o"]
    hdbase = p.val("eb_hd")
    for h in range(16):
        if h > 0:
            p.wait("sync", "eb_hd", hdbase + h)
        p.do("sync", lambda e, h=h: e.dma_start(
            out=q_sb[:], in_=p.drams["l1f_q"][h * 64:(h + 1) * 64, :]), inc="eb_ld", n=16)
        p.do("sync", lambda e, h=h: e.dma_start(
            out=k_sb[:], in_=p.drams["l1f_k"][h * 64:(h + 1) * 64, :]), inc="eb_ld", n=16)
        p.do("sync", lambda e, h=h: e.dma_start(
            out=v_sb[:, :, 0:64],
            in_=p.drams["l1f_v"][h].rearrange("(tt p) d -> p tt d", p=128)), inc="eb_ld", n=16)
        p.wait("vector", "eb_ld", p.val("eb_ld"))
        p.do("vector", lambda e: e.memset(v_sb[:, :, 64:65], 1.0), inc="ve")
        p.wait("tensor", "eb_ld", p.val("eb_ld"))
        p.wait("tensor", "ve", p.val("ve"))
        for w in range(16):
            for qsub in range(2):
                q0c = w * 256 + qsub * 128
                win0 = (w == 0)
                s_lo = 256 if win0 else 0
                kt0 = (w * 256 - 256) + s_lo
                nsl = 512 - s_lo
                p.wait("tensor", "ac", p.val("ac"))
                p.do("tensor", lambda e, q0c=q0c, kt0=kt0, nsl=nsl: e.matmul(
                    S_ps[:, 0:nsl], q_sb[:, q0c:q0c + 128], k_sb[:, kt0:kt0 + nsl],
                    start=True, stop=True), inc="mm")
                p.wait("scalar", "mm", p.val("mm"))
                p.wait("scalar", "vb", p.val("vb"))
                p.do("scalar", lambda e, s_lo=s_lo, nsl=nsl: e.activation(
                    out=S_sb[:, s_lo:512], in_=S_ps[:, 0:nsl], func=AF.Identity,
                    scale=8.0), inc="ac")
                p.wait("vector", "ac", p.val("ac"))
                mv = qsub * 2 + (1 if win0 else 0)
                p.do("vector", lambda e, mv=mv, s_lo=s_lo: e.tensor_add(
                    S_sb[:, s_lo:512], S_sb[:, s_lo:512], lmask[:, mv, s_lo:512]))
                p.do("vector", lambda e, s_lo=s_lo: e.reduce_max(
                    nm[:], S_sb[:, s_lo:512], axis=AX.X, negate=True), inc="vb")
                p.wait("scalar", "vb", p.val("vb"))
                p.wait("scalar", "mm3", p.val("mm3"))
                p.do("scalar", lambda e, s_lo=s_lo: e.activation(
                    out=P_sb[:, s_lo:512], in_=S_sb[:, s_lo:512], func=AF.Exp,
                    bias=nm[:]), inc="ac")
                p.wait("tensor", "ac", p.val("ac"))
                jlist = list(range(s_lo // 128, 4))
                for ji, j in enumerate(jlist):
                    tb = j % 2
                    if ji >= 2:
                        p.wait("tensor", "ve2", p.val("ve2"))
                    p.do("tensor", lambda e, j=j, tb=tb: e.transpose(
                        PT_ps[tb][:, 0:128], P_sb[:, j * 128:(j + 1) * 128], ident[:]),
                        inc="mm2")
                    p.wait("vector", "mm2", p.val("mm2"))
                    p.do("vector", lambda e, tb=tb: e.tensor_copy(
                        PT_sb[:, tb, :], PT_ps[tb][:, 0:128]), inc="ve2")
                    p.wait("tensor", "ve2", p.val("ve2"))
                    vtile = 2 * w - 2 + j
                    p.do("tensor", lambda e, tb=tb, vtile=vtile, ji=ji, last=(ji == len(jlist) - 1): e.matmul(
                        AV_ps[:, 0:65], PT_sb[:, tb, :], v_sb[:, vtile, :],
                        start=(ji == 0), stop=last),
                        inc=("mm3" if ji == len(jlist) - 1 else None))
                p.wait("vector", "mm3", p.val("mm3"))
                p.do("vector", lambda e: e.tensor_copy(AVc[:], AV_ps[:, 0:65]), inc="vb")
                p.wait("vector", "vb", p.val("vb"))
                p.do("vector", lambda e: e.reciprocal(nmr[:], AVc[:, 64:65]), inc="vb")
                p.wait("vector", "vb", p.val("vb"))
                p.do("vector", lambda e: e.tensor_scalar_mul(O32[:], AVc[:, 0:64], nmr[:]), inc="vb")
                p.wait("vector", "vb", p.val("vb"))
                p.do("vector", lambda e: e.tensor_copy(Obf[:], O32[:]), inc="ve")
                p.wait("tensor", "ve", p.val("ve"))
                p.do("tensor", lambda e: e.transpose(OT_ps[0:64, 0:128], Obf[:], ident[:]),
                     inc="mm2")
                p.wait("vector", "mm2", p.val("mm2"))
                spar = qsub
                p.wait("vector", f"eb_st{spar}", p.val(f"eb_st{spar}"))
                p.do("vector", lambda e, spar=spar: e.tensor_copy(
                    OT_sb[spar][:], OT_ps[0:64, 0:128]), inc="ve3")
                p.wait("sync", "ve3", p.val("ve3"))
                p.do("sync", lambda e, h=h, q0c=q0c, spar=spar: e.dma_start(
                    out=o_fm[h * 64:(h + 1) * 64, q0c:q0c + 128], in_=OT_sb[spar][:]),
                    inc=f"eb_st{spar}", n=16)
        p.do("sync", lambda e: e.nop(), inc="eb_hd")
    p.wait("sync", "eb_st0", p.val("eb_st0"))
    p.wait("sync", "eb_st1", p.val("eb_st1"))


def st_l1f_ffn(p, i, xn_name, res_name, out_f32_name, out_bf_name):
    w1a = p.sbuf("ee_w1a", [128, NCH, FF2], BF16)
    w1g = p.sbuf("ee_w1g", [128, NCH, FF2], BF16)
    w2 = p.sbuf("ee_w2", [128, FF2 // 128, 1024], BF16)
    p.do("gpsimd", lambda e: e.dma_start(
        out=w1a[:], in_=p.drams["w_l1f_1a"][i * D:(i + 1) * D, :]
            .rearrange("(c k) m -> k c m", k=128)), inc="ee_wld", n=16)
    p.do("gpsimd", lambda e: e.dma_start(
        out=w1g[:], in_=p.drams["w_l1f_1g"][i * D:(i + 1) * D, :]
            .rearrange("(c k) m -> k c m", k=128)), inc="ee_wld", n=16)
    p.do("gpsimd", lambda e: e.dma_start(
        out=w2[:], in_=p.drams["w_l1f_2p"][i * FF2:(i + 1) * FF2, :]
            .rearrange("(c k) m -> k c m", k=128)), inc="ee_wld", n=16)
    b1a = load_const(p, "ee_b1a", "b_l1f_1a", [128, 22], rows=(i * 128, (i + 1) * 128))
    b1g = load_const(p, "ee_b1g", "b_l1f_1g", [128, 22], rows=(i * 128, (i + 1) * 128))
    b2 = load_const(p, "ee_b2", "b_l1f_2", [128, 8], rows=(i * 128, (i + 1) * 128))
    wait_consts(p, "scalar")
    p.wait("tensor", "ee_wld", p.val("ee_wld"))
    xt = [p.sbuf(f"ee_x{x}", [128, NCH, TC], BF16) for x in range(2)]
    hh = p.sbuf("ee_hh", [128, FF2 // 128, TC], BF16)
    qa = [p.sbuf(f"ee_qa{i}", [128, TC], F32) for i in range(2)]
    gg = [p.sbuf(f"ee_gg{i}", [128, TC], F32) for i in range(2)]
    yv = [p.sbuf(f"ee_yv{i}", [128, TC], F32) for i in range(2)]
    ybf = [p.sbuf(f"ee_ybf{i}", [128, TC], BF16) for i in range(2)]
    rest = [p.sbuf(f"ee_res{i}", [128, TC], F32) for i in range(2)]
    psA = [p.psum(f"ee_psA{i}", [128, TC], F32) for i in range(2)]
    psB = [p.psum(f"ee_psB{i}", [128, TC], F32) for i in range(2)]
    psY = [p.psum(f"ee_psY{i}", [128, TC], F32) for i in range(2)]
    mm_done_tile = []
    for tc in range(NTC):
        xb = tc % 2
        if tc >= 2:
            p.wait("sync", "mm", mm_done_tile[tc - 2])
        p.do("sync", lambda e, tc=tc, xb=xb: e.dma_start(
            out=xt[xb][:], in_=p.drams[xn_name][:, tc * TC:(tc + 1) * TC]
                .rearrange("(c p) t -> p c t", p=128)), inc="ee_xld", n=16)
        p.wait("tensor", "ee_xld", p.val("ee_xld"))
        if tc >= 1:
            p.wait("vector", "mm", mm_done_tile[tc - 1])
        for mo in range(FF2 // 128):
            mp = mo % 2
            p.wait("tensor", f"eeh{mp}", p.val(f"eeh{mp}"))
            for k in range(NCH):
                p.do("tensor", lambda e, k=k, mo=mo, xb=xb, mp=mp: e.matmul(
                    psA[mp][:], w1a[:, k, mo * 128:(mo + 1) * 128], xt[xb][:, k, :],
                    start=(k == 0), stop=(k == NCH - 1)))
            for k in range(NCH):
                p.do("tensor", lambda e, k=k, mo=mo, xb=xb, mp=mp: e.matmul(
                    psB[mp][:], w1g[:, k, mo * 128:(mo + 1) * 128], xt[xb][:, k, :],
                    start=(k == 0), stop=(k == NCH - 1)),
                    inc=("mm" if k == NCH - 1 else None))
            p.wait("scalar", "mm", p.val("mm"))
            p.do("scalar", lambda e, mo=mo, mp=mp: e.activation(
                out=qa[mp][:], in_=psA[mp][:], func=AF.Identity, bias=b1a[:, mo:mo + 1]), inc="ac")
            p.do("scalar", lambda e, mo=mo, mp=mp: e.activation(
                out=gg[mp][:], in_=psB[mp][:], func=AF.Gelu, bias=b1g[:, mo:mo + 1]), inc=f"eeq{mp}")
            p.wait("vector", f"eeq{mp}", p.val(f"eeq{mp}"))
            p.do("vector", lambda e, mo=mo, mp=mp: e.tensor_tensor(
                hh[:, mo, :], qa[mp][:], gg[mp][:], ALU.mult), inc=f"eeh{mp}")
        for mp in range(2):
            p.wait("tensor", f"eeh{mp}", p.val(f"eeh{mp}"))
        for mo2 in range(NCH):
            mp = mo2 % 2
            p.wait("tensor", f"eey{mp}", p.val(f"eey{mp}"))
            for k in range(FF2 // 128):
                p.do("tensor", lambda e, k=k, mo2=mo2, mp=mp: e.matmul(
                    psY[mp][:], w2[:, k, mo2 * 128:(mo2 + 1) * 128], hh[:, k, :],
                    start=(k == 0), stop=(k == FF2 // 128 - 1)),
                    inc=("mm" if k == FF2 // 128 - 1 else None))
            p.wait("sync", f"eest{mp}", p.val(f"eest{mp}"))
            p.do("sync", lambda e, mo2=mo2, tc=tc, mp=mp: e.dma_start(
                out=rest[mp][:], in_=p.drams[res_name][mo2 * 128:(mo2 + 1) * 128,
                                                       tc * TC:(tc + 1) * TC]), inc="ee_rld", n=16)
            p.wait("scalar", "mm", p.val("mm"))
            p.wait("scalar", f"eest{mp}", p.val(f"eest{mp}"))
            p.do("scalar", lambda e, mo2=mo2, mp=mp: e.activation(
                out=yv[mp][:], in_=psY[mp][:], func=AF.Identity, bias=b2[:, mo2:mo2 + 1]), inc=f"eey{mp}")
            p.wait("vector", f"eey{mp}", p.val(f"eey{mp}"))
            p.wait("vector", "ee_rld", p.val("ee_rld"))
            p.do("vector", lambda e, mp=mp: e.tensor_add(yv[mp][:], yv[mp][:], rest[mp][:]), inc="vb")
            p.wait("vector", "vb", p.val("vb"))
            p.do("vector", lambda e, mp=mp: e.tensor_copy(ybf[mp][:], yv[mp][:]), inc="ve")
            p.wait("sync", "ve", p.val("ve"))
            p.do("sync", lambda e, mo2=mo2, tc=tc, mp=mp: e.dma_start(
                out=p.drams[out_f32_name][mo2 * 128:(mo2 + 1) * 128, tc * TC:(tc + 1) * TC],
                in_=yv[mp][:]), inc=f"eest{mp}", n=16)
            p.do("sync", lambda e, mo2=mo2, tc=tc, mp=mp: e.dma_start(
                out=p.drams[out_bf_name][mo2 * 128:(mo2 + 1) * 128, tc * TC:(tc + 1) * TC],
                in_=ybf[mp][:]), inc=f"eest{mp}", n=16)
        mm_done_tile.append(p.val("mm"))
    p.wait("sync", "eest0", p.val("eest0"))
    p.wait("sync", "eest1", p.val("eest1"))


def st_silu_combine(p, tag, x_name, s_name, g_name, out_name):
    xt = p.sbuf(f"{tag}_x", [128, NCH, TC], F32)
    st_ = p.sbuf(f"{tag}_s", [128, NCH, TC], F32)
    gt = p.sbuf(f"{tag}_g", [128, NCH, TC], F32)
    ot = p.sbuf(f"{tag}_o", [128, NCH, TC], F32)
    stbase = p.val(f"{tag}_st")
    for tc in range(NTC):
        if tc >= 1:
            p.wait("sync", f"{tag}_st", stbase + 16 * tc)
        for nm_, tile in ((x_name, xt), (s_name, st_), (g_name, gt)):
            p.do("sync", lambda e, nm_=nm_, tile=tile, tc=tc: e.dma_start(
                out=tile[:], in_=p.drams[nm_][:, tc * TC:(tc + 1) * TC]
                    .rearrange("(c p) t -> p c t", p=128)), inc=f"{tag}_ld", n=16)
        p.wait("scalar", f"{tag}_ld", p.val(f"{tag}_ld"))
        if tc >= 1:
            p.wait("scalar", f"{tag}_st", stbase + 16 * tc)
        for k in range(NCH):
            p.do("scalar", lambda e, k=k: e.activation(
                out=gt[:, k, :], in_=gt[:, k, :], func=AF.Silu),
                inc=("ac" if k == NCH - 1 else None))
        p.wait("vector", "ac", p.val("ac"))
        p.wait("vector", f"{tag}_ld", p.val(f"{tag}_ld"))
        if tc >= 1:
            p.wait("vector", f"{tag}_st", stbase + 16 * tc)
        for k in range(NCH):
            p.do("vector", lambda e, k=k: e.tensor_mul(ot[:, k, :], st_[:, k, :], gt[:, k, :]))
            p.do("vector", lambda e, k=k: e.tensor_add(ot[:, k, :], ot[:, k, :], xt[:, k, :]),
                 inc=("ve" if k == NCH - 1 else None))
        p.wait("sync", "ve", p.val("ve"))
        p.do("sync", lambda e, tc=tc: e.dma_start(
            out=p.drams[out_name][:, tc * TC:(tc + 1) * TC]
                .rearrange("(c p) t -> p c t", p=128), in_=ot[:]), inc=f"{tag}_st", n=16)
    p.wait("sync", f"{tag}_st", p.val(f"{tag}_st"))



MID_SPECS = {
    "x_fm": ([D, T], F32),
    "xn_f32": ([D, T], F32),
    "xn_bf": ([D, T], BF16),
    "l0_q": ([D, T], BF16),
    "l0_k": ([D, T], BF16),
    "l0_v": ([16, T, 64], BF16),
    "l0_o": ([D, T], BF16),
    "s_fm": ([D, T], F32),
    "q0_tm": ([T, D], BF16),
    "k0_tm": ([T, D], BF16),
    "v0_tm": ([T, D], BF16),
    "k0_all": ([NCORE, T, D], BF16),
    "v0_all": ([NCORE, T, D], BF16),
    "o0_fm": ([D, T], BF16),
    "y1pre": ([D, T], F32),
    "y1_f32": ([D, T], F32),
    "y1_bf": ([D, T], BF16),
    "y1_tm": ([T, D], BF16),
    "r_hist": ([1, 8], F32),
    "hist_all": ([NCORE, 8], F32),
    "r_loc": ([128, 32], F32),
    "r_idx": ([128, 32], F32),
    "r_gate": ([128, 32], F32),
    "r_dest": ([128, 32], I32),
    "moe_disp": ([8 * MOE_C + 1, D], BF16),
    "moe_y": ([8 * MOE_C + 1, D], F32),
    "r_gatek": ([128, 32], F32),
    "y2pre": ([D, T], F32),
    "g_fm": ([D, T], F32),
    "x1_fm": ([D, T], F32),
    "xn1_f32": ([D, T], F32),
    "xn1_bf": ([D, T], BF16),
    "l1_k": ([128, T], BF16),
    "l1_v": ([1, T, 128], BF16),
    "l1_q": ([D, T], BF16),
    "l1_o": ([D, T], BF16),
    "s1_fm": ([D, T], F32),
    "l1f_q": ([D, T], BF16),
    "l1f_k": ([D, T], BF16),
    "l1f_v": ([16, T, 64], BF16),
    "l1f_o": ([D, T], BF16),
    "l1mid_f32": ([D, T], F32),
    "l1mid_bf": ([D, T], BF16),
    "l1ln_bf": ([D, T], BF16),
    "l1x_a_f32": ([D, T], F32),
    "l1x_a_bf": ([D, T], BF16),
    "l1x_b_f32": ([D, T], F32),
    "l1x_b_bf": ([D, T], BF16),
    "out_fm": ([D, T], F32),
}

REP_MAP = {"k0_all": "k0_tm", "v0_all": "v0_tm", "hist_all": "r_hist"}

_NP2MYBIR = {
    np.dtype(np.float32): F32,
    np.dtype(bfloat16): BF16,
    np.dtype(np.int32): I32,
}


class Launch:
    def __init__(self, name, builder, ins, outs):
        self.name, self.builder, self.ins, self.outs = name, builder, ins, outs


def stage_launches():
    L = []
    L.append(Launch("a1", lambda p: st_ln(p, "a1", p.drams["x_fm"],
                                          p.drams["xn_f32"], p.drams["xn_bf"]),
                    ["x_fm"], ["xn_f32", "xn_bf"]))
    L.append(Launch("a2", st_l0_qkv,
                    ["xn_bf", "w_l0a_q", "w_l0a_k", "w_l0a_v",
                     "b_l0a_q", "b_l0a_k", "b_l0a_v", "c_ident_bf"],
                    ["l0_q", "l0_k", "l0_v"]))
    L.append(Launch("a3", lambda p: st_causal_attn(
                        p, "a3", p.drams["l0_q"], p.drams["l0_k"], p.drams["l0_v"],
                        p.drams["l0_o"], 16, 64, 0.125, "c_mask0", k_dram_rows=1024),
                    ["l0_q", "l0_k", "l0_v", "c_ident_bf", "c_mask0"], ["l0_o"]))
    L.append(Launch("a4", lambda p: st_proj_simple(
                        p, "a4", "l0_o", "w_l0a_o", D, "s_fm", "b_l0a_o", out_dtype="f32"),
                    ["l0_o", "w_l0a_o", "b_l0a_o"], ["s_fm"]))
    L.append(Launch("b1", st_l0f_qkv,
                    ["xn_bf", "w_l0f_qkv", "b_l0f_qkv", "c_ident_bf"],
                    ["q0_tm", "k0_tm", "v0_tm"]))
    L.append(Launch("b2", st_l0f_mha,
                    ["q0_tm", "k0_all", "v0_all", "c_ident_bf"], ["o0_fm"]))
    L.append(Launch("b2b", lambda p: st_proj_simple(
                        p, "b2b", "o0_fm", "w_l0f_o", D, "y1pre", "b_l0f_o",
                        out_dtype="f32", add_from="xn_f32"),
                    ["o0_fm", "w_l0f_o", "b_l0f_o", "xn_f32"], ["y1pre"]))
    L.append(Launch("b2c", lambda p: st_ln(p, "b2c", p.drams["y1pre"],
                                           p.drams["y1_f32"], p.drams["y1_bf"],
                                           g_name="g_l0f_1", b_name="bb_l0f_1",
                                           out_tm=p.drams["y1_tm"]),
                    ["y1pre", "g_l0f_1", "bb_l0f_1", "c_ident_bf"],
                    ["y1_f32", "y1_bf", "y1_tm"]))
    L.append(Launch("b3", st_moe_route_a,
                    ["y1_bf", "w_moe_g", "c_ltri", "c_iota8"],
                    ["r_loc", "r_idx", "r_gate", "r_hist"]))
    L.append(Launch("b4", st_moe_route_b,
                    ["y1_tm", "r_loc", "r_idx", "r_gate", "hist_all", "c_coresel", "c_iota8"],
                    ["moe_disp", "r_dest", "r_gatek"]))
    L.append(Launch("b5", st_moe_ffn,
                    ["moe_disp", "w_moe_1", "w_moe_2", "b_moe_1", "b_moe_2",
                     "c_ident_bf", "c_ident_f32"],
                    ["moe_y"]))
    L.append(Launch("b6", st_moe_gather,
                    ["moe_y", "y1_f32", "r_dest", "r_gatek", "c_ident_f32"],
                    ["y2pre"]))
    L.append(Launch("b7", lambda p: st_ln(p, "b7", p.drams["y2pre"],
                                          p.drams["g_fm"], None,
                                          g_name="g_l0f_2", b_name="bb_l0f_2",
                                          post_act=AF.Silu),
                    ["y2pre", "g_l0f_2", "bb_l0f_2"], ["g_fm"]))
    L.append(Launch("c1", lambda p: st_combine(p, "c1", "x_fm", "s_fm", "g_fm", "x1_fm"),
                    ["x_fm", "s_fm", "g_fm"], ["x1_fm"]))
    L.append(Launch("d1", lambda p: st_ln(p, "d1", p.drams["x1_fm"],
                                          p.drams["xn1_f32"], p.drams["xn1_bf"]),
                    ["x1_fm"], ["xn1_f32", "xn1_bf"]))
    L.append(Launch("d2kv", st_l1_kv,
                    ["xn1_bf", "w_l1a_k", "w_l1a_v", "c_ident_bf"], ["l1_k", "l1_v"]))
    L.append(Launch("d2q", lambda p: st_proj_simple(
                        p, "d2q", "xn1_bf", "w_l1a_q", D, "l1_q"),
                    ["xn1_bf", "w_l1a_q"], ["l1_q"]))
    L.append(Launch("d2c", lambda p: st_causal_attn(
                        p, "d2c", p.drams["l1_q"], p.drams["l1_k"], p.drams["l1_v"],
                        p.drams["l1_o"], 8, 128, 1.0 / 32.0, "c_maskinf", k_dram_rows=128),
                    ["l1_q", "l1_k", "l1_v", "c_ident_bf", "c_maskinf"], ["l1_o"]))
    L.append(Launch("d2d", lambda p: st_proj_simple(
                        p, "d2d", "l1_o", "w_l1a_p", D, "s1_fm", "b_l1a_p", out_dtype="f32"),
                    ["l1_o", "w_l1a_p", "b_l1a_p"], ["s1_fm"]))
    lat_ins = ["l1f_q", "l1f_k", "l1f_v", "c_ident_bf", "c_lmask"]
    qkv_ins = ["w_l1f_qkv", "s_l1f_q", "s_l1f_k", "c_ident_bf", "c_halfmask", "c_halfsel",
               "c_rot_cs", "c_rot_ss", "c_rot_ci", "c_rot_si"]
    for i in range(6):
        xin_f32 = "xn1_f32" if i == 0 else (f"l1x_{'a' if i % 2 == 1 else 'b'}_f32")
        xin_bf = "xn1_bf" if i == 0 else (f"l1x_{'a' if i % 2 == 1 else 'b'}_bf")
        xout_f32 = f"l1x_{'a' if i % 2 == 0 else 'b'}_f32"
        xout_bf = f"l1x_{'a' if i % 2 == 0 else 'b'}_bf"
        L.append(Launch(f"e{i}a", lambda p, i=i, xb=xin_bf: st_l1f_qkv(p, i, xb),
                        [xin_bf] + qkv_ins, ["l1f_q", "l1f_k", "l1f_v"]))
        L.append(Launch(f"e{i}b", st_l1f_lattn, lat_ins, ["l1f_o"]))
        L.append(Launch(f"e{i}c", lambda p, i=i, xf=xin_f32: st_proj_simple(
                            p, "ec", "l1f_o", "w_l1f_o", D, "l1mid_f32",
                            out_dtype="f32", add_from=xf, w_row0=i * D,
                            also_bf16="l1mid_bf"),
                        ["l1f_o", "w_l1f_o", xin_f32], ["l1mid_f32", "l1mid_bf"]))
        L.append(Launch(f"e{i}d", lambda p, i=i: st_ln(
                            p, "ed", p.drams["l1mid_f32"], None, p.drams["l1ln_bf"],
                            g_name="g_l1f", b_name="bb_l1f",
                            g_rows=(i * 128, (i + 1) * 128), shift_half=True),
                        ["l1mid_f32", "g_l1f", "bb_l1f"], ["l1ln_bf"]))
        L.append(Launch(f"e{i}e", lambda p, i=i, of=xout_f32, ob=xout_bf: st_l1f_ffn(
                            p, i, "l1ln_bf", "l1mid_f32", of, ob),
                        ["l1ln_bf", "l1mid_f32", "w_l1f_1a", "w_l1f_1g", "w_l1f_2p",
                         "b_l1f_1a", "b_l1f_1g", "b_l1f_2"],
                        [xout_f32, xout_bf]))
    L.append(Launch("c2", lambda p: st_silu_combine(
                        p, "c2", "x1_fm", "s1_fm", "l1x_b_f32", "out_fm"),
                    ["x1_fm", "s1_fm", "l1x_b_f32"], ["out_fm"]))
    return L


def run_pipeline(inputs, upto=None, mids_init=None):
    C = host_constants()
    W = host_weights(inputs)
    x = np.asarray(inputs["L0_attn_x"], dtype=np.float32)
    mids = {"x_fm": [np.ascontiguousarray(x[c].T) for c in range(NCORE)]}
    if mids_init:
        mids.update(mids_init)
    static = {}
    static.update(C)
    static.update(W)
    percore = {"c_coresel": []}
    for c in range(NCORE):
        sel = np.zeros((8, 1), np.float32)
        sel[:c, 0] = 1.0
        percore["c_coresel"].append(sel)
    for launch in stage_launches():
        if upto is not None and launch.name not in upto:
            continue
        p = Prog()
        for name in launch.ins:
            if name in static:
                a = static[name]
                p.dram(name, a.shape, _NP2MYBIR[a.dtype], kind="ExternalInput")
            elif name in percore:
                a = percore[name][0]
                p.dram(name, a.shape, _NP2MYBIR[a.dtype], kind="ExternalInput")
            else:
                shape, dt = MID_SPECS[name]
                p.dram(name, shape, dt, kind="ExternalInput")
        for name in launch.outs:
            shape, dt = MID_SPECS[name]
            p.dram(name, shape, dt, kind="ExternalOutput")
        launch.builder(p)
        nc = p.finalize()
        in_maps = []
        for c in range(NCORE):
            m = {}
            for name in launch.ins:
                if name in static:
                    m[name] = static[name]
                elif name in percore:
                    m[name] = percore[name][c]
                elif name in REP_MAP:
                    shape, dt = MID_SPECS[name]
                    npdt = {F32: np.float32, BF16: bfloat16, I32: np.int32}[dt]
                    m[name] = np.concatenate(
                        [np.asarray(a).reshape(1, -1) for a in mids[REP_MAP[name]]], 0
                    ).reshape(shape).astype(npdt, copy=False)
                else:
                    m[name] = mids[name][c]
            in_maps.append(m)
        res = run_bass_kernel_spmd(nc, in_maps, core_ids=list(range(NCORE)))
        for name in launch.outs:
            mids[name] = [res.results[c][name] for c in range(NCORE)]
    return mids


def emit_allgather(p, pairs):
    for src_name, dst_name in pairs:
        sap = p.drams[src_name][:]
        dap = p.drams[dst_name]
        if len(dap.shape) == 3:
            dap = dap.rearrange("m t d -> (m t) d")
        else:
            dap = dap[:]
        p.do("gpsimd", lambda e, sap=sap, dap=dap: e.collective_compute(
            "AllGather", ALU.bypass, replica_groups=[list(range(NCORE))],
            ins=[sap.opt()], outs=[dap.opt()]), inc="cc")
    p.wait("gpsimd", "cc", p.val("cc"))
    p.do("gpsimd", lambda e: e.nop(), inc="stg")
    v = p.val("stg")
    for eng in ENGINES:
        p.wait(eng, "stg", v)


GATHER_AFTER = {"b1": [("k0_tm", "k0_all"), ("v0_tm", "v0_all")],
                "b3": [("r_hist", "hist_all")]}

FUSED = True


def build_fused(static, debug_outs=()):
    p = Prog()
    for name, a in static.items():
        p.dram(name, a.shape, _NP2MYBIR[a.dtype], kind="ExternalInput")
    p.dram("c_coresel", [8, 1], F32, kind="ExternalInput")
    for name, (shape, dt) in MID_SPECS.items():
        if name == "x_fm":
            p.dram(name, shape, dt, kind="ExternalInput")
        elif name == "out_fm" or name in debug_outs:
            p.dram(name, shape, dt, kind="ExternalOutput")
        else:
            p.dram(name, shape, dt, kind="Internal")
    for launch in stage_launches():
        with p.region():
            launch.builder(p)
        p.barrier_all()
        if launch.name in GATHER_AFTER:
            emit_allgather(p, GATHER_AFTER[launch.name])
    return p.finalize()


def kernel(**inputs):
    if not FUSED:
        mids = run_pipeline(inputs)
        return np.stack([np.ascontiguousarray(np.asarray(mids["out_fm"][c], np.float32).T)
                         for c in range(NCORE)])
    C = host_constants()
    W = host_weights(inputs)
    static = {}
    static.update(C)
    static.update(W)
    nc = build_fused(static)
    x = np.asarray(inputs["L0_attn_x"], dtype=np.float32)
    in_maps = []
    for c in range(NCORE):
        sel = np.zeros((8, 1), np.float32)
        sel[:c, 0] = 1.0
        m = dict(static)
        m["c_coresel"] = sel
        m["x_fm"] = np.ascontiguousarray(x[c].T)
        in_maps.append(m)
    res = run_bass_kernel_spmd(nc, in_maps, core_ids=list(range(NCORE)))
    return np.stack([np.ascontiguousarray(np.asarray(res.results[c]["out_fm"], np.float32).T)
                     for c in range(NCORE)])
